# version 01
# baseline (speedup 1.0000x reference)
import contextlib
import numpy as np
import concourse.bass as bass
import concourse.mybir as mybir
from concourse.bass_utils import run_bass_kernel_spmd
from concourse.alu_op_type import AluOpType as ALU

F32 = mybir.dt.float32
BF16 = mybir.dt.bfloat16
AF = mybir.ActivationFunctionType

NCORES = 8
NB = 2
S = 2048
D = 1024
C = 8
TS = 512
NT = S // TS
NH = 16
HD = 64
KF = 4
LR = 64

STAGE = 3

ENGINES = ("te", "ve", "se", "sq", "gq")
DMA_ENGINES = ("sq", "gq")


class Builder:
    def __init__(self):
        self.ops = {e: [] for e in ENGINES}
        self.lw = {}
        self.rd = {}

    def op(self, eng, fn, r=(), w=()):
        idx = len(self.ops[eng])
        deps = {}

        def add(e, i):
            if e == eng:
                return
            if deps.get(e, -1) < i:
                deps[e] = i

        for reg in r:
            t = self.lw.get(reg)
            if t:
                add(*t)
        for reg in w:
            t = self.lw.get(reg)
            if t:
                add(*t)
            for t2 in self.rd.get(reg, ()):
                add(*t2)
        self.ops[eng].append([fn, deps])
        for reg in r:
            self.rd.setdefault(reg, []).append((eng, idx))
        for reg in w:
            self.lw[reg] = (eng, idx)
            self.rd[reg] = []

    def finalize(self, block, sems):
        marked = {e: set() for e in ENGINES}
        for e in ENGINES:
            for fn, deps in self.ops[e]:
                for de, di in deps.items():
                    if de not in DMA_ENGINES:
                        marked[de].add(di)
        cnt = {}
        for e in ENGINES:
            c = 0
            pref = []
            for i in range(len(self.ops[e])):
                if e in DMA_ENGINES or i in marked[e]:
                    c += 1
                pref.append(c)
            cnt[e] = pref

        def mk(e, handle):
            wm = {}
            for i, (fn, deps) in enumerate(self.ops[e]):
                for de in sorted(deps):
                    di = deps[de]
                    thr = cnt[de][di] * (16 if de in DMA_ENGINES else 1)
                    if wm.get(de, 0) >= thr:
                        continue
                    handle.wait_ge(sems[de], thr)
                    wm[de] = thr
                inst = fn(handle)
                if e in DMA_ENGINES:
                    inst.then_inc(sems[e], 16)
                elif i in marked[e]:
                    inst.then_inc(sems[e], 1)
            if e == "gq":
                handle.wait_ge(sems["gq"], 16 * len(self.ops["gq"]))

        @block.tensor
        def _(h):
            mk("te", h)

        @block.vector
        def _(h):
            mk("ve", h)

        @block.scalar
        def _(h):
            mk("se", h)

        @block.sync
        def _(h):
            mk("sq", h)

        @block.gpsimd
        def _(h):
            mk("gq", h)


def ck(name, lst=None):
    return [f"{name}:{c}" for c in (range(C) if lst is None else lst)]


def build_nc():
    nc = bass.Bass("TRN2", target_bir_lowering=False)
    dram = {}
    dram["x"] = nc.dram_tensor("x", [NB, D, S], F32, kind="ExternalInput").ap()
    dram["x1"] = nc.dram_tensor("x1", [NB, D, S], F32, kind="Internal").ap()
    dram["x2"] = nc.dram_tensor("x2", [NB, D, S], F32, kind="Internal").ap()
    dram["out"] = nc.dram_tensor("out", [NB, D, S], F32, kind="ExternalOutput").ap()
    for nm in ["wq", "wk", "wv", "wo", "wrT", "wkT", "wvT"]:
        dram[nm] = nc.dram_tensor(nm, [D, D], F32, kind="ExternalInput").ap()
    for nm in ["aq", "av", "ao"]:
        dram[nm] = nc.dram_tensor(nm, [D, LR], F32, kind="ExternalInput").ap()
    for nm in ["bq", "bv", "bo"]:
        dram[nm] = nc.dram_tensor(nm, [LR, D], F32, kind="ExternalInput").ap()
    dram["mur"] = nc.dram_tensor("mur", [128, C], F32, kind="ExternalInput").ap()
    dram["muk"] = nc.dram_tensor("muk", [128, C], F32, kind="ExternalInput").ap()
    dram["epsv"] = nc.dram_tensor("epsv", [128, 1], F32, kind="ExternalInput").ap()
    for nm, shp in [("cosP", [128, S]), ("sinP", [128, S]), ("PT", [128, TS]),
                    ("gate", [128, TS]), ("M0", [128, TS]), ("R", [128, 128]),
                    ("ident", [128, 128]), ("ones1", [1, 128]), ("onesln", [128, 1]),
                    ("onesden", [128, 33]), ("ind33a", [128, 33]), ("ind33b", [128, 33]),
                    ("bcA", [128, 128]), ("bcB", [128, 128])]:
        dram[nm] = nc.dram_tensor(nm, shp, F32, kind="ExternalInput").ap()

    es = contextlib.ExitStack()
    with es:
        def sbt(name, shape, dt=BF16):
            return es.enter_context(nc.sbuf_tensor(name, shape, dt)).ap()

        wsl = [sbt(f"wsl{i}", [128, C * D]) for i in range(4)]
        aq_sb = sbt("s_aq_sb", [128, C * LR])
        av_sb = sbt("s_av_sb", [128, C * LR])
        ao_sb = sbt("s_ao_sb", [128, C * LR])
        bq_sb = sbt("s_bq_sb", [LR, D])
        bv_sb = sbt("s_bv_sb", [LR, D])
        bo_sb = sbt("s_bo_sb", [LR, D])
        mur_sb = sbt("s_mur_sb", [128, C], F32)
        muk_sb = sbt("s_muk_sb", [128, C], F32)
        om_r = sbt("s_om_r", [128, C], F32)
        om_k = sbt("s_om_k", [128, C], F32)
        eps_sb = sbt("s_eps_sb", [128, 1], F32)
        cosP = sbt("s_cosP", [128, S])
        sinP = sbt("s_sinP", [128, S])
        PT = sbt("s_PT", [128, TS])
        gate = sbt("s_gate", [128, TS])
        M0 = sbt("s_M0", [128, TS])
        Rm = sbt("s_Rm", [128, 128])
        ident = sbt("s_ident", [128, 128])
        ones1 = sbt("s_ones1", [1, 128])
        onesln = sbt("s_onesln", [128, 1])
        onesden = sbt("s_onesden", [128, 33])
        ind33 = [sbt("s_i33a", [128, 33]), sbt("s_i33b", [128, 33])]
        bcAB = [sbt("s_bcA", [128, 128]), sbt("s_bcB", [128, 128])]
        xt = [sbt(f"xt{i}", [128, C * TS], F32) for i in range(2)]
        pool = [sbt(f"pl{i}", [128, C * TS]) for i in range(5)]
        xn = sbt("s_xn", [128, C * TS])
        xnb = sbt("s_xnb", [128, C])
        fullK = sbt("s_fullK", [128, C * TS])
        fVsm = sbt("s_fVsm", [128, KF * NH * HD])
        expb = [sbt("s_expb0", [128, KF * TS])]
        scr1 = sbt("s_scr1", [128, TS])
        scrA = sbt("s_scrA", [128, TS], F32)
        scrB = sbt("s_scrB", [128, TS], F32)
        ec_sb = sbt("s_ec_sb", [128, TS])
        rden_sb = sbt("s_rden_sb", [128, TS])
        dent_sb = sbt("s_dent_sb", [128, TS], F32)
        rowf = sbt("s_rowf", [1, 3 * TS], F32)
        istd_r = sbt("s_istd_r", [1, TS])
        mistd_r = sbt("s_mistd_r", [1, TS])
        tq_sb = sbt("s_tq_sb", [LR, TS])
        tvl_sb = sbt("s_tvl_sb", [LR, TS])
        to_sb = sbt("s_to_sb", [LR, TS])

        ps = [es.enter_context(nc.psum_tensor(f"ps{i}", [128, TS], F32)).ap()
              for i in range(7)]
        pst = es.enter_context(nc.psum_tensor("pst", [128, 128], BF16)).ap()

        sems = {e: es.enter_context(nc.semaphore(f"sem_{e}")) for e in ENGINES}
        block = es.enter_context(nc.Block())

        b = Builder()

        def gq_dma(dst, src, rkeys, wkeys):
            b.op("gq", lambda h, d=dst, s=src: h.dma_start(out=d, in_=s),
                 r=rkeys, w=wkeys)

        def sq_dma(dst, src, rkeys, wkeys):
            b.op("sq", lambda h, d=dst, s=src: h.dma_start(out=d, in_=s),
                 r=rkeys, w=wkeys)

        for nm, t in [("cosP", cosP), ("sinP", sinP), ("PT", PT), ("gate", gate),
                      ("M0", M0), ("R", Rm), ("ident", ident), ("ones1", ones1),
                      ("onesln", onesln), ("onesden", onesden),
                      ("ind33a", ind33[0]), ("ind33b", ind33[1]),
                      ("bcA", bcAB[0]), ("bcB", bcAB[1])]:
            gq_dma(t[:], dram[nm][:], [], [f"k:{nm}"])
        for nm, t in [("aq", aq_sb), ("av", av_sb), ("ao", ao_sb)]:
            gq_dma(t.rearrange("p (c r) -> p c r", r=LR),
                   dram[nm].rearrange("(c p) r -> p c r", p=128), [], [f"k:{nm}"])
        for nm, t in [("bq", bq_sb), ("bv", bv_sb), ("bo", bo_sb)]:
            gq_dma(t[:], dram[nm][:], [], [f"k:{nm}"])
        for nm, t in [("mur", mur_sb), ("muk", muk_sb), ("epsv", eps_sb)]:
            sq_dma(t[:], dram[nm][:], [], [f"k:{nm}"])
        b.op("ve", lambda h: h.tensor_scalar(out=om_r[:], in0=mur_sb[:], scalar1=-1.0,
                                             scalar2=1.0, op0=ALU.mult, op1=ALU.add),
             r=["k:mur"], w=["k:omr"])
        b.op("ve", lambda h: h.tensor_scalar(out=om_k[:], in0=muk_sb[:], scalar1=-1.0,
                                             scalar2=1.0, op0=ALU.mult, op1=ALU.add),
             r=["k:muk"], w=["k:omk"])
        b.op("ve", lambda h: h.memset(ec_sb[:], 0.0), w=["ec"])
        b.op("ve", lambda h: h.memset(rden_sb[:], 0.0), w=["rden"])

        def load_weights_f():
            for i, nm in enumerate(["wrT", "wkT", "wvT"]):
                gq_dma(wsl[i].rearrange("p (c d) -> p c d", d=D),
                       dram[nm].rearrange("(c p) d -> p c d", p=128),
                       [], [f"w:{i}"])
            b.op("ve", lambda h: h.tensor_scalar(
                out=wsl[2][:], in0=wsl[2][:], scalar1=0.5, scalar2=None,
                op0=ALU.mult), r=["w:2"], w=["w:2"])

        def load_weights_a():
            for i, nm in enumerate(["wq", "wk", "wv", "wo"]):
                gq_dma(wsl[i].rearrange("p (c d) -> p c d", d=D),
                       dram[nm].rearrange("(c p) d -> p c d", p=128),
                       [], [f"w:{i}"])

        def wslice(slot, kc, dc):
            return wsl[slot][:, kc * D + dc * 128: kc * D + dc * 128 + 128]

        def csl(buf, c):
            return buf[:, c * TS:(c + 1) * TS]

        def emit_ln(xtile, xkeys):
            xb_, sq_ = pool[0], pool[1]
            b.op("ve", lambda h: h.tensor_copy(out=xb_[:], in_=xtile[:]),
                 r=xkeys, w=ck("pl0"))
            b.op("se", lambda h: h.activation(out=sq_[:], in_=xb_[:], func=AF.Square),
                 r=ck("pl0"), w=ck("pl1"))
            for c in range(C):
                b.op("te", lambda h, c=c: h.matmul(
                    ps[6][0:1, :], onesln[:], csl(xb_, c),
                    start=(c == 0), stop=(c == C - 1)),
                    r=[f"pl0:{c}", "k:onesln"], w=["ps6"])
            for c in range(C):
                b.op("te", lambda h, c=c: h.matmul(
                    ps[5][0:1, :], onesln[:], csl(sq_, c),
                    start=(c == 0), stop=(c == C - 1)),
                    r=[f"pl1:{c}", "k:onesln"], w=["ps5"])
            b.op("ve", lambda h: h.tensor_copy(out=rowf[0:1, 0:TS], in_=ps[6][0:1, :]),
                 r=["ps6"], w=["rowm"])
            b.op("ve", lambda h: h.tensor_copy(out=rowf[0:1, TS:2 * TS],
                                               in_=ps[5][0:1, :]),
                 r=["ps5"], w=["rowE"])
            b.op("ve", lambda h: h.tensor_tensor(
                out=rowf[0:1, 2 * TS:3 * TS], in0=rowf[0:1, 0:TS], in1=rowf[0:1, 0:TS],
                op=ALU.mult), r=["rowm"], w=["rowv"])
            b.op("ve", lambda h: h.tensor_tensor(
                out=rowf[0:1, 2 * TS:3 * TS], in0=rowf[0:1, TS:2 * TS],
                in1=rowf[0:1, 2 * TS:3 * TS], op=ALU.subtract),
                r=["rowE", "rowv"], w=["rowv"])
            b.op("se", lambda h: h.activation(
                out=rowf[0:1, 2 * TS:3 * TS], in_=rowf[0:1, 2 * TS:3 * TS], func=AF.Ln,
                bias=eps_sb[0:1, :]), r=["rowv", "k:epsv"], w=["rowv"])
            b.op("se", lambda h: h.activation(
                out=istd_r[0:1, :], in_=rowf[0:1, 2 * TS:3 * TS], func=AF.Exp,
                scale=-0.5), r=["rowv"], w=["istd"])
            b.op("ve", lambda h: h.tensor_tensor(
                out=mistd_r[0:1, :], in0=rowf[0:1, 0:TS], in1=istd_r[0:1, :],
                op=ALU.mult), r=["rowm", "istd"], w=["mistd"])
            b.op("te", lambda h: h.matmul(ps[4][:], ones1[:], istd_r[0:1, :],
                                          start=True, stop=True),
                 r=["istd", "k:ones1"], w=["ps4"])
            b.op("te", lambda h: h.matmul(ps[5][:], ones1[:], mistd_r[0:1, :],
                                          start=True, stop=True),
                 r=["mistd", "k:ones1"], w=["ps5"])
            for c in range(C):
                b.op("ve", lambda h, c=c: h.tensor_tensor(
                    out=scrA[:], in0=csl(xb_, c), in1=ps[4][:], op=ALU.mult),
                    r=[f"pl0:{c}", "ps4"], w=["scrA"])
                b.op("ve", lambda h, c=c: h.tensor_tensor(
                    out=csl(xn, c), in0=scrA[:], in1=ps[5][:], op=ALU.subtract),
                    r=["scrA", "ps5"], w=[f"xn:{c}"])

        def emit_f_tile(bi, nt, src, src_nm, dst, dst_nm):
            tok0 = nt * TS
            par = nt % 2
            xtile = xt[par]
            xkeys = ck(f"xt{par}")
            sq_dma(xtile.rearrange("p (c t) -> p c t", t=TS),
                   src[bi].rearrange("(c p) t -> p c t", p=128)[:, :, tok0:tok0 + TS],
                   [f"d:{src_nm}:{bi}:{nt}"], xkeys)
            if nt == 0:
                b.op("ve", lambda h: h.memset(xnb[:], 0.0), r=[], w=["xnb"])
            else:
                xv = xn.rearrange("p (c t) -> p c t", t=TS)
                b.op("ve", lambda h, xv=xv: h.tensor_copy(out=xnb[:],
                                                          in_=xv[:, :, TS - 1]),
                     r=ck("xn"), w=["xnb"])
            emit_ln(xtile, xkeys)
            d_, xr_, xk_ = pool[0], pool[1], pool[2]
            for c in range(C):
                xc = csl(xn, c)
                b.op("ve", lambda h, c=c, xc=xc: h.tensor_tensor(
                    out=d_[:, c * TS + 1:(c + 1) * TS],
                    in0=xc[:, 1:TS], in1=xc[:, 0:TS - 1], op=ALU.subtract),
                    r=[f"xn:{c}"], w=[f"pl0:{c}"])
                b.op("ve", lambda h, c=c, xc=xc: h.tensor_tensor(
                    out=d_[:, c * TS:c * TS + 1],
                    in0=xc[:, 0:1], in1=xnb[:, c:c + 1], op=ALU.subtract),
                    r=[f"xn:{c}", "xnb"], w=[f"pl0:{c}"])
                for mu, dst_, key in ((om_r, xr_, "pl1"), (om_k, xk_, "pl2")):
                    b.op("ve", lambda h, c=c, mu=mu, dst_=dst_: h.tensor_scalar(
                        out=csl(dst_, c), in0=csl(d_, c),
                        scalar1=mu[:, c:c + 1], scalar2=None, op0=ALU.mult),
                        r=[f"pl0:{c}", "k:omr", "k:omk"], w=[f"{key}:{c}"])
                    b.op("ve", lambda h, c=c, dst_=dst_, xc=xc: h.tensor_tensor(
                        out=csl(dst_, c), in0=xc, in1=csl(dst_, c), op=ALU.subtract),
                        r=[f"xn:{c}", f"{key}:{c}"], w=[f"{key}:{c}"])
            r_, ka_ = pool[3], pool[4]
            for dc in range(C):
                bank = dc % 4
                for kc in range(C):
                    b.op("te", lambda h, dc=dc, kc=kc, bank=bank: h.matmul(
                        ps[bank][:], wslice(0, kc, dc), csl(xr_, kc),
                        start=(kc == 0), stop=(kc == C - 1)),
                        r=[f"pl1:{kc}", "w:0"], w=[f"ps{bank}"])
                b.op("se", lambda h, dc=dc, bank=bank: h.activation(
                    out=csl(r_, dc), in_=ps[bank][:], func=AF.Sigmoid),
                    r=[f"ps{bank}"], w=[f"pl3:{dc}"])
            for dc in range(C):
                bank = dc % 4
                for kc in range(C):
                    b.op("te", lambda h, dc=dc, kc=kc, bank=bank: h.matmul(
                        ps[bank][:], wslice(1, kc, dc), csl(xk_, kc),
                        start=(kc == 0), stop=(kc == C - 1)),
                        r=[f"pl2:{kc}", "w:1"], w=[f"ps{bank}"])
                b.op("se", lambda h, dc=dc, bank=bank: h.activation(
                    out=csl(ka_, dc), in_=ps[bank][:], func=AF.Relu),
                    r=[f"ps{bank}"], w=[f"pl4:{dc}"])
                b.op("se", lambda h, dc=dc: h.activation(
                    out=csl(ka_, dc), in_=csl(ka_, dc), func=AF.Square),
                    r=[f"pl4:{dc}"], w=[f"pl4:{dc}"])
            for dc in range(C):
                bank = dc % 4
                for kc in range(C):
                    b.op("te", lambda h, dc=dc, kc=kc, bank=bank: h.matmul(
                        ps[bank][:], wslice(2, kc, dc), csl(ka_, kc),
                        start=(kc == 0), stop=(kc == C - 1)),
                        r=[f"pl4:{kc}", "w:2"], w=[f"ps{bank}"])
                b.op("ve", lambda h, dc=dc, bank=bank: h.tensor_tensor(
                    out=scrB[:], in0=csl(r_, dc), in1=ps[bank][:], op=ALU.mult),
                    r=[f"pl3:{dc}", f"ps{bank}"], w=["scrB"])
                b.op("ve", lambda h, dc=dc: h.tensor_tensor(
                    out=csl(xtile, dc), in0=csl(xtile, dc), in1=scrB[:], op=ALU.add),
                    r=["scrB", f"xt{par}:{dc}"], w=[f"xt{par}:{dc}"])
            gq_dma(dst[bi].rearrange("(c p) t -> p c t", p=128)[:, :, tok0:tok0 + TS],
                   xtile.rearrange("p (c t) -> p c t", t=TS),
                   xkeys, [f"d:{dst_nm}:{bi}:{nt}"])

        def emit_a_tile(bi, nt, src, src_nm, dst, dst_nm):
            tok0 = nt * TS
            par = nt % 2
            xtile = xt[par]
            xkeys = ck(f"xt{par}")
            sq_dma(xtile.rearrange("p (c t) -> p c t", t=TS),
                   src[bi].rearrange("(c p) t -> p c t", p=128)[:, :, tok0:tok0 + TS],
                   [f"d:{src_nm}:{bi}:{nt}"], xkeys)
            emit_ln(xtile, xkeys)
            qsb_, aq_, pK_, pV_ = pool[0], pool[1], pool[2], pool[3]
            for c in range(C):
                b.op("te", lambda h, c=c: h.matmul(
                    ps[3][0:LR, :], aq_sb[:, c * LR:(c + 1) * LR], csl(xn, c),
                    start=(c == 0), stop=(c == C - 1)),
                    r=[f"xn:{c}", "k:aq"], w=["ps3"])
            b.op("ve", lambda h: h.tensor_copy(out=tq_sb[:], in_=ps[3][0:LR, :]),
                 r=["ps3"], w=["tq"])
            for c in range(C):
                b.op("te", lambda h, c=c: h.matmul(
                    ps[3][0:LR, :], av_sb[:, c * LR:(c + 1) * LR], csl(xn, c),
                    start=(c == 0), stop=(c == C - 1)),
                    r=[f"xn:{c}", "k:av"], w=["ps3"])
            b.op("ve", lambda h: h.tensor_copy(out=tvl_sb[:], in_=ps[3][0:LR, :]),
                 r=["ps3"], w=["tvl"])
            cosS = cosP[:, tok0:tok0 + TS]
            sinS = sinP[:, tok0:tok0 + TS]
            PTS = PT[:]
            gateS = gate[:]
            for c in range(C):
                bank = c % 2
                for kc in range(C):
                    b.op("te", lambda h, c=c, kc=kc, bank=bank: h.matmul(
                        ps[bank][:], wslice(0, kc, c), csl(xn, kc),
                        start=(kc == 0), stop=False),
                        r=[f"xn:{kc}", "w:0"], w=[f"ps{bank}"])
                b.op("te", lambda h, c=c, bank=bank: h.matmul(
                    ps[bank][:], bq_sb[:, c * 128:(c + 1) * 128], tq_sb[:],
                    start=False, stop=True), r=["tq", "k:bq"], w=[f"ps{bank}"])
                b.op("ve", lambda h, c=c, bank=bank: h.tensor_copy(
                    out=csl(qsb_, c), in_=ps[bank][:]), r=[f"ps{bank}"], w=[f"pl0:{c}"])
                b.op("te", lambda h, c=c: h.matmul(
                    ps[2][:], Rm[:], csl(qsb_, c), start=True, stop=True),
                    r=[f"pl0:{c}", "k:R"], w=["ps2"])
                b.op("ve", lambda h, c=c: h.tensor_tensor(
                    out=csl(aq_, c), in0=csl(qsb_, c), in1=cosS, op=ALU.mult),
                    r=[f"pl0:{c}", "k:cosP"], w=[f"pl1:{c}"])
                b.op("ve", lambda h, c=c: h.tensor_tensor(
                    out=scr1[:], in0=ps[2][:], in1=sinS, op=ALU.mult),
                    r=["ps2", "k:sinP"], w=["scr1"])
                b.op("ve", lambda h, c=c: h.tensor_tensor(
                    out=csl(aq_, c), in0=csl(aq_, c), in1=scr1[:], op=ALU.add),
                    r=["scr1", f"pl1:{c}"], w=[f"pl1:{c}"])
                for kc in range(C):
                    b.op("te", lambda h, c=c, kc=kc, bank=bank: h.matmul(
                        ps[bank][:], wslice(1, kc, c), csl(xn, kc),
                        start=(kc == 0), stop=(kc == C - 1)),
                        r=[f"xn:{kc}", "w:1"], w=[f"ps{bank}"])
                b.op("ve", lambda h, c=c, bank=bank: h.tensor_copy(
                    out=csl(qsb_, c), in_=ps[bank][:]), r=[f"ps{bank}"], w=[f"pl0:{c}"])
                b.op("te", lambda h, c=c: h.matmul(
                    ps[2][:], Rm[:], csl(qsb_, c), start=True, stop=True),
                    r=[f"pl0:{c}", "k:R"], w=["ps2"])
                b.op("ve", lambda h, c=c: h.tensor_tensor(
                    out=csl(pK_, c), in0=csl(qsb_, c), in1=cosS, op=ALU.mult),
                    r=[f"pl0:{c}", "k:cosP"], w=[f"pl2:{c}"])
                b.op("ve", lambda h, c=c: h.tensor_tensor(
                    out=scr1[:], in0=ps[2][:], in1=sinS, op=ALU.mult),
                    r=["ps2", "k:sinP"], w=["scr1"])
                b.op("ve", lambda h, c=c: h.tensor_tensor(
                    out=csl(pK_, c), in0=csl(pK_, c), in1=scr1[:], op=ALU.add),
                    r=["scr1", f"pl2:{c}"], w=[f"pl2:{c}"])
                b.op("ve", lambda h, c=c: h.tensor_tensor_scan(
                    out=csl(pK_, c), data0=gateS, data1=csl(pK_, c), initial=0.0,
                    op0=ALU.mult, op1=ALU.add),
                    r=[f"pl2:{c}", "k:gate"], w=[f"pl2:{c}"])
                for kc in range(C):
                    b.op("te", lambda h, c=c, kc=kc, bank=bank: h.matmul(
                        ps[bank][:], wslice(2, kc, c), csl(xn, kc),
                        start=(kc == 0), stop=False),
                        r=[f"xn:{kc}", "w:2"], w=[f"ps{bank}"])
                b.op("te", lambda h, c=c, bank=bank: h.matmul(
                    ps[bank][:], bv_sb[:, c * 128:(c + 1) * 128], tvl_sb[:],
                    start=False, stop=True), r=["tvl", "k:bv"], w=[f"ps{bank}"])
                b.op("ve", lambda h, c=c, bank=bank: h.tensor_tensor(
                    out=csl(pV_, c), in0=ps[bank][:], in1=PTS, op=ALU.mult),
                    r=[f"ps{bank}", "k:PT"], w=[f"pl3:{c}"])
                b.op("ve", lambda h, c=c: h.tensor_tensor_scan(
                    out=csl(pV_, c), data0=gateS, data1=csl(pV_, c), initial=0.0,
                    op0=ALU.mult, op1=ALU.add),
                    r=[f"pl3:{c}", "k:gate"], w=[f"pl3:{c}"])
                pKv = csl(pK_, c).rearrange("p (s k) -> p s k", k=KF)
                b.op("ve", lambda h, c=c, pKv=pKv: h.tensor_copy(
                    out=fullK[:, c * TS + nt * 128: c * TS + nt * 128 + 128],
                    in_=pKv[:, :, KF - 1]), r=[f"pl2:{c}"], w=[f"fK:{c}:{nt}"])
                pVv = csl(pV_, c).rearrange("p (s k) -> p s k", k=KF)
                b.op("ve", lambda h, c=c, pVv=pVv: h.tensor_copy(
                    out=scr1[:, 0:128], in_=pVv[:, :, KF - 1]),
                    r=[f"pl3:{c}"], w=["scr1"])
                b.op("te", lambda h, c=c: h.transpose(pst[:], scr1[:, 0:128], ident[:]),
                     r=["scr1", "k:ident"], w=["pst"])
                for j in range(2):
                    hh = 2 * c + j
                    b.op("ve", lambda h, hh=hh, j=j: h.tensor_copy(
                        out=fVsm[:, (nt * NH + hh) * HD: (nt * NH + hh) * HD + HD],
                        in_=pst[:, 64 * j: 64 * j + HD]),
                        r=["pst"], w=[f"fV:{hh}:{nt}"])
                b.op("ve", lambda h, c=c: h.tensor_tensor(
                    out=scr1[:], in0=csl(aq_, c), in1=csl(pK_, c), op=ALU.mult),
                    r=[f"pl1:{c}", f"pl2:{c}"], w=["scr1"])
                for j in range(2):
                    b.op("te", lambda h, j=j: h.matmul(
                        ps[3][32 * j:32 * j + 32, :], ind33[j][:, 0:32], scr1[:],
                        start=True, stop=True, skip_group_check=True),
                        r=["scr1", "k:ind33a", "k:ind33b"], w=["ps3"])
                b.op("se", lambda h: h.activation(
                    out=ec_sb[0:64, :], in_=ps[3][0:64, :], func=AF.Exp,
                    scale=1.0 / 16.0), r=["ps3"], w=["ec"])
                pvbank = 4 + (c % 2)
                for j in range(2):
                    hh = 2 * c + j
                    ebuf = expb[0]
                    ekey = "exp0"
                    for sc in range(nt + 1):
                        b.op("te", lambda h, c=c, j=j, sc=sc: h.matmul(
                            ps[2][:],
                            fullK[64 * j:64 * j + 64,
                                  c * TS + sc * 128: c * TS + sc * 128 + 128],
                            aq_[64 * j:64 * j + 64, c * TS:(c + 1) * TS],
                            start=True, stop=True),
                            r=[f"fK:{c}:{sc}", f"pl1:{c}"], w=["ps2"])
                        b.op("se", lambda h, sc=sc, ebuf=ebuf: h.activation(
                            out=ebuf[:, sc * TS:(sc + 1) * TS], in_=ps[2][:],
                            func=AF.Exp, scale=1.0 / 16.0),
                            r=["ps2"], w=[f"{ekey}:{sc}"])
                    b.op("ve", lambda h, ebuf=ebuf: h.tensor_tensor(
                        out=ebuf[:, nt * TS:(nt + 1) * TS],
                        in0=ebuf[:, nt * TS:(nt + 1) * TS], in1=M0[:],
                        op=ALU.mult), r=[f"{ekey}:{nt}", "k:M0"],
                        w=[f"{ekey}:{nt}"])
                    for sc in range(nt + 1):
                        b.op("te", lambda h, j=j, sc=sc, ebuf=ebuf: h.matmul(
                            ps[6][32 * j:32 * j + 32, :], onesden[:, 0:32],
                            ebuf[:, sc * TS:(sc + 1) * TS],
                            start=(sc == 0), stop=(sc == nt),
                            skip_group_check=True),
                            r=[f"{ekey}:{sc}", "k:onesden"], w=["ps6"])
                    for sc in range(nt + 1):
                        b.op("te", lambda h, hh=hh, j=j, sc=sc, pvbank=pvbank,
                             ebuf=ebuf: h.matmul(
                                 ps[pvbank][64 * j:64 * j + 64, :],
                                 fVsm[:, (sc * NH + hh) * HD:
                                      (sc * NH + hh) * HD + HD],
                                 ebuf[:, sc * TS:(sc + 1) * TS],
                                 start=(sc == 0), stop=(sc == nt),
                                 tile_position=(0, 64 * j),
                                 skip_group_check=True),
                             r=[f"fV:{hh}:{sc}", f"{ekey}:{sc}"],
                             w=[f"ps{pvbank}"])
                b.op("ve", lambda h: h.tensor_tensor(
                    out=dent_sb[0:64, :], in0=ps[6][0:64, :], in1=ec_sb[0:64, :],
                    op=ALU.add), r=["ps6", "ec"], w=["dent"])
                b.op("se", lambda h: h.activation(
                    out=dent_sb[0:64, :], in_=dent_sb[0:64, :], func=AF.Ln),
                    r=["dent"], w=["dent"])
                b.op("se", lambda h: h.activation(
                    out=rden_sb[0:64, :], in_=dent_sb[0:64, :], func=AF.Exp,
                    scale=-1.0), r=["dent"], w=["rden"])
                b.op("te", lambda h: h.matmul(
                    ps[2][:], bcAB[0][:], ec_sb[:], start=True, stop=True),
                    r=["ec", "k:bcA"], w=["ps2"])
                b.op("ve", lambda h, c=c: h.tensor_tensor(
                    out=scrA[:], in0=ps[2][:], in1=csl(pV_, c), op=ALU.mult),
                    r=["ps2", f"pl3:{c}"], w=["scrA"])
                b.op("ve", lambda h, pvbank=pvbank: h.tensor_tensor(
                    out=scrB[:], in0=ps[pvbank][:], in1=scrA[:], op=ALU.add),
                    r=[f"ps{pvbank}", "scrA"], w=["scrB"])
                b.op("te", lambda h: h.matmul(
                    ps[2][:], bcAB[0][:], rden_sb[:], start=True, stop=True),
                    r=["rden", "k:bcA"], w=["ps2"])
                b.op("ve", lambda h: h.tensor_tensor(
                    out=scrB[:], in0=scrB[:], in1=ps[2][:], op=ALU.mult),
                    r=["scrB", "ps2"], w=["scrB"])
                b.op("ve", lambda h, c=c: h.tensor_tensor(
                    out=csl(qsb_, c), in0=scrB[:], in1=PTS, op=ALU.mult),
                    r=["scrB", "k:PT"], w=[f"ao:{c}", f"pl0:{c}"])
            for c in range(C):
                b.op("te", lambda h, c=c: h.matmul(
                    ps[3][0:LR, :], ao_sb[:, c * LR:(c + 1) * LR], csl(qsb_, c),
                    start=(c == 0), stop=(c == C - 1)),
                    r=[f"ao:{c}", "k:ao"], w=["ps3"])
            b.op("ve", lambda h: h.tensor_copy(out=to_sb[:], in_=ps[3][0:LR, :]),
                 r=["ps3"], w=["to"])
            for dc in range(C):
                bank = dc % 2
                for kc in range(C):
                    b.op("te", lambda h, dc=dc, kc=kc, bank=bank: h.matmul(
                        ps[bank][:], wslice(3, kc, dc), csl(qsb_, kc),
                        start=(kc == 0), stop=False),
                        r=[f"ao:{kc}", "w:3"], w=[f"ps{bank}"])
                b.op("te", lambda h, dc=dc, bank=bank: h.matmul(
                    ps[bank][:], bo_sb[:, dc * 128:(dc + 1) * 128], to_sb[:],
                    start=False, stop=True), r=["to", "k:bo"], w=[f"ps{bank}"])
                b.op("ve", lambda h, dc=dc, bank=bank: h.tensor_tensor(
                    out=csl(xtile, dc), in0=csl(xtile, dc), in1=ps[bank][:],
                    op=ALU.add),
                    r=[f"ps{bank}", f"xt{par}:{dc}"], w=[f"xt{par}:{dc}"])
            gq_dma(dst[bi].rearrange("(c p) t -> p c t", p=128)[:, :, tok0:tok0 + TS],
                   xtile.rearrange("p (c t) -> p c t", t=TS),
                   xkeys, [f"d:{dst_nm}:{bi}:{nt}"])

        load_weights_f()
        f1_dst, f1_nm = (dram["out"], "out") if STAGE == 1 else (dram["x1"], "x1")
        for bi in range(NB):
            for nt in range(NT):
                emit_f_tile(bi, nt, dram["x"], "x", f1_dst, f1_nm)
        if STAGE >= 2:
            load_weights_a()
            a_dst, a_nm = (dram["out"], "out") if STAGE == 2 else (dram["x2"], "x2")
            for bi in range(NB):
                for nt in range(NT):
                    emit_a_tile(bi, nt, dram["x1"], "x1", a_dst, a_nm)
        if STAGE >= 3:
            load_weights_f()
            for bi in range(NB):
                for nt in range(NT):
                    emit_f_tile(bi, nt, dram["x2"], "x2", dram["out"], "out")

        b.finalize(block, sems)
    return nc


def make_consts():
    H = np.array([[1, 1, 1, 1], [1, -1, 1, -1], [1, 1, -1, -1], [1, -1, -1, 1]],
                 np.float32)
    i_idx = np.arange(S)
    c_idx = np.arange(128)
    PTm = H[(i_idx % 4)[None, :], (c_idx % 4)[:, None]].astype(np.float32)
    inv_freq = 1.0 / 10000.0 ** (np.arange(0, HD, 2, dtype=np.float64) / HD)
    theta = i_idx[None, :] * inv_freq[(c_idx % HD) % 32][:, None]
    cosP = (np.cos(theta) * PTm).astype(np.float32)
    sinP = (np.sin(theta) * PTm).astype(np.float32)
    gate = np.broadcast_to(
        np.where(i_idx % 4 == 0, 0.0, 1.0).astype(np.float32), (128, S)).copy()
    M0 = (np.arange(128)[:, None] < (np.arange(TS) // KF)[None, :]).astype(np.float32)
    R = np.zeros((128, 128), np.float32)
    for m in range(128):
        mm = m % HD
        base = HD * (m // HD)
        if mm < 32:
            R[base + mm + 32, m] = -1.0
        else:
            R[base + mm - 32, m] = 1.0
    onesden = np.zeros((128, 33), np.float32)
    onesden[:, 0] = 1.0
    ind33a = np.zeros((128, 33), np.float32)
    ind33a[0:64, 0] = 1.0
    ind33b = np.zeros((128, 33), np.float32)
    ind33b[64:128, 0] = 1.0
    bcA = np.zeros((128, 128), np.float32)
    bcA[0, 0:64] = 1.0
    bcA[32, 64:128] = 1.0
    bcB = np.zeros((128, 128), np.float32)
    bcB[64, 0:64] = 1.0
    bcB[96, 64:128] = 1.0
    return dict(cosP=cosP, sinP=sinP, PT=np.ascontiguousarray(PTm[:, :TS]),
                gate=np.ascontiguousarray(gate[:, :TS]), M0=M0, R=R,
                ident=np.eye(128, dtype=np.float32),
                ones1=np.ones((1, 128), np.float32),
                onesln=np.full((128, 1), 1.0 / D, np.float32),
                onesden=onesden, ind33a=ind33a, ind33b=ind33b, bcA=bcA, bcB=bcB,
                epsv=np.full((128, 1), 1e-5, np.float32))


def kernel(**inp):
    x = np.asarray(inp["L0_attn_hidden_states"], np.float32)
    xT = np.ascontiguousarray(x.transpose(0, 2, 1))
    shared = dict(
        wq=np.ascontiguousarray(inp["L0_attn_Wq"], np.float32),
        wk=np.ascontiguousarray(inp["L0_attn_Wk"], np.float32),
        wv=np.ascontiguousarray(inp["L0_attn_Wv"], np.float32),
        wo=np.ascontiguousarray(inp["L0_attn_Wo"], np.float32),
        wrT=np.ascontiguousarray(np.asarray(inp["L0_ffn_W_r"], np.float32).T),
        wkT=np.ascontiguousarray(np.asarray(inp["L0_ffn_W_k"], np.float32).T),
        wvT=np.ascontiguousarray(np.asarray(inp["L0_ffn_W_v"], np.float32).T),
        aq=np.ascontiguousarray(inp["L0_attn_Aq"], np.float32),
        av=np.ascontiguousarray(inp["L0_attn_Av"], np.float32),
        ao=np.ascontiguousarray(inp["L0_attn_Ao"], np.float32),
        bq=np.ascontiguousarray(inp["L0_attn_Bq"], np.float32),
        bv=np.ascontiguousarray(inp["L0_attn_Bv"], np.float32),
        bo=np.ascontiguousarray(inp["L0_attn_Bo"], np.float32),
        mur=np.ascontiguousarray(
            np.asarray(inp["L0_ffn_mu_r"], np.float32).reshape(C, 128).T),
        muk=np.ascontiguousarray(
            np.asarray(inp["L0_ffn_mu_k"], np.float32).reshape(C, 128).T),
    )
    shared.update(make_consts())
    nc = build_nc()
    in_maps = []
    for core in range(NCORES):
        m = dict(shared)
        m["x"] = np.ascontiguousarray(xT[2 * core: 2 * core + 2])
        in_maps.append(m)
    res = run_bass_kernel_spmd(nc, in_maps, core_ids=list(range(NCORES)))
    y = np.concatenate([r["out"] for r in res.results], axis=0)
    return np.ascontiguousarray(y.transpose(0, 2, 1))


# version 02
# speedup vs baseline: 1.0242x; 1.0242x over previous
import contextlib
import numpy as np
import concourse.bass as bass
import concourse.mybir as mybir
from concourse.bass_utils import run_bass_kernel_spmd
from concourse.alu_op_type import AluOpType as ALU

F32 = mybir.dt.float32
BF16 = mybir.dt.bfloat16
AF = mybir.ActivationFunctionType

NCORES = 8
NB = 2
S = 2048
D = 1024
C = 8
TS = 512
NT = S // TS
NH = 16
HD = 64
KF = 4
LR = 64

STAGE = 3

ENGINES = ("te", "ve", "se", "sq", "gq")
DMA_ENGINES = ("sq", "gq")


class Builder:
    def __init__(self):
        self.ops = {e: [] for e in ENGINES}
        self.lw = {}
        self.rd = {}

    def op(self, eng, fn, r=(), w=()):
        idx = len(self.ops[eng])
        deps = {}

        def add(e, i):
            if e == eng:
                return
            if deps.get(e, -1) < i:
                deps[e] = i

        for reg in r:
            t = self.lw.get(reg)
            if t:
                add(*t)
        for reg in w:
            t = self.lw.get(reg)
            if t:
                add(*t)
            for t2 in self.rd.get(reg, ()):
                add(*t2)
        self.ops[eng].append([fn, deps])
        for reg in r:
            self.rd.setdefault(reg, []).append((eng, idx))
        for reg in w:
            self.lw[reg] = (eng, idx)
            self.rd[reg] = []

    def finalize(self, block, sems):
        marked = {e: set() for e in ENGINES}
        for e in ENGINES:
            for fn, deps in self.ops[e]:
                for de, di in deps.items():
                    if de not in DMA_ENGINES:
                        marked[de].add(di)
        cnt = {}
        for e in ENGINES:
            c = 0
            pref = []
            for i in range(len(self.ops[e])):
                if e in DMA_ENGINES or i in marked[e]:
                    c += 1
                pref.append(c)
            cnt[e] = pref

        def mk(e, handle):
            wm = {}
            for i, (fn, deps) in enumerate(self.ops[e]):
                for de in sorted(deps):
                    di = deps[de]
                    thr = cnt[de][di] * (16 if de in DMA_ENGINES else 1)
                    if wm.get(de, 0) >= thr:
                        continue
                    handle.wait_ge(sems[de], thr)
                    wm[de] = thr
                inst = fn(handle)
                if e in DMA_ENGINES:
                    inst.then_inc(sems[e], 16)
                elif i in marked[e]:
                    inst.then_inc(sems[e], 1)
            if e == "gq":
                handle.wait_ge(sems["gq"], 16 * len(self.ops["gq"]))

        @block.tensor
        def _(h):
            mk("te", h)

        @block.vector
        def _(h):
            mk("ve", h)

        @block.scalar
        def _(h):
            mk("se", h)

        @block.sync
        def _(h):
            mk("sq", h)

        @block.gpsimd
        def _(h):
            mk("gq", h)


def ck(name, lst=None):
    return [f"{name}:{c}" for c in (range(C) if lst is None else lst)]


def build_nc():
    nc = bass.Bass("TRN2", target_bir_lowering=False)
    dram = {}
    dram["x"] = nc.dram_tensor("x", [NB, D, S], F32, kind="ExternalInput").ap()
    dram["x1"] = nc.dram_tensor("x1", [NB, D, S], F32, kind="Internal").ap()
    dram["x2"] = nc.dram_tensor("x2", [NB, D, S], F32, kind="Internal").ap()
    dram["out"] = nc.dram_tensor("out", [NB, D, S], F32, kind="ExternalOutput").ap()
    for nm in ["wq", "wk", "wv", "wo", "wrT", "wkT", "wvT"]:
        dram[nm] = nc.dram_tensor(nm, [D, D], F32, kind="ExternalInput").ap()
    for nm in ["aq", "av", "ao"]:
        dram[nm] = nc.dram_tensor(nm, [D, LR], F32, kind="ExternalInput").ap()
    for nm in ["bq", "bv", "bo"]:
        dram[nm] = nc.dram_tensor(nm, [LR, D], F32, kind="ExternalInput").ap()
    dram["mur"] = nc.dram_tensor("mur", [128, C], F32, kind="ExternalInput").ap()
    dram["muk"] = nc.dram_tensor("muk", [128, C], F32, kind="ExternalInput").ap()
    dram["epsv"] = nc.dram_tensor("epsv", [128, 1], F32, kind="ExternalInput").ap()
    dram["cosPb"] = nc.dram_tensor("cosPb", [128, S], BF16, kind="ExternalInput").ap()
    dram["sinPb"] = nc.dram_tensor("sinPb", [128, S], BF16, kind="ExternalInput").ap()
    for nm, shp in [("cosP", [128, S]), ("sinP", [128, S]), ("PT", [128, TS]),
                    ("gate", [128, TS]), ("M0", [128, TS]), ("R", [128, 128]),
                    ("ident", [128, 128]), ("ones1", [1, 128]), ("onesln", [128, 1]),
                    ("onesden", [128, 33]), ("ind33a", [128, 33]), ("ind33b", [128, 33]),
                    ("bcA", [128, 128]), ("bcB", [128, 128])]:
        dram[nm] = nc.dram_tensor(nm, shp, F32, kind="ExternalInput").ap()

    es = contextlib.ExitStack()
    with es:
        def sbt(name, shape, dt=BF16):
            return es.enter_context(nc.sbuf_tensor(name, shape, dt)).ap()

        wsl = [sbt(f"wsl{i}", [128, C * D]) for i in range(4)]
        aq_sb = sbt("s_aq", [128, C * LR])
        av_sb = sbt("s_av", [128, C * LR])
        ao_sb = sbt("s_ao", [128, C * LR])
        bq_sb = sbt("s_bq", [LR, D])
        bv_sb = sbt("s_bv", [LR, D])
        bo_sb = sbt("s_bo", [LR, D])
        mur_sb = sbt("s_mur", [128, C], F32)
        muk_sb = sbt("s_muk", [128, C], F32)
        om_r = sbt("s_omr", [128, C], F32)
        om_k = sbt("s_omk", [128, C], F32)
        eps_sb = sbt("s_eps", [128, 1], F32)
        PT = sbt("s_PT", [128, TS])
        gate = sbt("s_gate", [128, TS])
        M0 = sbt("s_M0", [128, TS])
        Rm = sbt("s_Rm", [128, 128])
        ident = sbt("s_ident", [128, 128])
        ones1 = sbt("s_ones1", [1, 128])
        onesln = sbt("s_onesln", [128, 1])
        onesden = sbt("s_onesden", [128, 33])
        ind33 = [sbt("s_i33a", [128, 33]), sbt("s_i33b", [128, 33])]
        bcA = sbt("s_bcA", [128, 128])
        cs_t = [sbt("s_cos", [128, TS]), sbt("s_sin", [128, TS])]
        xt = [sbt(f"xt{i}", [128, C * TS], F32) for i in range(2)]
        pool = [sbt(f"pl{i}", [128, C * TS]) for i in range(7)]
        xn = sbt("s_xn", [128, C * TS])
        xnb = sbt("s_xnb", [128, C])
        fullK = sbt("s_fullK", [128, C * TS])
        fVsm = sbt("s_fVsm", [128, KF * NH * HD])
        scr1 = sbt("s_scr1", [128, TS])
        scrA = sbt("s_scrA", [128, TS], F32)
        scrB = sbt("s_scrB", [128, TS], F32)
        ec_sb = sbt("s_ec", [128, TS])
        rden_sb = sbt("s_rden", [128, TS])
        rowf = sbt("s_rowf", [1, 2 * TS], F32)
        istd_r = sbt("s_istd", [1, TS])
        mistd_r = sbt("s_mistd", [1, TS])
        tq_sb = sbt("s_tq", [LR, TS])
        tvl_sb = sbt("s_tvl", [LR, TS])

        ps = [es.enter_context(nc.psum_tensor(f"ps{i}", [128, TS], F32)).ap()
              for i in range(7)]
        pst = es.enter_context(nc.psum_tensor("pst", [128, 128], BF16)).ap()

        sems = {e: es.enter_context(nc.semaphore(f"sem_{e}")) for e in ENGINES}
        block = es.enter_context(nc.Block())

        b = Builder()

        def gq_dma(dst, src, rkeys, wkeys):
            b.op("gq", lambda h, d=dst, s=src: h.dma_start(out=d, in_=s),
                 r=rkeys, w=wkeys)

        def sq_dma(dst, src, rkeys, wkeys):
            b.op("sq", lambda h, d=dst, s=src: h.dma_start(out=d, in_=s),
                 r=rkeys, w=wkeys)

        for nm, t in [("PT", PT), ("gate", gate), ("M0", M0), ("R", Rm),
                      ("ident", ident), ("ones1", ones1), ("onesln", onesln),
                      ("onesden", onesden), ("ind33a", ind33[0]),
                      ("ind33b", ind33[1]), ("bcA", bcA)]:
            gq_dma(t[:], dram[nm][:], [], [f"k:{nm}"])
        for nm, t in [("aq", aq_sb), ("av", av_sb), ("ao", ao_sb)]:
            gq_dma(t.rearrange("p (c r) -> p c r", r=LR),
                   dram[nm].rearrange("(c p) r -> p c r", p=128), [], [f"k:{nm}"])
        for nm, t in [("bq", bq_sb), ("bv", bv_sb), ("bo", bo_sb)]:
            gq_dma(t[:], dram[nm][:], [], [f"k:{nm}"])
        for nm, t in [("mur", mur_sb), ("muk", muk_sb), ("epsv", eps_sb)]:
            sq_dma(t[:], dram[nm][:], [], [f"k:{nm}"])
        b.op("ve", lambda h: h.tensor_scalar(out=om_r[:], in0=mur_sb[:], scalar1=-1.0,
                                             scalar2=1.0, op0=ALU.mult, op1=ALU.add),
             r=["k:mur"], w=["k:omr"])
        b.op("ve", lambda h: h.tensor_scalar(out=om_k[:], in0=muk_sb[:], scalar1=-1.0,
                                             scalar2=1.0, op0=ALU.mult, op1=ALU.add),
             r=["k:muk"], w=["k:omk"])
        b.op("ve", lambda h: h.memset(ec_sb[:], 0.0), w=["ec"])
        b.op("ve", lambda h: h.memset(rden_sb[:], 0.0), w=["rden"])

        def load_weights_f():
            for i, nm in enumerate(["wrT", "wkT", "wvT"]):
                gq_dma(wsl[i].rearrange("p (c d) -> p c d", d=D),
                       dram[nm].rearrange("(c p) d -> p c d", p=128),
                       [], [f"w:{i}"])
            b.op("ve", lambda h: h.tensor_scalar(
                out=wsl[2][:], in0=wsl[2][:], scalar1=0.5, scalar2=None,
                op0=ALU.mult), r=["w:2"], w=["w:2"])

        def load_weights_a():
            for i, nm in enumerate(["wq", "wk", "wv", "wo"]):
                gq_dma(wsl[i].rearrange("p (c d) -> p c d", d=D),
                       dram[nm].rearrange("(c p) d -> p c d", p=128),
                       [], [f"w:{i}"])

        def wslice(slot, kc, dc):
            return wsl[slot][:, kc * D + dc * 128: kc * D + dc * 128 + 128]

        def csl(buf, c):
            return buf[:, c * TS:(c + 1) * TS]

        def esl(hp, sc):
            return csl(pool[4], 4 * hp + sc)

        def ekey(hp, sc):
            return f"pl4:{4 * hp + sc}"

        def emit_ln(xtile, xkeys, sq_, sqn):
            xb_ = pool[0]
            b.op("ve", lambda h: h.tensor_copy(out=xb_[:], in_=xtile[:]),
                 r=xkeys, w=ck("pl0"))
            b.op("se", lambda h: h.activation(out=sq_[:], in_=xb_[:], func=AF.Square),
                 r=ck("pl0"), w=ck(sqn))
            for c in range(C):
                b.op("te", lambda h, c=c: h.matmul(
                    ps[6][0:1, :], onesln[:], csl(xb_, c),
                    start=(c == 0), stop=(c == C - 1)),
                    r=[f"pl0:{c}", "k:onesln"], w=["ps6"])
            for c in range(C):
                b.op("te", lambda h, c=c: h.matmul(
                    ps[5][0:1, :], onesln[:], csl(sq_, c),
                    start=(c == 0), stop=(c == C - 1)),
                    r=[f"{sqn}:{c}", "k:onesln"], w=["ps5"])
            b.op("ve", lambda h: h.tensor_copy(out=rowf[0:1, 0:TS], in_=ps[6][0:1, :]),
                 r=["ps6"], w=["rowm"])
            b.op("ve", lambda h: h.tensor_copy(out=rowf[0:1, TS:2 * TS],
                                               in_=ps[5][0:1, :]),
                 r=["ps5"], w=["rowE"])
            b.op("ve", lambda h: h.tensor_tensor(
                out=scrA[0:1, :], in0=rowf[0:1, 0:TS], in1=rowf[0:1, 0:TS],
                op=ALU.mult), r=["rowm"], w=["scrA"])
            b.op("ve", lambda h: h.tensor_tensor(
                out=scrA[0:1, :], in0=rowf[0:1, TS:2 * TS],
                in1=scrA[0:1, :], op=ALU.subtract),
                r=["rowE", "scrA"], w=["scrA"])
            b.op("se", lambda h: h.activation(
                out=scrA[0:1, :], in_=scrA[0:1, :], func=AF.Ln,
                bias=eps_sb[0:1, :]), r=["scrA", "k:epsv"], w=["scrA"])
            b.op("se", lambda h: h.activation(
                out=istd_r[0:1, :], in_=scrA[0:1, :], func=AF.Exp,
                scale=-0.5), r=["scrA"], w=["istd"])
            b.op("ve", lambda h: h.tensor_tensor(
                out=mistd_r[0:1, :], in0=rowf[0:1, 0:TS], in1=istd_r[0:1, :],
                op=ALU.mult), r=["rowm", "istd"], w=["mistd"])
            b.op("te", lambda h: h.matmul(ps[4][:], ones1[:], istd_r[0:1, :],
                                          start=True, stop=True),
                 r=["istd", "k:ones1"], w=["ps4"])
            b.op("te", lambda h: h.matmul(ps[5][:], ones1[:], mistd_r[0:1, :],
                                          start=True, stop=True),
                 r=["mistd", "k:ones1"], w=["ps5"])
            for c in range(C):
                b.op("ve", lambda h, c=c: h.tensor_tensor(
                    out=scrA[:], in0=csl(xb_, c), in1=ps[4][:], op=ALU.mult),
                    r=[f"pl0:{c}", "ps4"], w=["scrA"])
                b.op("ve", lambda h, c=c: h.tensor_tensor(
                    out=csl(xn, c), in0=scrA[:], in1=ps[5][:], op=ALU.subtract),
                    r=["scrA", "ps5"], w=[f"xn:{c}"])

        def f_bufs(nt):
            if nt % 2 == 0:
                return pool[1], pool[2], "pl1", "pl2"
            return pool[5], pool[6], "pl5", "pl6"

        def emit_f_A(bi, nt, src, src_nm):
            tok0 = nt * TS
            par = nt % 2
            xtile = xt[par]
            xkeys = ck(f"xt{par}")
            sq_dma(xtile.rearrange("p (c t) -> p c t", t=TS),
                   src[bi].rearrange("(c p) t -> p c t", p=128)[:, :, tok0:tok0 + TS],
                   [f"d:{src_nm}:{bi}:{nt}"], xkeys)
            if nt == 0:
                b.op("ve", lambda h: h.memset(xnb[:], 0.0), r=[], w=["xnb"])
            else:
                xv = xn.rearrange("p (c t) -> p c t", t=TS)
                b.op("ve", lambda h, xv=xv: h.tensor_copy(out=xnb[:],
                                                          in_=xv[:, :, TS - 1]),
                     r=ck("xn"), w=["xnb"])
            xr_, xk_, xrn, xkn = f_bufs(nt)
            emit_ln(xtile, xkeys, xr_, xrn)
            d_ = pool[0]
            for c in range(C):
                xc = csl(xn, c)
                b.op("ve", lambda h, c=c, xc=xc: h.tensor_tensor(
                    out=d_[:, c * TS + 1:(c + 1) * TS],
                    in0=xc[:, 1:TS], in1=xc[:, 0:TS - 1], op=ALU.subtract),
                    r=[f"xn:{c}"], w=[f"pl0:{c}"])
                b.op("ve", lambda h, c=c, xc=xc: h.tensor_tensor(
                    out=d_[:, c * TS:c * TS + 1],
                    in0=xc[:, 0:1], in1=xnb[:, c:c + 1], op=ALU.subtract),
                    r=[f"xn:{c}", "xnb"], w=[f"pl0:{c}"])
                for mu, dst_, key in ((om_r, xr_, xrn), (om_k, xk_, xkn)):
                    b.op("ve", lambda h, c=c, mu=mu, dst_=dst_: h.tensor_scalar(
                        out=csl(dst_, c), in0=csl(d_, c),
                        scalar1=mu[:, c:c + 1], scalar2=None, op0=ALU.mult),
                        r=[f"pl0:{c}", "k:omr", "k:omk"], w=[f"{key}:{c}"])
                    b.op("ve", lambda h, c=c, dst_=dst_, xc=xc: h.tensor_tensor(
                        out=csl(dst_, c), in0=xc, in1=csl(dst_, c), op=ALU.subtract),
                        r=[f"xn:{c}", f"{key}:{c}"], w=[f"{key}:{c}"])

        def emit_f_B1(bi, nt):
            xr_, xk_, xrn, xkn = f_bufs(nt)
            r_, ka_ = pool[3], pool[4]
            for dc in range(C):
                bank = dc % 4
                for kc in range(C):
                    b.op("te", lambda h, dc=dc, kc=kc, bank=bank: h.matmul(
                        ps[bank][:], wslice(0, kc, dc), csl(xr_, kc),
                        start=(kc == 0), stop=(kc == C - 1)),
                        r=[f"{xrn}:{kc}", "w:0"], w=[f"ps{bank}"])
                b.op("se", lambda h, dc=dc, bank=bank: h.activation(
                    out=csl(r_, dc), in_=ps[bank][:], func=AF.Sigmoid),
                    r=[f"ps{bank}"], w=[f"pl3:{dc}"])
            for dc in range(C):
                bank = dc % 4
                for kc in range(C):
                    b.op("te", lambda h, dc=dc, kc=kc, bank=bank: h.matmul(
                        ps[bank][:], wslice(1, kc, dc), csl(xk_, kc),
                        start=(kc == 0), stop=(kc == C - 1)),
                        r=[f"{xkn}:{kc}", "w:1"], w=[f"ps{bank}"])
                b.op("se", lambda h, dc=dc, bank=bank: h.activation(
                    out=csl(ka_, dc), in_=ps[bank][:], func=AF.Relu),
                    r=[f"ps{bank}"], w=[f"pl4:{dc}"])
                b.op("se", lambda h, dc=dc: h.activation(
                    out=csl(ka_, dc), in_=csl(ka_, dc), func=AF.Square),
                    r=[f"pl4:{dc}"], w=[f"pl4:{dc}"])

        def emit_f_B2(bi, nt, dst, dst_nm):
            tok0 = nt * TS
            par = nt % 2
            xtile = xt[par]
            xkeys = ck(f"xt{par}")
            r_, ka_ = pool[3], pool[4]
            for dc in range(C):
                bank = dc % 4
                for kc in range(C):
                    b.op("te", lambda h, dc=dc, kc=kc, bank=bank: h.matmul(
                        ps[bank][:], wslice(2, kc, dc), csl(ka_, kc),
                        start=(kc == 0), stop=(kc == C - 1)),
                        r=[f"pl4:{kc}", "w:2"], w=[f"ps{bank}"])
                b.op("ve", lambda h, dc=dc, bank=bank: h.tensor_tensor(
                    out=scrB[:], in0=csl(r_, dc), in1=ps[bank][:], op=ALU.mult),
                    r=[f"pl3:{dc}", f"ps{bank}"], w=["scrB"])
                b.op("ve", lambda h, dc=dc: h.tensor_tensor(
                    out=csl(xtile, dc), in0=csl(xtile, dc), in1=scrB[:], op=ALU.add),
                    r=["scrB", f"xt{par}:{dc}"], w=[f"xt{par}:{dc}"])
            gq_dma(dst[bi].rearrange("(c p) t -> p c t", p=128)[:, :, tok0:tok0 + TS],
                   xtile.rearrange("p (c t) -> p c t", t=TS),
                   xkeys, [f"d:{dst_nm}:{bi}:{nt}"])

        def emit_a_tile(bi, nt, src, src_nm, dst, dst_nm):
            tok0 = nt * TS
            par = nt % 2
            xtile = xt[par]
            xkeys = ck(f"xt{par}")
            sq_dma(xtile.rearrange("p (c t) -> p c t", t=TS),
                   src[bi].rearrange("(c p) t -> p c t", p=128)[:, :, tok0:tok0 + TS],
                   [f"d:{src_nm}:{bi}:{nt}"], xkeys)
            sq_dma(cs_t[0][:], dram["cosPb"][:, tok0:tok0 + TS], [], ["cs0"])
            sq_dma(cs_t[1][:], dram["sinPb"][:, tok0:tok0 + TS], [], ["cs1"])
            emit_ln(xtile, xkeys, pool[1], "pl1")
            qsb_, aq_, pK_, pV_ = pool[0], pool[1], pool[2], pool[3]
            for c in range(C):
                b.op("te", lambda h, c=c: h.matmul(
                    ps[3][0:LR, :], aq_sb[:, c * LR:(c + 1) * LR], csl(xn, c),
                    start=(c == 0), stop=(c == C - 1)),
                    r=[f"xn:{c}", "k:aq"], w=["ps3"])
            b.op("ve", lambda h: h.tensor_copy(out=tq_sb[:], in_=ps[3][0:LR, :]),
                 r=["ps3"], w=["tq"])
            for c in range(C):
                b.op("te", lambda h, c=c: h.matmul(
                    ps[3][0:LR, :], av_sb[:, c * LR:(c + 1) * LR], csl(xn, c),
                    start=(c == 0), stop=(c == C - 1)),
                    r=[f"xn:{c}", "k:av"], w=["ps3"])
            b.op("ve", lambda h: h.tensor_copy(out=tvl_sb[:], in_=ps[3][0:LR, :]),
                 r=["ps3"], w=["tvl"])
            cosS = cs_t[0][:]
            sinS = cs_t[1][:]
            PTS = PT[:]
            gateS = gate[:]
            for c in range(C):
                bank = c % 2
                for kc in range(C):
                    b.op("te", lambda h, c=c, kc=kc, bank=bank: h.matmul(
                        ps[bank][:], wslice(0, kc, c), csl(xn, kc),
                        start=(kc == 0), stop=False),
                        r=[f"xn:{kc}", "w:0"], w=[f"ps{bank}"])
                b.op("te", lambda h, c=c, bank=bank: h.matmul(
                    ps[bank][:], bq_sb[:, c * 128:(c + 1) * 128], tq_sb[:],
                    start=False, stop=True), r=["tq", "k:bq"], w=[f"ps{bank}"])
                b.op("ve", lambda h, c=c, bank=bank: h.tensor_copy(
                    out=csl(qsb_, c), in_=ps[bank][:]), r=[f"ps{bank}"], w=[f"pl0:{c}"])
                b.op("te", lambda h, c=c: h.matmul(
                    ps[2][:], Rm[:], csl(qsb_, c), start=True, stop=True),
                    r=[f"pl0:{c}", "k:R"], w=["ps2"])
                b.op("ve", lambda h, c=c: h.tensor_tensor(
                    out=csl(aq_, c), in0=csl(qsb_, c), in1=cosS, op=ALU.mult),
                    r=[f"pl0:{c}", "cs0"], w=[f"pl1:{c}"])
                b.op("ve", lambda h, c=c: h.tensor_tensor(
                    out=scr1[:], in0=ps[2][:], in1=sinS, op=ALU.mult),
                    r=["ps2", "cs1"], w=["scr1"])
                b.op("ve", lambda h, c=c: h.tensor_tensor(
                    out=csl(aq_, c), in0=csl(aq_, c), in1=scr1[:], op=ALU.add),
                    r=["scr1", f"pl1:{c}"], w=[f"pl1:{c}"])
                for kc in range(C):
                    b.op("te", lambda h, c=c, kc=kc, bank=bank: h.matmul(
                        ps[bank][:], wslice(1, kc, c), csl(xn, kc),
                        start=(kc == 0), stop=(kc == C - 1)),
                        r=[f"xn:{kc}", "w:1"], w=[f"ps{bank}"])
                b.op("ve", lambda h, c=c, bank=bank: h.tensor_copy(
                    out=csl(qsb_, c), in_=ps[bank][:]), r=[f"ps{bank}"], w=[f"pl0:{c}"])
                b.op("te", lambda h, c=c: h.matmul(
                    ps[2][:], Rm[:], csl(qsb_, c), start=True, stop=True),
                    r=[f"pl0:{c}", "k:R"], w=["ps2"])
                b.op("ve", lambda h, c=c: h.tensor_tensor(
                    out=csl(pK_, c), in0=csl(qsb_, c), in1=cosS, op=ALU.mult),
                    r=[f"pl0:{c}", "cs0"], w=[f"pl2:{c}"])
                b.op("ve", lambda h, c=c: h.tensor_tensor(
                    out=scr1[:], in0=ps[2][:], in1=sinS, op=ALU.mult),
                    r=["ps2", "cs1"], w=["scr1"])
                b.op("ve", lambda h, c=c: h.tensor_tensor(
                    out=csl(pK_, c), in0=csl(pK_, c), in1=scr1[:], op=ALU.add),
                    r=["scr1", f"pl2:{c}"], w=[f"pl2:{c}"])
                b.op("ve", lambda h, c=c: h.tensor_tensor_scan(
                    out=csl(pK_, c), data0=gateS, data1=csl(pK_, c), initial=0.0,
                    op0=ALU.mult, op1=ALU.add),
                    r=[f"pl2:{c}", "k:gate"], w=[f"pl2:{c}"])
                for kc in range(C):
                    b.op("te", lambda h, c=c, kc=kc, bank=bank: h.matmul(
                        ps[bank][:], wslice(2, kc, c), csl(xn, kc),
                        start=(kc == 0), stop=False),
                        r=[f"xn:{kc}", "w:2"], w=[f"ps{bank}"])
                b.op("te", lambda h, c=c, bank=bank: h.matmul(
                    ps[bank][:], bv_sb[:, c * 128:(c + 1) * 128], tvl_sb[:],
                    start=False, stop=True), r=["tvl", "k:bv"], w=[f"ps{bank}"])
                b.op("ve", lambda h, c=c, bank=bank: h.tensor_tensor(
                    out=csl(pV_, c), in0=ps[bank][:], in1=PTS, op=ALU.mult),
                    r=[f"ps{bank}", "k:PT"], w=[f"pl3:{c}"])
                b.op("ve", lambda h, c=c: h.tensor_tensor_scan(
                    out=csl(pV_, c), data0=gateS, data1=csl(pV_, c), initial=0.0,
                    op0=ALU.mult, op1=ALU.add),
                    r=[f"pl3:{c}", "k:gate"], w=[f"pl3:{c}"])
                pKv = csl(pK_, c).rearrange("p (s k) -> p s k", k=KF)
                b.op("ve", lambda h, c=c, pKv=pKv: h.tensor_copy(
                    out=fullK[:, c * TS + nt * 128: c * TS + nt * 128 + 128],
                    in_=pKv[:, :, KF - 1]), r=[f"pl2:{c}"], w=[f"fK:{c}:{nt}"])
                pVv = csl(pV_, c).rearrange("p (s k) -> p s k", k=KF)
                b.op("ve", lambda h, c=c, pVv=pVv: h.tensor_copy(
                    out=scr1[:, 0:128], in_=pVv[:, :, KF - 1]),
                    r=[f"pl3:{c}"], w=["scr1"])
                b.op("te", lambda h, c=c: h.transpose(pst[:], scr1[:, 0:128], ident[:]),
                     r=["scr1", "k:ident"], w=["pst"])
                for j in range(2):
                    hh = 2 * c + j
                    b.op("ve", lambda h, hh=hh, j=j: h.tensor_copy(
                        out=fVsm[:, (nt * NH + hh) * HD: (nt * NH + hh) * HD + HD],
                        in_=pst[:, 64 * j: 64 * j + HD]),
                        r=["pst"], w=[f"fV:{hh}:{nt}"])
                b.op("ve", lambda h, c=c: h.tensor_tensor(
                    out=scr1[:], in0=csl(aq_, c), in1=csl(pK_, c), op=ALU.mult),
                    r=[f"pl1:{c}", f"pl2:{c}"], w=["scr1"])
                for j in range(2):
                    b.op("te", lambda h, j=j: h.matmul(
                        ps[3][32 * j:32 * j + 32, :], ind33[j][:, 0:32], scr1[:],
                        start=True, stop=True, skip_group_check=True),
                        r=["scr1", "k:ind33a", "k:ind33b"], w=["ps3"])
                b.op("se", lambda h: h.activation(
                    out=ec_sb[0:64, :], in_=ps[3][0:64, :], func=AF.Exp,
                    scale=1.0 / 16.0), r=["ps3"], w=["ec"])
                pvbank = 4 + (c % 2)
                for j in range(2):
                    hh = 2 * c + j
                    ebuf = pool[4]
                    ekey = "pl4"
                    for sc in range(nt + 1):
                        b.op("te", lambda h, c=c, j=j, sc=sc: h.matmul(
                            ps[2][:],
                            fullK[64 * j:64 * j + 64,
                                  c * TS + sc * 128: c * TS + sc * 128 + 128],
                            aq_[64 * j:64 * j + 64, c * TS:(c + 1) * TS],
                            start=True, stop=True),
                            r=[f"fK:{c}:{sc}", f"pl1:{c}"], w=["ps2"])
                        b.op("se", lambda h, sc=sc, ebuf=ebuf: h.activation(
                            out=ebuf[:, sc * TS:(sc + 1) * TS], in_=ps[2][:],
                            func=AF.Exp, scale=1.0 / 16.0),
                            r=["ps2"], w=[f"{ekey}:{sc}"])
                    b.op("ve", lambda h, ebuf=ebuf: h.tensor_tensor(
                        out=ebuf[:, nt * TS:(nt + 1) * TS],
                        in0=ebuf[:, nt * TS:(nt + 1) * TS], in1=M0[:],
                        op=ALU.mult), r=[f"{ekey}:{nt}", "k:M0"],
                        w=[f"{ekey}:{nt}"])
                    for sc in range(nt + 1):
                        b.op("te", lambda h, j=j, sc=sc, ebuf=ebuf: h.matmul(
                            ps[6][32 * j:32 * j + 32, :], onesden[:, 0:32],
                            ebuf[:, sc * TS:(sc + 1) * TS],
                            start=(sc == 0), stop=(sc == nt),
                            skip_group_check=True),
                            r=[f"{ekey}:{sc}", "k:onesden"], w=["ps6"])
                    for sc in range(nt + 1):
                        b.op("te", lambda h, hh=hh, j=j, sc=sc, pvbank=pvbank,
                             ebuf=ebuf: h.matmul(
                                 ps[pvbank][64 * j:64 * j + 64, :],
                                 fVsm[:, (sc * NH + hh) * HD:
                                      (sc * NH + hh) * HD + HD],
                                 ebuf[:, sc * TS:(sc + 1) * TS],
                                 start=(sc == 0), stop=(sc == nt),
                                 tile_position=(0, 64 * j),
                                 skip_group_check=True),
                             r=[f"fV:{hh}:{sc}", f"{ekey}:{sc}"],
                             w=[f"ps{pvbank}"])
                b.op("ve", lambda h: h.tensor_tensor(
                    out=scrA[0:64, :], in0=ps[6][0:64, :], in1=ec_sb[0:64, :],
                    op=ALU.add), r=["ps6", "ec"], w=["scrA"])
                b.op("se", lambda h: h.activation(
                    out=scrA[0:64, :], in_=scrA[0:64, :], func=AF.Ln),
                    r=["scrA"], w=["scrA"])
                b.op("se", lambda h: h.activation(
                    out=rden_sb[0:64, :], in_=scrA[0:64, :], func=AF.Exp,
                    scale=-1.0), r=["scrA"], w=["rden"])
                b.op("te", lambda h: h.matmul(
                    ps[2][:], bcA[:], ec_sb[:], start=True, stop=True),
                    r=["ec", "k:bcA"], w=["ps2"])
                b.op("ve", lambda h, c=c: h.tensor_tensor(
                    out=scrA[:], in0=ps[2][:], in1=csl(pV_, c), op=ALU.mult),
                    r=["ps2", f"pl3:{c}"], w=["scrA"])
                b.op("ve", lambda h, pvbank=pvbank: h.tensor_tensor(
                    out=scrB[:], in0=ps[pvbank][:], in1=scrA[:], op=ALU.add),
                    r=[f"ps{pvbank}", "scrA"], w=["scrB"])
                b.op("te", lambda h: h.matmul(
                    ps[2][:], bcA[:], rden_sb[:], start=True, stop=True),
                    r=["rden", "k:bcA"], w=["ps2"])
                b.op("ve", lambda h: h.tensor_tensor(
                    out=scrB[:], in0=scrB[:], in1=ps[2][:], op=ALU.mult),
                    r=["scrB", "ps2"], w=["scrB"])
                b.op("ve", lambda h, c=c: h.tensor_tensor(
                    out=csl(qsb_, c), in0=scrB[:], in1=PTS, op=ALU.mult),
                    r=["scrB", "k:PT"], w=[f"ao:{c}", f"pl0:{c}"])
            for c in range(C):
                b.op("te", lambda h, c=c: h.matmul(
                    ps[3][0:LR, :], ao_sb[:, c * LR:(c + 1) * LR], csl(qsb_, c),
                    start=(c == 0), stop=(c == C - 1)),
                    r=[f"ao:{c}", "k:ao"], w=["ps3"])
            b.op("ve", lambda h: h.tensor_copy(out=tq_sb[:], in_=ps[3][0:LR, :]),
                 r=["ps3"], w=["tq"])
            for dc in range(C):
                bank = dc % 2
                for kc in range(C):
                    b.op("te", lambda h, dc=dc, kc=kc, bank=bank: h.matmul(
                        ps[bank][:], wslice(3, kc, dc), csl(qsb_, kc),
                        start=(kc == 0), stop=False),
                        r=[f"ao:{kc}", "w:3"], w=[f"ps{bank}"])
                b.op("te", lambda h, dc=dc, bank=bank: h.matmul(
                    ps[bank][:], bo_sb[:, dc * 128:(dc + 1) * 128], tq_sb[:],
                    start=False, stop=True), r=["tq", "k:bo"], w=[f"ps{bank}"])
                b.op("ve", lambda h, dc=dc, bank=bank: h.tensor_tensor(
                    out=csl(xtile, dc), in0=csl(xtile, dc), in1=ps[bank][:],
                    op=ALU.add),
                    r=[f"ps{bank}", f"xt{par}:{dc}"], w=[f"xt{par}:{dc}"])
            gq_dma(dst[bi].rearrange("(c p) t -> p c t", p=128)[:, :, tok0:tok0 + TS],
                   xtile.rearrange("p (c t) -> p c t", t=TS),
                   xkeys, [f"d:{dst_nm}:{bi}:{nt}"])


        tiles = [(bi, nt) for bi in range(NB) for nt in range(NT)]

        def run_f_phase(src, src_nm, dst, dst_nm):
            n = len(tiles)
            emit_f_A(*tiles[0], src, src_nm)
            if n > 1:
                emit_f_A(*tiles[1], src, src_nm)
            for i in range(n):
                emit_f_B1(*tiles[i])
                emit_f_B2(*tiles[i], dst, dst_nm)
                if i + 2 < n:
                    emit_f_A(*tiles[i + 2], src, src_nm)

        def run_a_phase(src, src_nm, dst, dst_nm):
            for bi, nt in tiles:
                emit_a_tile(bi, nt, src, src_nm, dst, dst_nm)

        load_weights_f()
        if STAGE == 1:
            run_f_phase(dram["x"], "x", dram["out"], "out")
        else:
            run_f_phase(dram["x"], "x", dram["x1"], "x1")
            load_weights_a()
            if STAGE == 2:
                run_a_phase(dram["x1"], "x1", dram["out"], "out")
            else:
                run_a_phase(dram["x1"], "x1", dram["x2"], "x2")
                load_weights_f()
                run_f_phase(dram["x2"], "x2", dram["out"], "out")

        b.finalize(block, sems)
    return nc


def make_consts():
    H = np.array([[1, 1, 1, 1], [1, -1, 1, -1], [1, 1, -1, -1], [1, -1, -1, 1]],
                 np.float32)
    i_idx = np.arange(S)
    c_idx = np.arange(128)
    PTm = H[(i_idx % 4)[None, :], (c_idx % 4)[:, None]].astype(np.float32)
    inv_freq = 1.0 / 10000.0 ** (np.arange(0, HD, 2, dtype=np.float64) / HD)
    theta = i_idx[None, :] * inv_freq[(c_idx % HD) % 32][:, None]
    cosP = (np.cos(theta) * PTm).astype(np.float32)
    sinP = (np.sin(theta) * PTm).astype(np.float32)
    gate = np.broadcast_to(
        np.where(i_idx % 4 == 0, 0.0, 1.0).astype(np.float32), (128, S)).copy()
    M0 = (np.arange(128)[:, None] < (np.arange(TS) // KF)[None, :]).astype(np.float32)
    R = np.zeros((128, 128), np.float32)
    for m in range(128):
        mm = m % HD
        base = HD * (m // HD)
        if mm < 32:
            R[base + mm + 32, m] = -1.0
        else:
            R[base + mm - 32, m] = 1.0
    onesden = np.zeros((128, 33), np.float32)
    onesden[:, 0] = 1.0
    ind33a = np.zeros((128, 33), np.float32)
    ind33a[0:64, 0] = 1.0
    ind33b = np.zeros((128, 33), np.float32)
    ind33b[64:128, 0] = 1.0
    bcA = np.zeros((128, 128), np.float32)
    bcA[0, 0:64] = 1.0
    bcA[32, 64:128] = 1.0
    bcB = np.zeros((128, 128), np.float32)
    bcB[64, 0:64] = 1.0
    bcB[96, 64:128] = 1.0
    return dict(cosP=cosP, sinP=sinP, PT=np.ascontiguousarray(PTm[:, :TS]),
                gate=np.ascontiguousarray(gate[:, :TS]), M0=M0, R=R,
                ident=np.eye(128, dtype=np.float32),
                ones1=np.ones((1, 128), np.float32),
                onesln=np.full((128, 1), 1.0 / D, np.float32),
                onesden=onesden, ind33a=ind33a, ind33b=ind33b, bcA=bcA, bcB=bcB,
                epsv=np.full((128, 1), 1e-5, np.float32))


def kernel(**inp):
    x = np.asarray(inp["L0_attn_hidden_states"], np.float32)
    xT = np.ascontiguousarray(x.transpose(0, 2, 1))
    shared = dict(
        wq=np.ascontiguousarray(inp["L0_attn_Wq"], np.float32),
        wk=np.ascontiguousarray(inp["L0_attn_Wk"], np.float32),
        wv=np.ascontiguousarray(inp["L0_attn_Wv"], np.float32),
        wo=np.ascontiguousarray(inp["L0_attn_Wo"], np.float32),
        wrT=np.ascontiguousarray(np.asarray(inp["L0_ffn_W_r"], np.float32).T),
        wkT=np.ascontiguousarray(np.asarray(inp["L0_ffn_W_k"], np.float32).T),
        wvT=np.ascontiguousarray(np.asarray(inp["L0_ffn_W_v"], np.float32).T),
        aq=np.ascontiguousarray(inp["L0_attn_Aq"], np.float32),
        av=np.ascontiguousarray(inp["L0_attn_Av"], np.float32),
        ao=np.ascontiguousarray(inp["L0_attn_Ao"], np.float32),
        bq=np.ascontiguousarray(inp["L0_attn_Bq"], np.float32),
        bv=np.ascontiguousarray(inp["L0_attn_Bv"], np.float32),
        bo=np.ascontiguousarray(inp["L0_attn_Bo"], np.float32),
        mur=np.ascontiguousarray(
            np.asarray(inp["L0_ffn_mu_r"], np.float32).reshape(C, 128).T),
        muk=np.ascontiguousarray(
            np.asarray(inp["L0_ffn_mu_k"], np.float32).reshape(C, 128).T),
    )
    shared.update(make_consts())
    import ml_dtypes
    shared["cosPb"] = shared["cosP"].astype(ml_dtypes.bfloat16)
    shared["sinPb"] = shared["sinP"].astype(ml_dtypes.bfloat16)
    nc = build_nc()
    in_maps = []
    for core in range(NCORES):
        m = dict(shared)
        m["x"] = np.ascontiguousarray(xT[2 * core: 2 * core + 2])
        in_maps.append(m)
    res = run_bass_kernel_spmd(nc, in_maps, core_ids=list(range(NCORES)))
    y = np.concatenate([r["out"] for r in res.results], axis=0)
    return np.ascontiguousarray(y.transpose(0, 2, 1))


# version 03
# speedup vs baseline: 1.0280x; 1.0036x over previous
import contextlib
import numpy as np
import concourse.bass as bass
import concourse.mybir as mybir
from concourse.bass_utils import run_bass_kernel_spmd
from concourse.alu_op_type import AluOpType as ALU

F32 = mybir.dt.float32
BF16 = mybir.dt.bfloat16
AF = mybir.ActivationFunctionType

NCORES = 8
NB = 2
S = 2048
D = 1024
C = 8
TS = 512
NT = S // TS
NH = 16
HD = 64
KF = 4
LR = 64

STAGE = 3

ENGINES = ("te", "ve", "se", "sq", "gq")
DMA_ENGINES = ("sq", "gq")


class Builder:
    def __init__(self):
        self.ops = {e: [] for e in ENGINES}
        self.lw = {}
        self.rd = {}

    def op(self, eng, fn, r=(), w=()):
        idx = len(self.ops[eng])
        deps = {}

        def add(e, i):
            if e == eng:
                return
            if deps.get(e, -1) < i:
                deps[e] = i

        for reg in r:
            t = self.lw.get(reg)
            if t:
                add(*t)
        for reg in w:
            t = self.lw.get(reg)
            if t:
                add(*t)
            for t2 in self.rd.get(reg, ()):
                add(*t2)
        self.ops[eng].append([fn, deps])
        for reg in r:
            self.rd.setdefault(reg, []).append((eng, idx))
        for reg in w:
            self.lw[reg] = (eng, idx)
            self.rd[reg] = []

    def finalize(self, block, sems):
        marked = {e: set() for e in ENGINES}
        for e in ENGINES:
            for fn, deps in self.ops[e]:
                for de, di in deps.items():
                    if de not in DMA_ENGINES:
                        marked[de].add(di)
        cnt = {}
        for e in ENGINES:
            c = 0
            pref = []
            for i in range(len(self.ops[e])):
                if e in DMA_ENGINES or i in marked[e]:
                    c += 1
                pref.append(c)
            cnt[e] = pref

        def mk(e, handle):
            wm = {}
            for i, (fn, deps) in enumerate(self.ops[e]):
                for de in sorted(deps):
                    di = deps[de]
                    thr = cnt[de][di] * (16 if de in DMA_ENGINES else 1)
                    if wm.get(de, 0) >= thr:
                        continue
                    handle.wait_ge(sems[de], thr)
                    wm[de] = thr
                inst = fn(handle)
                if e in DMA_ENGINES:
                    inst.then_inc(sems[e], 16)
                elif i in marked[e]:
                    inst.then_inc(sems[e], 1)
            if e == "gq":
                handle.wait_ge(sems["gq"], 16 * len(self.ops["gq"]))

        @block.tensor
        def _(h):
            mk("te", h)

        @block.vector
        def _(h):
            mk("ve", h)

        @block.scalar
        def _(h):
            mk("se", h)

        @block.sync
        def _(h):
            mk("sq", h)

        @block.gpsimd
        def _(h):
            mk("gq", h)


def ck(name, lst=None):
    return [f"{name}:{c}" for c in (range(C) if lst is None else lst)]


def build_nc():
    nc = bass.Bass("TRN2", target_bir_lowering=False)
    dram = {}
    dram["x"] = nc.dram_tensor("x", [NB, D, S], F32, kind="ExternalInput").ap()
    dram["x1"] = nc.dram_tensor("x1", [NB, D, S], F32, kind="Internal").ap()
    dram["x2"] = nc.dram_tensor("x2", [NB, D, S], F32, kind="Internal").ap()
    dram["out"] = nc.dram_tensor("out", [NB, D, S], F32, kind="ExternalOutput").ap()
    for nm in ["wq", "wk", "wv", "wo", "wrT", "wkT", "wvT"]:
        dram[nm] = nc.dram_tensor(nm, [D, D], F32, kind="ExternalInput").ap()
    for nm in ["aq", "av", "ao"]:
        dram[nm] = nc.dram_tensor(nm, [D, LR], F32, kind="ExternalInput").ap()
    for nm in ["bq", "bv", "bo"]:
        dram[nm] = nc.dram_tensor(nm, [LR, D], F32, kind="ExternalInput").ap()
    dram["mur"] = nc.dram_tensor("mur", [128, C], F32, kind="ExternalInput").ap()
    dram["muk"] = nc.dram_tensor("muk", [128, C], F32, kind="ExternalInput").ap()
    dram["epsv"] = nc.dram_tensor("epsv", [128, 1], F32, kind="ExternalInput").ap()
    dram["cosPb"] = nc.dram_tensor("cosPb", [128, S], BF16, kind="ExternalInput").ap()
    dram["sinPb"] = nc.dram_tensor("sinPb", [128, S], BF16, kind="ExternalInput").ap()
    for nm, shp in [("cosP", [128, S]), ("sinP", [128, S]), ("PT", [128, TS]),
                    ("gate", [128, TS]), ("M0", [128, TS]), ("R", [128, 128]),
                    ("ident", [128, 128]), ("ones1", [1, 128]), ("onesln", [128, 1]),
                    ("onesden", [128, 33]), ("ind33a", [128, 33]), ("ind33b", [128, 33]),
                    ("bcA", [128, 128]), ("bcB", [128, 128])]:
        dram[nm] = nc.dram_tensor(nm, shp, F32, kind="ExternalInput").ap()

    es = contextlib.ExitStack()
    with es:
        def sbt(name, shape, dt=BF16):
            return es.enter_context(nc.sbuf_tensor(name, shape, dt)).ap()

        wsl = [sbt(f"wsl{i}", [128, C * D]) for i in range(4)]
        aq_sb = sbt("s_aq", [128, C * LR])
        av_sb = sbt("s_av", [128, C * LR])
        ao_sb = sbt("s_ao", [128, C * LR])
        bq_sb = sbt("s_bq", [LR, D])
        bv_sb = sbt("s_bv", [LR, D])
        bo_sb = sbt("s_bo", [LR, D])
        mur_sb = sbt("s_mur", [128, C], F32)
        muk_sb = sbt("s_muk", [128, C], F32)
        om_r = sbt("s_omr", [128, C], F32)
        om_k = sbt("s_omk", [128, C], F32)
        eps_sb = sbt("s_eps", [128, 1], F32)
        PT = sbt("s_PT", [128, TS])
        gate = sbt("s_gate", [128, TS])
        M0 = sbt("s_M0", [128, TS])
        Rm = sbt("s_Rm", [128, 128])
        ident = sbt("s_ident", [128, 128])
        ones1 = sbt("s_ones1", [1, 128])
        onesln = sbt("s_onesln", [128, 1])
        onesden = sbt("s_onesden", [128, 33])
        ind33 = [sbt("s_i33a", [128, 33]), sbt("s_i33b", [128, 33])]
        bcA = sbt("s_bcA", [128, 128])
        cs_t = [sbt("s_cos", [128, TS]), sbt("s_sin", [128, TS])]
        xt = [sbt(f"xt{i}", [128, C * TS], F32) for i in range(2)]
        pool = [sbt(f"pl{i}", [128, C * TS]) for i in range(7)]
        xn = sbt("s_xn", [128, C * TS])
        xnb = sbt("s_xnb", [128, C])
        fullK = sbt("s_fullK", [128, C * TS])
        fVsm = sbt("s_fVsm", [128, KF * NH * HD])
        scr1 = sbt("s_scr1", [128, TS])
        scrA = sbt("s_scrA", [128, TS], F32)
        scrB = sbt("s_scrB", [128, TS], F32)
        ec_sb = sbt("s_ec", [128, TS])
        rden_sb = sbt("s_rden", [128, TS])
        rowf = sbt("s_rowf", [1, 2 * TS], F32)
        istd_r = sbt("s_istd", [1, TS])
        mistd_r = sbt("s_mistd", [1, TS])
        tq_sb = sbt("s_tq", [LR, TS])
        tvl_sb = sbt("s_tvl", [LR, TS])

        ps = [es.enter_context(nc.psum_tensor(f"ps{i}", [128, TS], F32)).ap()
              for i in range(7)]
        pst = es.enter_context(nc.psum_tensor("pst", [128, 128], BF16)).ap()

        sems = {e: es.enter_context(nc.semaphore(f"sem_{e}")) for e in ENGINES}
        block = es.enter_context(nc.Block())

        b = Builder()

        def gq_dma(dst, src, rkeys, wkeys):
            b.op("gq", lambda h, d=dst, s=src: h.dma_start(out=d, in_=s),
                 r=rkeys, w=wkeys)

        def sq_dma(dst, src, rkeys, wkeys):
            b.op("sq", lambda h, d=dst, s=src: h.dma_start(out=d, in_=s),
                 r=rkeys, w=wkeys)

        for nm, t in [("PT", PT), ("gate", gate), ("M0", M0), ("R", Rm),
                      ("ident", ident), ("ones1", ones1), ("onesln", onesln),
                      ("onesden", onesden), ("ind33a", ind33[0]),
                      ("ind33b", ind33[1]), ("bcA", bcA)]:
            gq_dma(t[:], dram[nm][:], [], [f"k:{nm}"])
        for nm, t in [("aq", aq_sb), ("av", av_sb), ("ao", ao_sb)]:
            gq_dma(t.rearrange("p (c r) -> p c r", r=LR),
                   dram[nm].rearrange("(c p) r -> p c r", p=128), [], [f"k:{nm}"])
        for nm, t in [("bq", bq_sb), ("bv", bv_sb), ("bo", bo_sb)]:
            gq_dma(t[:], dram[nm][:], [], [f"k:{nm}"])
        for nm, t in [("mur", mur_sb), ("muk", muk_sb), ("epsv", eps_sb)]:
            sq_dma(t[:], dram[nm][:], [], [f"k:{nm}"])
        b.op("ve", lambda h: h.tensor_scalar(out=om_r[:], in0=mur_sb[:], scalar1=-1.0,
                                             scalar2=1.0, op0=ALU.mult, op1=ALU.add),
             r=["k:mur"], w=["k:omr"])
        b.op("ve", lambda h: h.tensor_scalar(out=om_k[:], in0=muk_sb[:], scalar1=-1.0,
                                             scalar2=1.0, op0=ALU.mult, op1=ALU.add),
             r=["k:muk"], w=["k:omk"])
        b.op("ve", lambda h: h.memset(ec_sb[:], 0.0), w=["ec"])
        b.op("ve", lambda h: h.memset(rden_sb[:], 0.0), w=["rden"])

        def load_weights_f():
            for i, nm in enumerate(["wrT", "wkT", "wvT"]):
                gq_dma(wsl[i].rearrange("p (c d) -> p c d", d=D),
                       dram[nm].rearrange("(c p) d -> p c d", p=128),
                       [], [f"w:{i}"])
            b.op("ve", lambda h: h.tensor_scalar(
                out=wsl[2][:], in0=wsl[2][:], scalar1=0.5, scalar2=None,
                op0=ALU.mult), r=["w:2"], w=["w:2"])

        def load_weights_a():
            for i, nm in enumerate(["wq", "wk", "wv", "wo"]):
                gq_dma(wsl[i].rearrange("p (c d) -> p c d", d=D),
                       dram[nm].rearrange("(c p) d -> p c d", p=128),
                       [], [f"w:{i}"])

        def wslice(slot, kc, dc):
            return wsl[slot][:, kc * D + dc * 128: kc * D + dc * 128 + 128]

        def csl(buf, c):
            return buf[:, c * TS:(c + 1) * TS]

        def esl(hp, sc):
            return csl(pool[4], 4 * hp + sc)

        def ekey(hp, sc):
            return f"pl4:{4 * hp + sc}"

        def emit_ln(xtile, xkeys, sq_, sqn):
            xb_ = pool[0]
            b.op("se", lambda h: h.copy(out=xb_[:], in_=xtile[:]),
                 r=xkeys, w=ck("pl0"))
            b.op("se", lambda h: h.activation(out=sq_[:], in_=xb_[:], func=AF.Square),
                 r=ck("pl0"), w=ck(sqn))
            for c in range(C):
                b.op("te", lambda h, c=c: h.matmul(
                    ps[6][0:1, :], onesln[:], csl(xb_, c),
                    start=(c == 0), stop=(c == C - 1)),
                    r=[f"pl0:{c}", "k:onesln"], w=["ps6"])
            for c in range(C):
                b.op("te", lambda h, c=c: h.matmul(
                    ps[5][0:1, :], onesln[:], csl(sq_, c),
                    start=(c == 0), stop=(c == C - 1)),
                    r=[f"{sqn}:{c}", "k:onesln"], w=["ps5"])
            b.op("ve", lambda h: h.tensor_copy(out=rowf[0:1, 0:TS], in_=ps[6][0:1, :]),
                 r=["ps6"], w=["rowm"])
            b.op("ve", lambda h: h.tensor_copy(out=rowf[0:1, TS:2 * TS],
                                               in_=ps[5][0:1, :]),
                 r=["ps5"], w=["rowE"])
            b.op("ve", lambda h: h.tensor_tensor(
                out=scrA[0:1, :], in0=rowf[0:1, 0:TS], in1=rowf[0:1, 0:TS],
                op=ALU.mult), r=["rowm"], w=["scrA"])
            b.op("ve", lambda h: h.tensor_tensor(
                out=scrA[0:1, :], in0=rowf[0:1, TS:2 * TS],
                in1=scrA[0:1, :], op=ALU.subtract),
                r=["rowE", "scrA"], w=["scrA"])
            b.op("se", lambda h: h.activation(
                out=scrA[0:1, :], in_=scrA[0:1, :], func=AF.Ln,
                bias=eps_sb[0:1, :]), r=["scrA", "k:epsv"], w=["scrA"])
            b.op("se", lambda h: h.activation(
                out=istd_r[0:1, :], in_=scrA[0:1, :], func=AF.Exp,
                scale=-0.5), r=["scrA"], w=["istd"])
            b.op("ve", lambda h: h.tensor_tensor(
                out=mistd_r[0:1, :], in0=rowf[0:1, 0:TS], in1=istd_r[0:1, :],
                op=ALU.mult), r=["rowm", "istd"], w=["mistd"])
            b.op("te", lambda h: h.matmul(ps[4][:], ones1[:], istd_r[0:1, :],
                                          start=True, stop=True),
                 r=["istd", "k:ones1"], w=["ps4"])
            b.op("te", lambda h: h.matmul(ps[5][:], ones1[:], mistd_r[0:1, :],
                                          start=True, stop=True),
                 r=["mistd", "k:ones1"], w=["ps5"])
            for c in range(C):
                b.op("ve", lambda h, c=c: h.tensor_tensor(
                    out=scrA[:], in0=csl(xb_, c), in1=ps[4][:], op=ALU.mult),
                    r=[f"pl0:{c}", "ps4"], w=["scrA"])
                b.op("ve", lambda h, c=c: h.tensor_tensor(
                    out=csl(xn, c), in0=scrA[:], in1=ps[5][:], op=ALU.subtract),
                    r=["scrA", "ps5"], w=[f"xn:{c}"])

        def f_bufs(nt):
            if nt % 2 == 0:
                return pool[1], pool[2], "pl1", "pl2"
            return pool[5], pool[6], "pl5", "pl6"

        def emit_f_A(bi, nt, src, src_nm):
            tok0 = nt * TS
            par = nt % 2
            xtile = xt[par]
            xkeys = ck(f"xt{par}")
            sq_dma(xtile.rearrange("p (c t) -> p c t", t=TS),
                   src[bi].rearrange("(c p) t -> p c t", p=128)[:, :, tok0:tok0 + TS],
                   [f"d:{src_nm}:{bi}:{nt}"], xkeys)
            if nt == 0:
                b.op("ve", lambda h: h.memset(xnb[:], 0.0), r=[], w=["xnb"])
            else:
                xv = xn.rearrange("p (c t) -> p c t", t=TS)
                b.op("ve", lambda h, xv=xv: h.tensor_copy(out=xnb[:],
                                                          in_=xv[:, :, TS - 1]),
                     r=ck("xn"), w=["xnb"])
            xr_, xk_, xrn, xkn = f_bufs(nt)
            emit_ln(xtile, xkeys, xr_, xrn)
            d_ = pool[0]
            for c in range(C):
                xc = csl(xn, c)
                b.op("ve", lambda h, c=c, xc=xc: h.tensor_tensor(
                    out=d_[:, c * TS + 1:(c + 1) * TS],
                    in0=xc[:, 1:TS], in1=xc[:, 0:TS - 1], op=ALU.subtract),
                    r=[f"xn:{c}"], w=[f"pl0:{c}"])
                b.op("ve", lambda h, c=c, xc=xc: h.tensor_tensor(
                    out=d_[:, c * TS:c * TS + 1],
                    in0=xc[:, 0:1], in1=xnb[:, c:c + 1], op=ALU.subtract),
                    r=[f"xn:{c}", "xnb"], w=[f"pl0:{c}"])
                for mu, dst_, key in ((om_r, xr_, xrn), (om_k, xk_, xkn)):
                    b.op("ve", lambda h, c=c, mu=mu, dst_=dst_: h.tensor_scalar(
                        out=csl(dst_, c), in0=csl(d_, c),
                        scalar1=mu[:, c:c + 1], scalar2=None, op0=ALU.mult),
                        r=[f"pl0:{c}", "k:omr", "k:omk"], w=[f"{key}:{c}"])
                    b.op("ve", lambda h, c=c, dst_=dst_, xc=xc: h.tensor_tensor(
                        out=csl(dst_, c), in0=xc, in1=csl(dst_, c), op=ALU.subtract),
                        r=[f"xn:{c}", f"{key}:{c}"], w=[f"{key}:{c}"])

        def emit_f_B1(bi, nt):
            xr_, xk_, xrn, xkn = f_bufs(nt)
            r_, ka_ = pool[3], pool[4]
            for dc in range(C):
                bank = dc % 4
                for kc in range(C):
                    b.op("te", lambda h, dc=dc, kc=kc, bank=bank: h.matmul(
                        ps[bank][:], wslice(0, kc, dc), csl(xr_, kc),
                        start=(kc == 0), stop=(kc == C - 1)),
                        r=[f"{xrn}:{kc}", "w:0"], w=[f"ps{bank}"])
                b.op("se", lambda h, dc=dc, bank=bank: h.activation(
                    out=csl(r_, dc), in_=ps[bank][:], func=AF.Sigmoid),
                    r=[f"ps{bank}"], w=[f"pl3:{dc}"])
            for dc in range(C):
                bank = dc % 4
                for kc in range(C):
                    b.op("te", lambda h, dc=dc, kc=kc, bank=bank: h.matmul(
                        ps[bank][:], wslice(1, kc, dc), csl(xk_, kc),
                        start=(kc == 0), stop=(kc == C - 1)),
                        r=[f"{xkn}:{kc}", "w:1"], w=[f"ps{bank}"])
                b.op("se", lambda h, dc=dc, bank=bank: h.activation(
                    out=csl(ka_, dc), in_=ps[bank][:], func=AF.Relu),
                    r=[f"ps{bank}"], w=[f"pl4:{dc}"])
                b.op("se", lambda h, dc=dc: h.activation(
                    out=csl(ka_, dc), in_=csl(ka_, dc), func=AF.Square),
                    r=[f"pl4:{dc}"], w=[f"pl4:{dc}"])

        def emit_f_B2(bi, nt, dst, dst_nm):
            tok0 = nt * TS
            par = nt % 2
            xtile = xt[par]
            xkeys = ck(f"xt{par}")
            r_, ka_ = pool[3], pool[4]
            for dc in range(C):
                bank = dc % 4
                for kc in range(C):
                    b.op("te", lambda h, dc=dc, kc=kc, bank=bank: h.matmul(
                        ps[bank][:], wslice(2, kc, dc), csl(ka_, kc),
                        start=(kc == 0), stop=(kc == C - 1)),
                        r=[f"pl4:{kc}", "w:2"], w=[f"ps{bank}"])
                b.op("ve", lambda h, dc=dc, bank=bank: h.tensor_tensor(
                    out=scrB[:], in0=csl(r_, dc), in1=ps[bank][:], op=ALU.mult),
                    r=[f"pl3:{dc}", f"ps{bank}"], w=["scrB"])
                b.op("ve", lambda h, dc=dc: h.tensor_tensor(
                    out=csl(xtile, dc), in0=csl(xtile, dc), in1=scrB[:], op=ALU.add),
                    r=["scrB", f"xt{par}:{dc}"], w=[f"xt{par}:{dc}"])
            gq_dma(dst[bi].rearrange("(c p) t -> p c t", p=128)[:, :, tok0:tok0 + TS],
                   xtile.rearrange("p (c t) -> p c t", t=TS),
                   xkeys, [f"d:{dst_nm}:{bi}:{nt}:{dc}" for dc in range(C)])

        def emit_a_tile(bi, nt, src, src_nm, dst, dst_nm):
            tok0 = nt * TS
            par = nt % 2
            xtile = xt[par]
            xkeys = ck(f"xt{par}")
            sq_dma(xtile.rearrange("p (c t) -> p c t", t=TS),
                   src[bi].rearrange("(c p) t -> p c t", p=128)[:, :, tok0:tok0 + TS],
                   [f"d:{src_nm}:{bi}:{nt}:{dc}" for dc in range(C)], xkeys)
            sq_dma(cs_t[0][:], dram["cosPb"][:, tok0:tok0 + TS], [], ["cs0"])
            sq_dma(cs_t[1][:], dram["sinPb"][:, tok0:tok0 + TS], [], ["cs1"])
            emit_ln(xtile, xkeys, pool[1], "pl1")
            qsb_, aq_, pK_, pV_ = pool[0], pool[1], pool[2], pool[3]
            for c in range(C):
                b.op("te", lambda h, c=c: h.matmul(
                    ps[3][0:LR, :], aq_sb[:, c * LR:(c + 1) * LR], csl(xn, c),
                    start=(c == 0), stop=(c == C - 1)),
                    r=[f"xn:{c}", "k:aq"], w=["ps3"])
            b.op("ve", lambda h: h.tensor_copy(out=tq_sb[:], in_=ps[3][0:LR, :]),
                 r=["ps3"], w=["tq"])
            for c in range(C):
                b.op("te", lambda h, c=c: h.matmul(
                    ps[3][0:LR, :], av_sb[:, c * LR:(c + 1) * LR], csl(xn, c),
                    start=(c == 0), stop=(c == C - 1)),
                    r=[f"xn:{c}", "k:av"], w=["ps3"])
            b.op("ve", lambda h: h.tensor_copy(out=tvl_sb[:], in_=ps[3][0:LR, :]),
                 r=["ps3"], w=["tvl"])
            cosS = cs_t[0][:]
            sinS = cs_t[1][:]
            PTS = PT[:]
            gateS = gate[:]
            for c in range(C):
                bank = c % 2
                for kc in range(C):
                    b.op("te", lambda h, c=c, kc=kc, bank=bank: h.matmul(
                        ps[bank][:], wslice(0, kc, c), csl(xn, kc),
                        start=(kc == 0), stop=False),
                        r=[f"xn:{kc}", "w:0"], w=[f"ps{bank}"])
                b.op("te", lambda h, c=c, bank=bank: h.matmul(
                    ps[bank][:], bq_sb[:, c * 128:(c + 1) * 128], tq_sb[:],
                    start=False, stop=True), r=["tq", "k:bq"], w=[f"ps{bank}"])
                b.op("ve", lambda h, c=c, bank=bank: h.tensor_copy(
                    out=csl(qsb_, c), in_=ps[bank][:]), r=[f"ps{bank}"], w=[f"pl0:{c}"])
                b.op("te", lambda h, c=c: h.matmul(
                    ps[2][:], Rm[:], csl(qsb_, c), start=True, stop=True),
                    r=[f"pl0:{c}", "k:R"], w=["ps2"])
                b.op("ve", lambda h, c=c: h.tensor_tensor(
                    out=csl(aq_, c), in0=csl(qsb_, c), in1=cosS, op=ALU.mult),
                    r=[f"pl0:{c}", "cs0"], w=[f"pl1:{c}"])
                b.op("ve", lambda h, c=c: h.tensor_tensor(
                    out=scr1[:], in0=ps[2][:], in1=sinS, op=ALU.mult),
                    r=["ps2", "cs1"], w=["scr1"])
                b.op("ve", lambda h, c=c: h.tensor_tensor(
                    out=csl(aq_, c), in0=csl(aq_, c), in1=scr1[:], op=ALU.add),
                    r=["scr1", f"pl1:{c}"], w=[f"pl1:{c}"])
                for kc in range(C):
                    b.op("te", lambda h, c=c, kc=kc, bank=bank: h.matmul(
                        ps[bank][:], wslice(1, kc, c), csl(xn, kc),
                        start=(kc == 0), stop=(kc == C - 1)),
                        r=[f"xn:{kc}", "w:1"], w=[f"ps{bank}"])
                b.op("ve", lambda h, c=c, bank=bank: h.tensor_copy(
                    out=csl(qsb_, c), in_=ps[bank][:]), r=[f"ps{bank}"], w=[f"pl0:{c}"])
                b.op("te", lambda h, c=c: h.matmul(
                    ps[2][:], Rm[:], csl(qsb_, c), start=True, stop=True),
                    r=[f"pl0:{c}", "k:R"], w=["ps2"])
                b.op("ve", lambda h, c=c: h.tensor_tensor(
                    out=csl(pK_, c), in0=csl(qsb_, c), in1=cosS, op=ALU.mult),
                    r=[f"pl0:{c}", "cs0"], w=[f"pl2:{c}"])
                b.op("ve", lambda h, c=c: h.tensor_tensor(
                    out=scr1[:], in0=ps[2][:], in1=sinS, op=ALU.mult),
                    r=["ps2", "cs1"], w=["scr1"])
                b.op("ve", lambda h, c=c: h.tensor_tensor(
                    out=csl(pK_, c), in0=csl(pK_, c), in1=scr1[:], op=ALU.add),
                    r=["scr1", f"pl2:{c}"], w=[f"pl2:{c}"])
                b.op("ve", lambda h, c=c: h.tensor_tensor_scan(
                    out=csl(pK_, c), data0=gateS, data1=csl(pK_, c), initial=0.0,
                    op0=ALU.mult, op1=ALU.add),
                    r=[f"pl2:{c}", "k:gate"], w=[f"pl2:{c}"])
                for kc in range(C):
                    b.op("te", lambda h, c=c, kc=kc, bank=bank: h.matmul(
                        ps[bank][:], wslice(2, kc, c), csl(xn, kc),
                        start=(kc == 0), stop=False),
                        r=[f"xn:{kc}", "w:2"], w=[f"ps{bank}"])
                b.op("te", lambda h, c=c, bank=bank: h.matmul(
                    ps[bank][:], bv_sb[:, c * 128:(c + 1) * 128], tvl_sb[:],
                    start=False, stop=True), r=["tvl", "k:bv"], w=[f"ps{bank}"])
                b.op("ve", lambda h, c=c, bank=bank: h.tensor_tensor(
                    out=csl(pV_, c), in0=ps[bank][:], in1=PTS, op=ALU.mult),
                    r=[f"ps{bank}", "k:PT"], w=[f"pl3:{c}"])
                b.op("ve", lambda h, c=c: h.tensor_tensor_scan(
                    out=csl(pV_, c), data0=gateS, data1=csl(pV_, c), initial=0.0,
                    op0=ALU.mult, op1=ALU.add),
                    r=[f"pl3:{c}", "k:gate"], w=[f"pl3:{c}"])
                pKv = csl(pK_, c).rearrange("p (s k) -> p s k", k=KF)
                b.op("ve", lambda h, c=c, pKv=pKv: h.tensor_copy(
                    out=fullK[:, c * TS + nt * 128: c * TS + nt * 128 + 128],
                    in_=pKv[:, :, KF - 1]), r=[f"pl2:{c}"], w=[f"fK:{c}:{nt}"])
                pVv = csl(pV_, c).rearrange("p (s k) -> p s k", k=KF)
                b.op("ve", lambda h, c=c, pVv=pVv: h.tensor_copy(
                    out=scr1[:, 0:128], in_=pVv[:, :, KF - 1]),
                    r=[f"pl3:{c}"], w=["scr1"])
                b.op("te", lambda h, c=c: h.transpose(pst[:], scr1[:, 0:128], ident[:]),
                     r=["scr1", "k:ident"], w=["pst"])
                for j in range(2):
                    hh = 2 * c + j
                    b.op("ve", lambda h, hh=hh, j=j: h.tensor_copy(
                        out=fVsm[:, (nt * NH + hh) * HD: (nt * NH + hh) * HD + HD],
                        in_=pst[:, 64 * j: 64 * j + HD]),
                        r=["pst"], w=[f"fV:{hh}:{nt}"])
                b.op("ve", lambda h, c=c: h.tensor_tensor(
                    out=scr1[:], in0=csl(aq_, c), in1=csl(pK_, c), op=ALU.mult),
                    r=[f"pl1:{c}", f"pl2:{c}"], w=["scr1"])
                for j in range(2):
                    b.op("te", lambda h, j=j: h.matmul(
                        ps[3][32 * j:32 * j + 32, :], ind33[j][:, 0:32], scr1[:],
                        start=True, stop=True, skip_group_check=True),
                        r=["scr1", "k:ind33a", "k:ind33b"], w=["ps3"])
                b.op("se", lambda h: h.activation(
                    out=ec_sb[0:64, :], in_=ps[3][0:64, :], func=AF.Exp,
                    scale=1.0 / 16.0), r=["ps3"], w=["ec"])
                pvbank = 4 + (c % 2)
                for j in range(2):
                    hh = 2 * c + j
                    ebuf = pool[4]
                    ekey = "pl4"
                    for sc in range(nt + 1):
                        b.op("te", lambda h, c=c, j=j, sc=sc: h.matmul(
                            ps[2][:],
                            fullK[64 * j:64 * j + 64,
                                  c * TS + sc * 128: c * TS + sc * 128 + 128],
                            aq_[64 * j:64 * j + 64, c * TS:(c + 1) * TS],
                            start=True, stop=True),
                            r=[f"fK:{c}:{sc}", f"pl1:{c}"], w=["ps2"])
                        b.op("se", lambda h, sc=sc, ebuf=ebuf: h.activation(
                            out=ebuf[:, sc * TS:(sc + 1) * TS], in_=ps[2][:],
                            func=AF.Exp, scale=1.0 / 16.0),
                            r=["ps2"], w=[f"{ekey}:{sc}"])
                    b.op("ve", lambda h, ebuf=ebuf: h.tensor_tensor(
                        out=ebuf[:, nt * TS:(nt + 1) * TS],
                        in0=ebuf[:, nt * TS:(nt + 1) * TS], in1=M0[:],
                        op=ALU.mult), r=[f"{ekey}:{nt}", "k:M0"],
                        w=[f"{ekey}:{nt}"])
                    for sc in range(nt + 1):
                        b.op("te", lambda h, j=j, sc=sc, ebuf=ebuf: h.matmul(
                            ps[6][32 * j:32 * j + 32, :], onesden[:, 0:32],
                            ebuf[:, sc * TS:(sc + 1) * TS],
                            start=(sc == 0), stop=(sc == nt),
                            skip_group_check=True),
                            r=[f"{ekey}:{sc}", "k:onesden"], w=["ps6"])
                    for sc in range(nt + 1):
                        b.op("te", lambda h, hh=hh, j=j, sc=sc, pvbank=pvbank,
                             ebuf=ebuf: h.matmul(
                                 ps[pvbank][64 * j:64 * j + 64, :],
                                 fVsm[:, (sc * NH + hh) * HD:
                                      (sc * NH + hh) * HD + HD],
                                 ebuf[:, sc * TS:(sc + 1) * TS],
                                 start=(sc == 0), stop=(sc == nt),
                                 tile_position=(0, 64 * j),
                                 skip_group_check=True),
                             r=[f"fV:{hh}:{sc}", f"{ekey}:{sc}"],
                             w=[f"ps{pvbank}"])
                b.op("ve", lambda h: h.tensor_tensor(
                    out=scrA[0:64, :], in0=ps[6][0:64, :], in1=ec_sb[0:64, :],
                    op=ALU.add), r=["ps6", "ec"], w=["scrA"])
                b.op("se", lambda h: h.activation(
                    out=scrA[0:64, :], in_=scrA[0:64, :], func=AF.Ln),
                    r=["scrA"], w=["scrA"])
                b.op("se", lambda h: h.activation(
                    out=rden_sb[0:64, :], in_=scrA[0:64, :], func=AF.Exp,
                    scale=-1.0), r=["scrA"], w=["rden"])
                b.op("te", lambda h: h.matmul(
                    ps[2][:], bcA[:], ec_sb[:], start=True, stop=True),
                    r=["ec", "k:bcA"], w=["ps2"])
                b.op("ve", lambda h, c=c: h.tensor_tensor(
                    out=scrA[:], in0=ps[2][:], in1=csl(pV_, c), op=ALU.mult),
                    r=["ps2", f"pl3:{c}"], w=["scrA"])
                b.op("ve", lambda h, pvbank=pvbank: h.tensor_tensor(
                    out=scrB[:], in0=ps[pvbank][:], in1=scrA[:], op=ALU.add),
                    r=[f"ps{pvbank}", "scrA"], w=["scrB"])
                b.op("te", lambda h: h.matmul(
                    ps[2][:], bcA[:], rden_sb[:], start=True, stop=True),
                    r=["rden", "k:bcA"], w=["ps2"])
                b.op("ve", lambda h: h.tensor_tensor(
                    out=scrB[:], in0=scrB[:], in1=ps[2][:], op=ALU.mult),
                    r=["scrB", "ps2"], w=["scrB"])
                b.op("ve", lambda h, c=c: h.tensor_tensor(
                    out=csl(qsb_, c), in0=scrB[:], in1=PTS, op=ALU.mult),
                    r=["scrB", "k:PT"], w=[f"ao:{c}", f"pl0:{c}"])
            for c in range(C):
                b.op("te", lambda h, c=c: h.matmul(
                    ps[3][0:LR, :], ao_sb[:, c * LR:(c + 1) * LR], csl(qsb_, c),
                    start=(c == 0), stop=(c == C - 1)),
                    r=[f"ao:{c}", f"pl0:{c}", "k:ao"], w=["ps3"])
            b.op("ve", lambda h: h.tensor_copy(out=tq_sb[:], in_=ps[3][0:LR, :]),
                 r=["ps3"], w=["tq"])
            for dc in range(C):
                bank = dc % 2
                for kc in range(C):
                    b.op("te", lambda h, dc=dc, kc=kc, bank=bank: h.matmul(
                        ps[bank][:], wslice(3, kc, dc), csl(qsb_, kc),
                        start=(kc == 0), stop=False),
                        r=[f"ao:{kc}", f"pl0:{kc}", "w:3"], w=[f"ps{bank}"])
                b.op("te", lambda h, dc=dc, bank=bank: h.matmul(
                    ps[bank][:], bo_sb[:, dc * 128:(dc + 1) * 128], tq_sb[:],
                    start=False, stop=True), r=["tq", "k:bo"], w=[f"ps{bank}"])
                b.op("ve", lambda h, dc=dc, bank=bank: h.tensor_tensor(
                    out=csl(xtile, dc), in0=csl(xtile, dc), in1=ps[bank][:],
                    op=ALU.add),
                    r=[f"ps{bank}", f"xt{par}:{dc}"], w=[f"xt{par}:{dc}"])
            gq_dma(dst[bi].rearrange("(c p) t -> p c t", p=128)[:, :, tok0:tok0 + TS],
                   xtile.rearrange("p (c t) -> p c t", t=TS),
                   xkeys, [f"d:{dst_nm}:{bi}:{nt}:{dc}" for dc in range(C)])


        tiles = [(bi, nt) for bi in range(NB) for nt in range(NT)]

        def run_f_phase(src, src_nm, dst, dst_nm):
            n = len(tiles)
            emit_f_A(*tiles[0], src, src_nm)
            emit_f_A(*tiles[1], src, src_nm)
            emit_f_B1(*tiles[0])
            for i in range(1, n):
                emit_f_B2(*tiles[i - 1], dst, dst_nm)
                emit_f_B1(*tiles[i])
                if i + 1 < n:
                    emit_f_A(*tiles[i + 1], src, src_nm)
            emit_f_B2(*tiles[n - 1], dst, dst_nm)

        def run_a_phase(src, src_nm, dst, dst_nm):
            for bi, nt in tiles:
                emit_a_tile(bi, nt, src, src_nm, dst, dst_nm)

        load_weights_f()
        if STAGE == 1:
            run_f_phase(dram["x"], "x", dram["out"], "out")
        else:
            run_f_phase(dram["x"], "x", dram["x1"], "x1")
            load_weights_a()
            if STAGE == 2:
                run_a_phase(dram["x1"], "x1", dram["out"], "out")
            else:
                run_a_phase(dram["x1"], "x1", dram["x2"], "x2")
                load_weights_f()
                run_f_phase(dram["x2"], "x2", dram["out"], "out")

        b.finalize(block, sems)
    return nc


def make_consts():
    H = np.array([[1, 1, 1, 1], [1, -1, 1, -1], [1, 1, -1, -1], [1, -1, -1, 1]],
                 np.float32)
    i_idx = np.arange(S)
    c_idx = np.arange(128)
    PTm = H[(i_idx % 4)[None, :], (c_idx % 4)[:, None]].astype(np.float32)
    inv_freq = 1.0 / 10000.0 ** (np.arange(0, HD, 2, dtype=np.float64) / HD)
    theta = i_idx[None, :] * inv_freq[(c_idx % HD) % 32][:, None]
    cosP = (np.cos(theta) * PTm).astype(np.float32)
    sinP = (np.sin(theta) * PTm).astype(np.float32)
    gate = np.broadcast_to(
        np.where(i_idx % 4 == 0, 0.0, 1.0).astype(np.float32), (128, S)).copy()
    M0 = (np.arange(128)[:, None] < (np.arange(TS) // KF)[None, :]).astype(np.float32)
    R = np.zeros((128, 128), np.float32)
    for m in range(128):
        mm = m % HD
        base = HD * (m // HD)
        if mm < 32:
            R[base + mm + 32, m] = -1.0
        else:
            R[base + mm - 32, m] = 1.0
    onesden = np.zeros((128, 33), np.float32)
    onesden[:, 0] = 1.0
    ind33a = np.zeros((128, 33), np.float32)
    ind33a[0:64, 0] = 1.0
    ind33b = np.zeros((128, 33), np.float32)
    ind33b[64:128, 0] = 1.0
    bcA = np.zeros((128, 128), np.float32)
    bcA[0, 0:64] = 1.0
    bcA[32, 64:128] = 1.0
    bcB = np.zeros((128, 128), np.float32)
    bcB[64, 0:64] = 1.0
    bcB[96, 64:128] = 1.0
    return dict(cosP=cosP, sinP=sinP, PT=np.ascontiguousarray(PTm[:, :TS]),
                gate=np.ascontiguousarray(gate[:, :TS]), M0=M0, R=R,
                ident=np.eye(128, dtype=np.float32),
                ones1=np.ones((1, 128), np.float32),
                onesln=np.full((128, 1), 1.0 / D, np.float32),
                onesden=onesden, ind33a=ind33a, ind33b=ind33b, bcA=bcA, bcB=bcB,
                epsv=np.full((128, 1), 1e-5, np.float32))


def kernel(**inp):
    x = np.asarray(inp["L0_attn_hidden_states"], np.float32)
    xT = np.ascontiguousarray(x.transpose(0, 2, 1))
    shared = dict(
        wq=np.ascontiguousarray(inp["L0_attn_Wq"], np.float32),
        wk=np.ascontiguousarray(inp["L0_attn_Wk"], np.float32),
        wv=np.ascontiguousarray(inp["L0_attn_Wv"], np.float32),
        wo=np.ascontiguousarray(inp["L0_attn_Wo"], np.float32),
        wrT=np.ascontiguousarray(np.asarray(inp["L0_ffn_W_r"], np.float32).T),
        wkT=np.ascontiguousarray(np.asarray(inp["L0_ffn_W_k"], np.float32).T),
        wvT=np.ascontiguousarray(np.asarray(inp["L0_ffn_W_v"], np.float32).T),
        aq=np.ascontiguousarray(inp["L0_attn_Aq"], np.float32),
        av=np.ascontiguousarray(inp["L0_attn_Av"], np.float32),
        ao=np.ascontiguousarray(inp["L0_attn_Ao"], np.float32),
        bq=np.ascontiguousarray(inp["L0_attn_Bq"], np.float32),
        bv=np.ascontiguousarray(inp["L0_attn_Bv"], np.float32),
        bo=np.ascontiguousarray(inp["L0_attn_Bo"], np.float32),
        mur=np.ascontiguousarray(
            np.asarray(inp["L0_ffn_mu_r"], np.float32).reshape(C, 128).T),
        muk=np.ascontiguousarray(
            np.asarray(inp["L0_ffn_mu_k"], np.float32).reshape(C, 128).T),
    )
    shared.update(make_consts())
    import ml_dtypes
    shared["cosPb"] = shared["cosP"].astype(ml_dtypes.bfloat16)
    shared["sinPb"] = shared["sinP"].astype(ml_dtypes.bfloat16)
    nc = build_nc()
    in_maps = []
    for core in range(NCORES):
        m = dict(shared)
        m["x"] = np.ascontiguousarray(xT[2 * core: 2 * core + 2])
        in_maps.append(m)
    res = run_bass_kernel_spmd(nc, in_maps, core_ids=list(range(NCORES)))
    y = np.concatenate([r["out"] for r in res.results], axis=0)
    return np.ascontiguousarray(y.transpose(0, 2, 1))


# version 06
# speedup vs baseline: 1.0754x; 1.0462x over previous
import contextlib
import numpy as np
import concourse.bass as bass
import concourse.mybir as mybir
from concourse.bass_utils import run_bass_kernel_spmd
from concourse.alu_op_type import AluOpType as ALU

F32 = mybir.dt.float32
BF16 = mybir.dt.bfloat16
AF = mybir.ActivationFunctionType

NCORES = 8
NB = 2
S = 2048
D = 1024
C = 8
TS = 512
NT = S // TS
NH = 16
HD = 64
KF = 4
LR = 64

STAGE = 3

ENGINES = ("te", "ve", "se", "sq", "gq")
DMA_ENGINES = ("sq", "gq")


class Builder:
    def __init__(self):
        self.ops = {e: [] for e in ENGINES}
        self.lw = {}
        self.rd = {}

    def op(self, eng, fn, r=(), w=(), dma=False):
        idx = len(self.ops[eng])
        deps = {}

        def add(e, i):
            if e == eng:
                return
            if deps.get(e, -1) < i:
                deps[e] = i

        for reg in r:
            t = self.lw.get(reg)
            if t:
                add(*t)
        for reg in w:
            t = self.lw.get(reg)
            if t:
                add(*t)
            for t2 in self.rd.get(reg, ()):
                add(*t2)
        self.ops[eng].append([fn, deps, dma])
        for reg in r:
            self.rd.setdefault(reg, []).append((eng, idx))
        for reg in w:
            self.lw[reg] = (eng, idx)
            self.rd[reg] = []

    def finalize(self, block, sems):
        marked = {e: set() for e in ENGINES}
        for e in ENGINES:
            for fn, deps, dma in self.ops[e]:
                for de, di in deps.items():
                    if not self.ops[de][di][2]:
                        marked[de].add(di)
        cnt = {}
        for e in ENGINES:
            c = 0
            pref = []
            for i, (fn, deps, dma) in enumerate(self.ops[e]):
                if dma:
                    c += 16
                elif i in marked[e]:
                    c += 1
                pref.append(c)
            cnt[e] = pref

        def mk(e, handle):
            wm = {}
            for i, (fn, deps, dma) in enumerate(self.ops[e]):
                for de in sorted(deps):
                    di = deps[de]
                    thr = cnt[de][di]
                    if wm.get(de, 0) >= thr:
                        continue
                    handle.wait_ge(sems[de], thr)
                    wm[de] = thr
                inst = fn(handle)
                if dma:
                    inst.then_inc(sems[e], 16)
                elif i in marked[e]:
                    inst.then_inc(sems[e], 1)
            if e == "gq" and cnt["gq"]:
                handle.wait_ge(sems["gq"], cnt["gq"][-1])

        @block.tensor
        def _(h):
            mk("te", h)

        @block.vector
        def _(h):
            mk("ve", h)

        @block.scalar
        def _(h):
            mk("se", h)

        @block.sync
        def _(h):
            mk("sq", h)

        @block.gpsimd
        def _(h):
            mk("gq", h)


def ck(name, lst=None):
    return [f"{name}:{c}" for c in (range(C) if lst is None else lst)]


def build_nc():
    nc = bass.Bass("TRN2", target_bir_lowering=False)
    dram = {}
    dram["x"] = nc.dram_tensor("x", [NB, D, S], F32, kind="ExternalInput").ap()
    dram["x1"] = nc.dram_tensor("x1", [NB, D, S], F32, kind="Internal").ap()
    dram["x2"] = nc.dram_tensor("x2", [NB, D, S], F32, kind="Internal").ap()
    dram["out"] = nc.dram_tensor("out", [NB, D, S], F32, kind="ExternalOutput").ap()
    for nm in ["wq", "wk", "wv", "wo", "wrT", "wkT", "wvT"]:
        dram[nm] = nc.dram_tensor(nm, [D, D], F32, kind="ExternalInput").ap()
    for nm in ["aq", "av", "ao"]:
        dram[nm] = nc.dram_tensor(nm, [D, LR], F32, kind="ExternalInput").ap()
    for nm in ["bq", "bv", "bo"]:
        dram[nm] = nc.dram_tensor(nm, [LR, D], F32, kind="ExternalInput").ap()
    dram["mur"] = nc.dram_tensor("mur", [128, C], F32, kind="ExternalInput").ap()
    dram["muk"] = nc.dram_tensor("muk", [128, C], F32, kind="ExternalInput").ap()
    dram["epsv"] = nc.dram_tensor("epsv", [128, 1], F32, kind="ExternalInput").ap()
    dram["cosPb"] = nc.dram_tensor("cosPb", [128, S], BF16, kind="ExternalInput").ap()
    dram["sinPb"] = nc.dram_tensor("sinPb", [128, S], BF16, kind="ExternalInput").ap()
    for nm, shp in [("cosP", [128, S]), ("sinP", [128, S]), ("PT", [128, TS]),
                    ("gate", [128, TS]), ("M0", [128, TS]), ("R", [128, 128]),
                    ("ident", [128, 128]), ("ones1", [1, 128]), ("onesln", [128, 1]),
                    ("onesden", [128, 33]), ("ind33a", [128, 33]), ("ind33b", [128, 33]),
                    ("bcA", [128, 128]), ("bcB", [128, 128])]:
        dram[nm] = nc.dram_tensor(nm, shp, F32, kind="ExternalInput").ap()

    es = contextlib.ExitStack()
    with es:
        def sbt(name, shape, dt=BF16):
            return es.enter_context(nc.sbuf_tensor(name, shape, dt)).ap()

        wsl = [sbt(f"wsl{i}", [128, C * D]) for i in range(4)]
        aq_sb = sbt("s_aq", [128, C * LR])
        av_sb = sbt("s_av", [128, C * LR])
        ao_sb = sbt("s_ao", [128, C * LR])
        bq_sb = sbt("s_bq", [LR, D])
        bv_sb = sbt("s_bv", [LR, D])
        bo_sb = sbt("s_bo", [LR, D])
        mur_sb = sbt("s_mur", [128, C], F32)
        muk_sb = sbt("s_muk", [128, C], F32)
        om_r = sbt("s_omr", [128, C], F32)
        om_k = sbt("s_omk", [128, C], F32)
        eps_sb = sbt("s_eps", [128, 1], F32)
        PT = sbt("s_PT", [128, TS])
        gate = sbt("s_gate", [128, TS])
        M0 = sbt("s_M0", [128, TS])
        Rm = sbt("s_Rm", [128, 128])
        ident = sbt("s_ident", [128, 128])
        ones1 = sbt("s_ones1", [1, 128])
        onesln = sbt("s_onesln", [128, 1])
        onesden = sbt("s_onesden", [128, 33])
        ind33 = [sbt("s_i33a", [128, 33]), sbt("s_i33b", [128, 33])]
        bcA = sbt("s_bcA", [128, 128])
        cs_t = [sbt("s_cos", [128, TS]), sbt("s_sin", [128, TS])]
        xt = [sbt(f"xt{i}", [128, C * TS], F32) for i in range(2)]
        pool = [sbt(f"pl{i}", [128, C * TS]) for i in range(7)]
        xn = sbt("s_xn", [128, C * TS])
        xnb = sbt("s_xnb", [128, C])
        fullK = sbt("s_fullK", [128, C * TS])
        fVsm = sbt("s_fVsm", [128, KF * NH * HD])
        scr1 = sbt("s_scr1", [128, TS])
        scrA = sbt("s_scrA", [128, TS], F32)
        scrB = sbt("s_scrB", [128, TS], F32)
        ec_sb = sbt("s_ec", [128, TS])
        rden_sb = sbt("s_rden", [128, TS])
        rowf = sbt("s_rowf", [1, 2 * TS], F32)
        istd_r = sbt("s_istd", [1, TS])
        mistd_r = sbt("s_mistd", [1, TS])
        tq_sb = sbt("s_tq", [LR, TS])
        tvl_sb = sbt("s_tvl", [LR, TS])

        ps = [es.enter_context(nc.psum_tensor(f"ps{i}", [128, TS], F32)).ap()
              for i in range(7)]
        pst = es.enter_context(nc.psum_tensor("pst", [128, 128], BF16)).ap()

        sems = {e: es.enter_context(nc.semaphore(f"sem_{e}")) for e in ENGINES}
        block = es.enter_context(nc.Block())

        b = Builder()

        def gq_dma(dst, src, rkeys, wkeys):
            b.op("gq", lambda h, d=dst, s=src: h.dma_start(out=d, in_=s),
                 r=rkeys, w=wkeys, dma=True)

        def sq_dma(dst, src, rkeys, wkeys):
            b.op("sq", lambda h, d=dst, s=src: h.dma_start(out=d, in_=s),
                 r=rkeys, w=wkeys, dma=True)

        for nm, t in [("PT", PT), ("gate", gate), ("M0", M0), ("R", Rm),
                      ("ident", ident), ("ones1", ones1), ("onesln", onesln),
                      ("onesden", onesden), ("ind33a", ind33[0]),
                      ("ind33b", ind33[1]), ("bcA", bcA)]:
            gq_dma(t[:], dram[nm][:], [], [f"k:{nm}"])
        for nm, t in [("aq", aq_sb), ("av", av_sb), ("ao", ao_sb)]:
            gq_dma(t.rearrange("p (c r) -> p c r", r=LR),
                   dram[nm].rearrange("(c p) r -> p c r", p=128), [], [f"k:{nm}"])
        for nm, t in [("bq", bq_sb), ("bv", bv_sb), ("bo", bo_sb)]:
            gq_dma(t[:], dram[nm][:], [], [f"k:{nm}"])
        for nm, t in [("mur", mur_sb), ("muk", muk_sb), ("epsv", eps_sb)]:
            sq_dma(t[:], dram[nm][:], [], [f"k:{nm}"])
        b.op("ve", lambda h: h.tensor_scalar(out=om_r[:], in0=mur_sb[:], scalar1=-1.0,
                                             scalar2=1.0, op0=ALU.mult, op1=ALU.add),
             r=["k:mur"], w=["k:omr"])
        b.op("ve", lambda h: h.tensor_scalar(out=om_k[:], in0=muk_sb[:], scalar1=-1.0,
                                             scalar2=1.0, op0=ALU.mult, op1=ALU.add),
             r=["k:muk"], w=["k:omk"])
        b.op("ve", lambda h: h.memset(ec_sb[:], 0.0), w=["ec"])
        b.op("ve", lambda h: h.memset(rden_sb[:], 0.0), w=["rden"])

        def load_weights_f():
            for i, nm in enumerate(["wrT", "wkT", "wvT"]):
                gq_dma(wsl[i].rearrange("p (c d) -> p c d", d=D),
                       dram[nm].rearrange("(c p) d -> p c d", p=128),
                       [], [f"w:{i}"])
            b.op("ve", lambda h: h.tensor_scalar(
                out=wsl[2][:], in0=wsl[2][:], scalar1=0.5, scalar2=None,
                op0=ALU.mult), r=["w:2"], w=["w:2"])

        def load_weights_a():
            for i, nm in enumerate(["wq", "wk", "wv", "wo"]):
                gq_dma(wsl[i].rearrange("p (c d) -> p c d", d=D),
                       dram[nm].rearrange("(c p) d -> p c d", p=128),
                       [], [f"w:{i}"])

        def wslice(slot, kc, dc):
            return wsl[slot][:, kc * D + dc * 128: kc * D + dc * 128 + 128]

        def csl(buf, c):
            return buf[:, c * TS:(c + 1) * TS]

        def esl(hp, sc):
            return csl(pool[4], 4 * hp + sc)

        def ekey(hp, sc):
            return f"pl4:{4 * hp + sc}"

        def emit_ln(xtile, xkeys, sq_, sqn, xnb_=None, xnk="xn"):
            if xnb_ is None:
                xnb_ = xn
            """LN of fp32 tile -> xn (bf16). xb: pool[0]; sq into caller buffer."""
            xb_ = pool[0]
            b.op("se", lambda h: h.copy(out=xb_[:], in_=xtile[:]),
                 r=xkeys, w=ck("pl0"))
            b.op("se", lambda h: h.activation(out=sq_[:], in_=xb_[:], func=AF.Square),
                 r=ck("pl0"), w=ck(sqn))
            for c in range(C):
                b.op("te", lambda h, c=c: h.matmul(
                    ps[6][0:1, :], onesln[:], csl(xb_, c),
                    start=(c == 0), stop=(c == C - 1)),
                    r=[f"pl0:{c}", "k:onesln"], w=["ps6"])
            for c in range(C):
                b.op("te", lambda h, c=c: h.matmul(
                    ps[5][0:1, :], onesln[:], csl(sq_, c),
                    start=(c == 0), stop=(c == C - 1)),
                    r=[f"{sqn}:{c}", "k:onesln"], w=["ps5"])
            b.op("ve", lambda h: h.tensor_copy(out=rowf[0:1, 0:TS], in_=ps[6][0:1, :]),
                 r=["ps6"], w=["rowm"])
            b.op("ve", lambda h: h.tensor_copy(out=rowf[0:1, TS:2 * TS],
                                               in_=ps[5][0:1, :]),
                 r=["ps5"], w=["rowE"])
            b.op("ve", lambda h: h.tensor_tensor(
                out=scrA[0:1, :], in0=rowf[0:1, 0:TS], in1=rowf[0:1, 0:TS],
                op=ALU.mult), r=["rowm"], w=["scrA"])
            b.op("ve", lambda h: h.tensor_tensor(
                out=scrA[0:1, :], in0=rowf[0:1, TS:2 * TS],
                in1=scrA[0:1, :], op=ALU.subtract),
                r=["rowE", "scrA"], w=["scrA"])
            b.op("se", lambda h: h.activation(
                out=scrA[0:1, :], in_=scrA[0:1, :], func=AF.Ln,
                bias=eps_sb[0:1, :]), r=["scrA", "k:epsv"], w=["scrA"])
            b.op("se", lambda h: h.activation(
                out=istd_r[0:1, :], in_=scrA[0:1, :], func=AF.Exp,
                scale=-0.5), r=["scrA"], w=["istd"])
            b.op("ve", lambda h: h.tensor_tensor(
                out=mistd_r[0:1, :], in0=rowf[0:1, 0:TS], in1=istd_r[0:1, :],
                op=ALU.mult), r=["rowm", "istd"], w=["mistd"])
            b.op("te", lambda h: h.matmul(ps[4][:], ones1[:], istd_r[0:1, :],
                                          start=True, stop=True),
                 r=["istd", "k:ones1"], w=["ps4"])
            b.op("te", lambda h: h.matmul(ps[5][:], ones1[:], mistd_r[0:1, :],
                                          start=True, stop=True),
                 r=["mistd", "k:ones1"], w=["ps5"])
            for c in range(C):
                b.op("ve", lambda h, c=c: h.tensor_tensor(
                    out=scrA[:], in0=csl(xb_, c), in1=ps[4][:], op=ALU.mult),
                    r=[f"pl0:{c}", "ps4"], w=["scrA"])
                b.op("ve", lambda h, c=c, xnb_=xnb_: h.tensor_tensor(
                    out=csl(xnb_, c), in0=scrA[:], in1=ps[5][:], op=ALU.subtract),
                    r=["scrA", "ps5"], w=[f"{xnk}:{c}"])

        def f_bufs(nt):
            if nt % 2 == 0:
                return pool[1], pool[2], "pl1", "pl2"
            return pool[5], pool[6], "pl5", "pl6"

        def emit_f_A(bi, nt, src, src_nm):
            tok0 = nt * TS
            par = nt % 2
            xtile = xt[par]
            xkeys = ck(f"xt{par}")
            sq_dma(xtile.rearrange("p (c t) -> p c t", t=TS),
                   src[bi].rearrange("(c p) t -> p c t", p=128)[:, :, tok0:tok0 + TS],
                   [f"d:{src_nm}:{bi}:{nt}"], xkeys)
            if nt == 0:
                b.op("ve", lambda h: h.memset(xnb[:], 0.0), r=[], w=["xnb"])
            else:
                xv = xn.rearrange("p (c t) -> p c t", t=TS)
                b.op("ve", lambda h, xv=xv: h.tensor_copy(out=xnb[:],
                                                          in_=xv[:, :, TS - 1]),
                     r=ck("xn"), w=["xnb"])
            xr_, xk_, xrn, xkn = f_bufs(nt)
            emit_ln(xtile, xkeys, xr_, xrn)
            d_ = pool[0]
            for c in range(C):
                xc = csl(xn, c)
                b.op("ve", lambda h, c=c, xc=xc: h.tensor_tensor(
                    out=d_[:, c * TS + 1:(c + 1) * TS],
                    in0=xc[:, 1:TS], in1=xc[:, 0:TS - 1], op=ALU.subtract),
                    r=[f"xn:{c}"], w=[f"pl0:{c}"])
                b.op("ve", lambda h, c=c, xc=xc: h.tensor_tensor(
                    out=d_[:, c * TS:c * TS + 1],
                    in0=xc[:, 0:1], in1=xnb[:, c:c + 1], op=ALU.subtract),
                    r=[f"xn:{c}", "xnb"], w=[f"pl0:{c}"])
                for mu, dst_, key in ((om_r, xr_, xrn), (om_k, xk_, xkn)):
                    b.op("ve", lambda h, c=c, mu=mu, dst_=dst_: h.tensor_scalar(
                        out=csl(dst_, c), in0=csl(d_, c),
                        scalar1=mu[:, c:c + 1], scalar2=None, op0=ALU.mult),
                        r=[f"pl0:{c}", "k:omr", "k:omk"], w=[f"{key}:{c}"])
                    b.op("ve", lambda h, c=c, dst_=dst_, xc=xc: h.tensor_tensor(
                        out=csl(dst_, c), in0=xc, in1=csl(dst_, c), op=ALU.subtract),
                        r=[f"xn:{c}", f"{key}:{c}"], w=[f"{key}:{c}"])

        def emit_f_B1(bi, nt):
            xr_, xk_, xrn, xkn = f_bufs(nt)
            r_, ka_ = pool[3], pool[4]
            for dc in range(C):
                bank = dc % 4
                for kc in range(C):
                    b.op("te", lambda h, dc=dc, kc=kc, bank=bank: h.matmul(
                        ps[bank][:], wslice(0, kc, dc), csl(xr_, kc),
                        start=(kc == 0), stop=(kc == C - 1)),
                        r=[f"{xrn}:{kc}", "w:0"], w=[f"ps{bank}"])
                b.op("se", lambda h, dc=dc, bank=bank: h.activation(
                    out=csl(r_, dc), in_=ps[bank][:], func=AF.Sigmoid),
                    r=[f"ps{bank}"], w=[f"pl3:{dc}"])
            for dc in range(C):
                bank = dc % 4
                for kc in range(C):
                    b.op("te", lambda h, dc=dc, kc=kc, bank=bank: h.matmul(
                        ps[bank][:], wslice(1, kc, dc), csl(xk_, kc),
                        start=(kc == 0), stop=(kc == C - 1)),
                        r=[f"{xkn}:{kc}", "w:1"], w=[f"ps{bank}"])
                b.op("se", lambda h, dc=dc, bank=bank: h.activation(
                    out=csl(ka_, dc), in_=ps[bank][:], func=AF.Relu),
                    r=[f"ps{bank}"], w=[f"pl4:{dc}"])
                b.op("se", lambda h, dc=dc: h.activation(
                    out=csl(ka_, dc), in_=csl(ka_, dc), func=AF.Square),
                    r=[f"pl4:{dc}"], w=[f"pl4:{dc}"])

        def emit_f_B2(bi, nt, dst, dst_nm):
            tok0 = nt * TS
            par = nt % 2
            xtile = xt[par]
            xkeys = ck(f"xt{par}")
            r_, ka_ = pool[3], pool[4]
            for dc in range(C):
                bank = dc % 4
                for kc in range(C):
                    b.op("te", lambda h, dc=dc, kc=kc, bank=bank: h.matmul(
                        ps[bank][:], wslice(2, kc, dc), csl(ka_, kc),
                        start=(kc == 0), stop=(kc == C - 1)),
                        r=[f"pl4:{kc}", "w:2"], w=[f"ps{bank}"])
                b.op("ve", lambda h, dc=dc, bank=bank: h.tensor_tensor(
                    out=scrB[:], in0=csl(r_, dc), in1=ps[bank][:], op=ALU.mult),
                    r=[f"pl3:{dc}", f"ps{bank}"], w=["scrB"])
                b.op("ve", lambda h, dc=dc: h.tensor_tensor(
                    out=csl(xtile, dc), in0=csl(xtile, dc), in1=scrB[:], op=ALU.add),
                    r=["scrB", f"xt{par}:{dc}"], w=[f"xt{par}:{dc}"])
            gq_dma(dst[bi].rearrange("(c p) t -> p c t", p=128)[:, :, tok0:tok0 + TS],
                   xtile.rearrange("p (c t) -> p c t", t=TS),
                   xkeys, [f"d:{dst_nm}:{bi}:{nt}:{dc}" for dc in range(C)])

        def a_xn(ti):
            return (xn, "xn")

        def emit_a_S1(bi, nt, src, src_nm):
            ti = bi * NT + nt
            tok0 = nt * TS
            par = nt % 2
            xtile = xt[par]
            xkeys = ck(f"xt{par}")
            sq_dma(xtile.rearrange("p (c t) -> p c t", t=TS),
                   src[bi].rearrange("(c p) t -> p c t", p=128)[:, :, tok0:tok0 + TS],
                   [f"d:{src_nm}:{bi}:{nt}:{dc}" for dc in range(C)], xkeys)
            sq_dma(cs_t[0][:], dram["cosPb"][:, tok0:tok0 + TS], [], ["cs0"])
            sq_dma(cs_t[1][:], dram["sinPb"][:, tok0:tok0 + TS], [], ["cs1"])
            xnbuf, xnkey = a_xn(ti)
            emit_ln(xtile, xkeys, pool[6], "pl6", xnbuf, xnkey)

        def emit_a_S2(bi, nt):
            ti = bi * NT + nt
            xnb_, xnk = a_xn(ti)
            qsb_, aq_, pK_, pV_ = pool[0], pool[1], pool[2], pool[3]
            cosS = cs_t[0][:]
            sinS = cs_t[1][:]
            PTS = PT[:]
            gateS = gate[:]
            for c in range(C):
                b.op("te", lambda h, c=c: h.matmul(
                    ps[3][0:LR, :], aq_sb[:, c * LR:(c + 1) * LR], csl(xnb_, c),
                    start=(c == 0), stop=(c == C - 1)),
                    r=[f"{xnk}:{c}", "k:aq"], w=["ps3"])
            b.op("ve", lambda h: h.tensor_copy(out=tq_sb[:], in_=ps[3][0:LR, :]),
                 r=["ps3"], w=["tq"])
            for c in range(C):
                b.op("te", lambda h, c=c: h.matmul(
                    ps[3][0:LR, :], av_sb[:, c * LR:(c + 1) * LR], csl(xnb_, c),
                    start=(c == 0), stop=(c == C - 1)),
                    r=[f"{xnk}:{c}", "k:av"], w=["ps3"])
            b.op("ve", lambda h: h.tensor_copy(out=tvl_sb[:], in_=ps[3][0:LR, :]),
                 r=["ps3"], w=["tvl"])
            for c in range(C):
                bank = c % 2
                for kc in range(C):
                    b.op("te", lambda h, c=c, kc=kc, bank=bank: h.matmul(
                        ps[bank][:], wslice(0, kc, c), csl(xnb_, kc),
                        start=(kc == 0), stop=False),
                        r=[f"{xnk}:{kc}", "w:0"], w=[f"ps{bank}"])
                b.op("te", lambda h, c=c, bank=bank: h.matmul(
                    ps[bank][:], bq_sb[:, c * 128:(c + 1) * 128], tq_sb[:],
                    start=False, stop=True), r=["tq", "k:bq"], w=[f"ps{bank}"])
                b.op("ve", lambda h, c=c, bank=bank: h.tensor_copy(
                    out=csl(qsb_, c), in_=ps[bank][:]), r=[f"ps{bank}"], w=[f"pl0:{c}"])
                b.op("te", lambda h, c=c: h.matmul(
                    ps[2][:], Rm[:], csl(qsb_, c), start=True, stop=True),
                    r=[f"pl0:{c}", "k:R"], w=["ps2"])
                b.op("ve", lambda h, c=c, cosS=cosS: h.tensor_tensor(
                    out=csl(aq_, c), in0=csl(qsb_, c), in1=cosS, op=ALU.mult),
                    r=[f"pl0:{c}", "cs0"], w=[f"pl1:{c}"])
                b.op("ve", lambda h, c=c, sinS=sinS: h.tensor_tensor(
                    out=scr1[:], in0=ps[2][:], in1=sinS, op=ALU.mult),
                    r=["ps2", "cs1"], w=["scr1"])
                b.op("ve", lambda h, c=c: h.tensor_tensor(
                    out=csl(aq_, c), in0=csl(aq_, c), in1=scr1[:], op=ALU.add),
                    r=["scr1", f"pl1:{c}"], w=[f"pl1:{c}"])
                for kc in range(C):
                    b.op("te", lambda h, c=c, kc=kc, bank=bank: h.matmul(
                        ps[bank][:], wslice(1, kc, c), csl(xnb_, kc),
                        start=(kc == 0), stop=(kc == C - 1)),
                        r=[f"{xnk}:{kc}", "w:1"], w=[f"ps{bank}"])
                b.op("ve", lambda h, c=c, bank=bank: h.tensor_copy(
                    out=csl(qsb_, c), in_=ps[bank][:]), r=[f"ps{bank}"], w=[f"pl0:{c}"])
                b.op("te", lambda h, c=c: h.matmul(
                    ps[2][:], Rm[:], csl(qsb_, c), start=True, stop=True),
                    r=[f"pl0:{c}", "k:R"], w=["ps2"])
                b.op("ve", lambda h, c=c, cosS=cosS: h.tensor_tensor(
                    out=csl(pK_, c), in0=csl(qsb_, c), in1=cosS, op=ALU.mult),
                    r=[f"pl0:{c}", "cs0"], w=[f"pl2:{c}"])
                b.op("ve", lambda h, c=c, sinS=sinS: h.tensor_tensor(
                    out=scr1[:], in0=ps[2][:], in1=sinS, op=ALU.mult),
                    r=["ps2", "cs1"], w=["scr1"])
                b.op("ve", lambda h, c=c: h.tensor_tensor(
                    out=csl(pK_, c), in0=csl(pK_, c), in1=scr1[:], op=ALU.add),
                    r=["scr1", f"pl2:{c}"], w=[f"pl2:{c}"])
                b.op("ve", lambda h, c=c, gateS=gateS: h.tensor_tensor_scan(
                    out=csl(pK_, c), data0=gateS, data1=csl(pK_, c), initial=0.0,
                    op0=ALU.mult, op1=ALU.add),
                    r=[f"pl2:{c}", "k:gate"], w=[f"pl2:{c}"])
                for kc in range(C):
                    b.op("te", lambda h, c=c, kc=kc, bank=bank: h.matmul(
                        ps[bank][:], wslice(2, kc, c), csl(xnb_, kc),
                        start=(kc == 0), stop=False),
                        r=[f"{xnk}:{kc}", "w:2"], w=[f"ps{bank}"])
                b.op("te", lambda h, c=c, bank=bank: h.matmul(
                    ps[bank][:], bv_sb[:, c * 128:(c + 1) * 128], tvl_sb[:],
                    start=False, stop=True), r=["tvl", "k:bv"], w=[f"ps{bank}"])
                b.op("ve", lambda h, c=c, bank=bank, PTS=PTS: h.tensor_tensor(
                    out=csl(pV_, c), in0=ps[bank][:], in1=PTS, op=ALU.mult),
                    r=[f"ps{bank}", "k:PT"], w=[f"pl3:{c}"])
                b.op("ve", lambda h, c=c, gateS=gateS: h.tensor_tensor_scan(
                    out=csl(pV_, c), data0=gateS, data1=csl(pV_, c), initial=0.0,
                    op0=ALU.mult, op1=ALU.add),
                    r=[f"pl3:{c}", "k:gate"], w=[f"pl3:{c}"])
                pKv = csl(pK_, c).rearrange("p (s k) -> p s k", k=KF)
                b.op("gq", lambda h, c=c, pKv=pKv: h.tensor_copy(
                    out=fullK[:, c * TS + nt * 128: c * TS + nt * 128 + 128],
                    in_=pKv[:, :, KF - 1]), r=[f"pl2:{c}"], w=[f"fK:{c}:{nt}"])
                pVv = csl(pV_, c).rearrange("p (s k) -> p s k", k=KF)
                b.op("gq", lambda h, c=c, pVv=pVv: h.tensor_copy(
                    out=scr1[:, 0:128], in_=pVv[:, :, KF - 1]),
                    r=[f"pl3:{c}"], w=["scr1"])
                b.op("te", lambda h, c=c: h.transpose(pst[:], scr1[:, 0:128], ident[:]),
                     r=["scr1", "k:ident"], w=["pst"])
                for j in range(2):
                    hh = 2 * c + j
                    b.op("ve", lambda h, hh=hh, j=j: h.tensor_copy(
                        out=fVsm[:, (nt * NH + hh) * HD: (nt * NH + hh) * HD + HD],
                        in_=pst[:, 64 * j: 64 * j + HD]),
                        r=["pst"], w=[f"fV:{hh}:{nt}"])

        def emit_a_S3(bi, nt):
            qsb_, aq_, pK_, pV_ = pool[0], pool[1], pool[2], pool[3]
            ao_ = pool[6]
            PTS = PT[:]
            for c in range(C):
                b.op("gq", lambda h, c=c: h.tensor_tensor(
                    out=scr1[:], in0=csl(aq_, c), in1=csl(pK_, c), op=ALU.mult),
                    r=[f"pl1:{c}", f"pl2:{c}"], w=["scr1"])
                for j in range(2):
                    b.op("te", lambda h, j=j: h.matmul(
                        ps[3][32 * j:32 * j + 32, :], ind33[j][:, 0:32], scr1[:],
                        start=True, stop=True, skip_group_check=True),
                        r=["scr1", "k:ind33a", "k:ind33b"], w=["ps3"])
                b.op("se", lambda h: h.activation(
                    out=ec_sb[0:64, :], in_=ps[3][0:64, :], func=AF.Exp,
                    scale=1.0 / 16.0), r=["ps3"], w=["ec"])
                pvbank = c % 2
                for j in range(2):
                    hh = 2 * c + j
                    ebuf = pool[4]
                    hp = j
                    for sc in range(nt + 1):
                        sbank = 2 + (sc % 2)
                        b.op("te", lambda h, c=c, j=j, sc=sc, sbank=sbank: h.matmul(
                            ps[sbank][:],
                            fullK[64 * j:64 * j + 64,
                                  c * TS + sc * 128: c * TS + sc * 128 + 128],
                            aq_[64 * j:64 * j + 64, c * TS:(c + 1) * TS],
                            start=True, stop=True),
                            r=[f"fK:{c}:{sc}", f"pl1:{c}"], w=[f"ps{sbank}"])
                        b.op("se", lambda h, hp=hp, sc=sc, ebuf=ebuf, sbank=sbank:
                             h.activation(
                                 out=ebuf[:, (4 * hp + sc) * TS:
                                          (4 * hp + sc + 1) * TS],
                                 in_=ps[sbank][:], func=AF.Exp, scale=1.0 / 16.0),
                             r=[f"ps{sbank}"], w=[f"pl4:{4 * hp + sc}"])
                    b.op("ve", lambda h, hp=hp, ebuf=ebuf: h.tensor_tensor(
                        out=ebuf[:, (4 * hp + nt) * TS:(4 * hp + nt + 1) * TS],
                        in0=ebuf[:, (4 * hp + nt) * TS:(4 * hp + nt + 1) * TS],
                        in1=M0[:], op=ALU.mult),
                        r=[f"pl4:{4 * hp + nt}", "k:M0"], w=[f"pl4:{4 * hp + nt}"])
                    for sc in range(nt + 1):
                        b.op("te", lambda h, j=j, hp=hp, sc=sc, ebuf=ebuf: h.matmul(
                            ps[6][32 * j:32 * j + 32, :], onesden[:, 0:32],
                            ebuf[:, (4 * hp + sc) * TS:(4 * hp + sc + 1) * TS],
                            start=(sc == 0), stop=(sc == nt),
                            skip_group_check=True),
                            r=[f"pl4:{4 * hp + sc}", "k:onesden"], w=["ps6"])
                    for sc in range(nt + 1):
                        b.op("te", lambda h, hh=hh, j=j, hp=hp, sc=sc, pvbank=pvbank,
                             ebuf=ebuf: h.matmul(
                                 ps[pvbank][64 * j:64 * j + 64, :],
                                 fVsm[:, (sc * NH + hh) * HD:
                                      (sc * NH + hh) * HD + HD],
                                 ebuf[:, (4 * hp + sc) * TS:(4 * hp + sc + 1) * TS],
                                 start=(sc == 0), stop=(sc == nt),
                                 tile_position=(0, 64 * j),
                                 skip_group_check=True),
                             r=[f"fV:{hh}:{sc}", f"pl4:{4 * hp + sc}"],
                             w=[f"ps{pvbank}"])
                b.op("ve", lambda h: h.tensor_tensor(
                    out=scrA[0:64, :], in0=ps[6][0:64, :], in1=ec_sb[0:64, :],
                    op=ALU.add), r=["ps6", "ec"], w=["scrA"])
                b.op("se", lambda h: h.activation(
                    out=scrA[0:64, :], in_=scrA[0:64, :], func=AF.Ln),
                    r=["scrA"], w=["scrA"])
                b.op("se", lambda h: h.activation(
                    out=rden_sb[0:64, :], in_=scrA[0:64, :], func=AF.Exp,
                    scale=-1.0), r=["scrA"], w=["rden"])
                ebank = 2 + (c % 2)
                b.op("te", lambda h, ebank=ebank: h.matmul(
                    ps[ebank][:], bcA[:], ec_sb[:], start=True, stop=True),
                    r=["ec", "k:bcA"], w=[f"ps{ebank}"])
                b.op("ve", lambda h, c=c, ebank=ebank: h.tensor_tensor(
                    out=scrA[:], in0=ps[ebank][:], in1=csl(pV_, c), op=ALU.mult),
                    r=[f"ps{ebank}", f"pl3:{c}"], w=["scrA"])
                b.op("ve", lambda h, pvbank=pvbank: h.tensor_tensor(
                    out=scrB[:], in0=ps[pvbank][:], in1=scrA[:], op=ALU.add),
                    r=[f"ps{pvbank}", "scrA"], w=["scrB"])
                b.op("te", lambda h, ebank=ebank: h.matmul(
                    ps[ebank][:], bcA[:], rden_sb[:], start=True, stop=True),
                    r=["rden", "k:bcA"], w=[f"ps{ebank}"])
                b.op("ve", lambda h, ebank=ebank: h.tensor_tensor(
                    out=scrB[:], in0=scrB[:], in1=ps[ebank][:], op=ALU.mult),
                    r=["scrB", f"ps{ebank}"], w=["scrB"])
                b.op("ve", lambda h, c=c, PTS=PTS: h.tensor_tensor(
                    out=csl(ao_, c), in0=scrB[:], in1=PTS, op=ALU.mult),
                    r=["scrB", "k:PT"], w=[f"pl6:{c}"])

        def emit_a_S4(bi, nt, dst, dst_nm):
            tok0 = nt * TS
            par = nt % 2
            xtile = xt[par]
            xkeys = ck(f"xt{par}")
            ao_ = pool[6]
            for c in range(C):
                b.op("te", lambda h, c=c: h.matmul(
                    ps[3][0:LR, :], ao_sb[:, c * LR:(c + 1) * LR], csl(ao_, c),
                    start=(c == 0), stop=(c == C - 1)),
                    r=[f"pl6:{c}", "k:ao"], w=["ps3"])
            b.op("ve", lambda h: h.tensor_copy(out=tq_sb[:], in_=ps[3][0:LR, :]),
                 r=["ps3"], w=["tq"])
            for dc in range(C):
                bank = dc % 2
                for kc in range(C):
                    b.op("te", lambda h, dc=dc, kc=kc, bank=bank: h.matmul(
                        ps[bank][:], wslice(3, kc, dc), csl(ao_, kc),
                        start=(kc == 0), stop=False),
                        r=[f"pl6:{kc}", "w:3"], w=[f"ps{bank}"])
                b.op("te", lambda h, dc=dc, bank=bank: h.matmul(
                    ps[bank][:], bo_sb[:, dc * 128:(dc + 1) * 128], tq_sb[:],
                    start=False, stop=True), r=["tq", "k:bo"], w=[f"ps{bank}"])
                b.op("ve", lambda h, dc=dc, bank=bank: h.tensor_tensor(
                    out=csl(xtile, dc), in0=csl(xtile, dc), in1=ps[bank][:],
                    op=ALU.add),
                    r=[f"ps{bank}", f"xt{par}:{dc}"], w=[f"xt{par}:{dc}"])
            gq_dma(dst[bi].rearrange("(c p) t -> p c t", p=128)[:, :, tok0:tok0 + TS],
                   xtile.rearrange("p (c t) -> p c t", t=TS),
                   xkeys, [f"d:{dst_nm}:{bi}:{nt}:{dc}" for dc in range(C)])

        tiles = [(bi, nt) for bi in range(NB) for nt in range(NT)]

        def run_f_phase(src, src_nm, dst, dst_nm):
            n = len(tiles)
            emit_f_A(*tiles[0], src, src_nm)
            emit_f_A(*tiles[1], src, src_nm)
            emit_f_B1(*tiles[0])
            for i in range(1, n):
                emit_f_B2(*tiles[i - 1], dst, dst_nm)
                emit_f_B1(*tiles[i])
                if i + 1 < n:
                    emit_f_A(*tiles[i + 1], src, src_nm)
            emit_f_B2(*tiles[n - 1], dst, dst_nm)

        def run_a_phase(src, src_nm, dst, dst_nm):
            n = len(tiles)
            emit_a_S1(*tiles[0], src, src_nm)
            for i in range(n):
                emit_a_S2(*tiles[i])
                if i + 1 < n:
                    emit_a_S1(*tiles[i + 1], src, src_nm)
                emit_a_S3(*tiles[i])
                emit_a_S4(*tiles[i], dst, dst_nm)

        load_weights_f()
        if STAGE == 1:
            run_f_phase(dram["x"], "x", dram["out"], "out")
        else:
            run_f_phase(dram["x"], "x", dram["x1"], "x1")
            load_weights_a()
            if STAGE == 2:
                run_a_phase(dram["x1"], "x1", dram["out"], "out")
            else:
                run_a_phase(dram["x1"], "x1", dram["x2"], "x2")
                load_weights_f()
                run_f_phase(dram["x2"], "x2", dram["out"], "out")

        b.finalize(block, sems)
    return nc


def make_consts():
    H = np.array([[1, 1, 1, 1], [1, -1, 1, -1], [1, 1, -1, -1], [1, -1, -1, 1]],
                 np.float32)
    i_idx = np.arange(S)
    c_idx = np.arange(128)
    PTm = H[(i_idx % 4)[None, :], (c_idx % 4)[:, None]].astype(np.float32)
    inv_freq = 1.0 / 10000.0 ** (np.arange(0, HD, 2, dtype=np.float64) / HD)
    theta = i_idx[None, :] * inv_freq[(c_idx % HD) % 32][:, None]
    cosP = (np.cos(theta) * PTm).astype(np.float32)
    sinP = (np.sin(theta) * PTm).astype(np.float32)
    gate = np.broadcast_to(
        np.where(i_idx % 4 == 0, 0.0, 1.0).astype(np.float32), (128, S)).copy()
    M0 = (np.arange(128)[:, None] < (np.arange(TS) // KF)[None, :]).astype(np.float32)
    R = np.zeros((128, 128), np.float32)
    for m in range(128):
        mm = m % HD
        base = HD * (m // HD)
        if mm < 32:
            R[base + mm + 32, m] = -1.0
        else:
            R[base + mm - 32, m] = 1.0
    onesden = np.zeros((128, 33), np.float32)
    onesden[:, 0] = 1.0
    ind33a = np.zeros((128, 33), np.float32)
    ind33a[0:64, 0] = 1.0
    ind33b = np.zeros((128, 33), np.float32)
    ind33b[64:128, 0] = 1.0
    bcA = np.zeros((128, 128), np.float32)
    bcA[0, 0:64] = 1.0
    bcA[32, 64:128] = 1.0
    bcB = np.zeros((128, 128), np.float32)
    bcB[64, 0:64] = 1.0
    bcB[96, 64:128] = 1.0
    return dict(cosP=cosP, sinP=sinP, PT=np.ascontiguousarray(PTm[:, :TS]),
                gate=np.ascontiguousarray(gate[:, :TS]), M0=M0, R=R,
                ident=np.eye(128, dtype=np.float32),
                ones1=np.ones((1, 128), np.float32),
                onesln=np.full((128, 1), 1.0 / D, np.float32),
                onesden=onesden, ind33a=ind33a, ind33b=ind33b, bcA=bcA, bcB=bcB,
                epsv=np.full((128, 1), 1e-5, np.float32))


def kernel(**inp):
    x = np.asarray(inp["L0_attn_hidden_states"], np.float32)
    xT = np.ascontiguousarray(x.transpose(0, 2, 1))
    shared = dict(
        wq=np.ascontiguousarray(inp["L0_attn_Wq"], np.float32),
        wk=np.ascontiguousarray(inp["L0_attn_Wk"], np.float32),
        wv=np.ascontiguousarray(inp["L0_attn_Wv"], np.float32),
        wo=np.ascontiguousarray(inp["L0_attn_Wo"], np.float32),
        wrT=np.ascontiguousarray(np.asarray(inp["L0_ffn_W_r"], np.float32).T),
        wkT=np.ascontiguousarray(np.asarray(inp["L0_ffn_W_k"], np.float32).T),
        wvT=np.ascontiguousarray(np.asarray(inp["L0_ffn_W_v"], np.float32).T),
        aq=np.ascontiguousarray(inp["L0_attn_Aq"], np.float32),
        av=np.ascontiguousarray(inp["L0_attn_Av"], np.float32),
        ao=np.ascontiguousarray(inp["L0_attn_Ao"], np.float32),
        bq=np.ascontiguousarray(inp["L0_attn_Bq"], np.float32),
        bv=np.ascontiguousarray(inp["L0_attn_Bv"], np.float32),
        bo=np.ascontiguousarray(inp["L0_attn_Bo"], np.float32),
        mur=np.ascontiguousarray(
            np.asarray(inp["L0_ffn_mu_r"], np.float32).reshape(C, 128).T),
        muk=np.ascontiguousarray(
            np.asarray(inp["L0_ffn_mu_k"], np.float32).reshape(C, 128).T),
    )
    shared.update(make_consts())
    import ml_dtypes
    shared["cosPb"] = shared["cosP"].astype(ml_dtypes.bfloat16)
    shared["sinPb"] = shared["sinP"].astype(ml_dtypes.bfloat16)
    nc = build_nc()
    in_maps = []
    for core in range(NCORES):
        m = dict(shared)
        m["x"] = np.ascontiguousarray(xT[2 * core: 2 * core + 2])
        in_maps.append(m)
    res = run_bass_kernel_spmd(nc, in_maps, core_ids=list(range(NCORES)))
    y = np.concatenate([r["out"] for r in res.results], axis=0)
    return np.ascontiguousarray(y.transpose(0, 2, 1))


# version 08
# speedup vs baseline: 1.1283x; 1.0492x over previous
import contextlib
import numpy as np
import concourse.bass as bass
import concourse.mybir as mybir
from concourse.bass_utils import run_bass_kernel_spmd
from concourse.alu_op_type import AluOpType as ALU

F32 = mybir.dt.float32
BF16 = mybir.dt.bfloat16
AF = mybir.ActivationFunctionType

NCORES = 8
NB = 2
S = 2048
D = 1024
C = 8
TS = 512
NT = S // TS
NH = 16
HD = 64
KF = 4
LR = 64

STAGE = 3

ENGINES = ("te", "ve", "se", "sq", "gq")
DMA_ENGINES = ("sq", "gq")


class Builder:
    def __init__(self):
        self.ops = {e: [] for e in ENGINES}
        self.lw = {}
        self.rd = {}

    def op(self, eng, fn, r=(), w=(), dma=False):
        idx = len(self.ops[eng])
        deps = {}

        def add(e, i):
            if e == eng:
                return
            if deps.get(e, -1) < i:
                deps[e] = i

        for reg in r:
            t = self.lw.get(reg)
            if t:
                add(*t)
        for reg in w:
            t = self.lw.get(reg)
            if t:
                add(*t)
            for t2 in self.rd.get(reg, ()):
                add(*t2)
        self.ops[eng].append([fn, deps, dma])
        for reg in r:
            self.rd.setdefault(reg, []).append((eng, idx))
        for reg in w:
            self.lw[reg] = (eng, idx)
            self.rd[reg] = []

    def finalize(self, block, sems):
        marked = {e: set() for e in ENGINES}
        for e in ENGINES:
            for fn, deps, dma in self.ops[e]:
                for de, di in deps.items():
                    if not self.ops[de][di][2]:
                        marked[de].add(di)
        cnt = {}
        for e in ENGINES:
            c = 0
            pref = []
            for i, (fn, deps, dma) in enumerate(self.ops[e]):
                if dma:
                    c += 16
                elif i in marked[e]:
                    c += 1
                pref.append(c)
            cnt[e] = pref

        def mk(e, handle):
            wm = {}
            for i, (fn, deps, dma) in enumerate(self.ops[e]):
                for de in sorted(deps):
                    di = deps[de]
                    thr = cnt[de][di]
                    if wm.get(de, 0) >= thr:
                        continue
                    handle.wait_ge(sems[de], thr)
                    wm[de] = thr
                inst = fn(handle)
                if dma:
                    inst.then_inc(sems[e], 16)
                elif i in marked[e]:
                    inst.then_inc(sems[e], 1)
            if e == "gq" and cnt["gq"]:
                handle.wait_ge(sems["gq"], cnt["gq"][-1])

        @block.tensor
        def _(h):
            mk("te", h)

        @block.vector
        def _(h):
            mk("ve", h)

        @block.scalar
        def _(h):
            mk("se", h)

        @block.sync
        def _(h):
            mk("sq", h)

        @block.gpsimd
        def _(h):
            mk("gq", h)


def ck(name, lst=None):
    return [f"{name}:{c}" for c in (range(C) if lst is None else lst)]


def build_nc():
    nc = bass.Bass("TRN2", target_bir_lowering=False)
    dram = {}
    dram["x"] = nc.dram_tensor("x", [NB, D, S], F32, kind="ExternalInput").ap()
    dram["x1"] = nc.dram_tensor("x1", [NB, D, S], F32, kind="Internal").ap()
    dram["x2"] = nc.dram_tensor("x2", [NB, D, S], F32, kind="Internal").ap()
    dram["out"] = nc.dram_tensor("out", [NB, D, S], F32, kind="ExternalOutput").ap()
    for nm in ["wq", "wk", "wv", "wo", "wrT", "wkT", "wvT"]:
        dram[nm] = nc.dram_tensor(nm, [D, D], F32, kind="ExternalInput").ap()
    for nm in ["aq", "av", "ao"]:
        dram[nm] = nc.dram_tensor(nm, [D, LR], F32, kind="ExternalInput").ap()
    for nm in ["bq", "bv", "bo"]:
        dram[nm] = nc.dram_tensor(nm, [LR, D], F32, kind="ExternalInput").ap()
    dram["mur"] = nc.dram_tensor("mur", [128, C], F32, kind="ExternalInput").ap()
    dram["muk"] = nc.dram_tensor("muk", [128, C], F32, kind="ExternalInput").ap()
    dram["epsv"] = nc.dram_tensor("epsv", [128, 1], F32, kind="ExternalInput").ap()
    dram["cosPb"] = nc.dram_tensor("cosPb", [128, S], BF16, kind="ExternalInput").ap()
    dram["sinPb"] = nc.dram_tensor("sinPb", [128, S], BF16, kind="ExternalInput").ap()
    for nm, shp in [("cosP", [128, S]), ("sinP", [128, S]), ("PT", [128, TS]),
                    ("gate", [128, TS]), ("M0", [128, TS]), ("R", [128, 128]),
                    ("ident", [128, 128]), ("ones1", [1, 128]), ("onesln", [128, 1]),
                    ("onesden", [128, 33]), ("ind33a", [128, 33]), ("ind33b", [128, 33]),
                    ("bcA", [128, 128]), ("bcB", [128, 128])]:
        dram[nm] = nc.dram_tensor(nm, shp, F32, kind="ExternalInput").ap()

    es = contextlib.ExitStack()
    with es:
        def sbt(name, shape, dt=BF16):
            return es.enter_context(nc.sbuf_tensor(name, shape, dt)).ap()

        wsl = [sbt(f"wsl{i}", [128, C * D]) for i in range(4)]
        aq_sb = sbt("s_aq", [128, C * LR])
        av_sb = sbt("s_av", [128, C * LR])
        ao_sb = sbt("s_ao", [128, C * LR])
        bq_sb = sbt("s_bq", [LR, D])
        bv_sb = sbt("s_bv", [LR, D])
        bo_sb = sbt("s_bo", [LR, D])
        mur_sb = sbt("s_mur", [128, C], F32)
        muk_sb = sbt("s_muk", [128, C], F32)
        om_r = sbt("s_omr", [128, C], F32)
        om_k = sbt("s_omk", [128, C], F32)
        eps_sb = sbt("s_eps", [128, 1], F32)
        PT = sbt("s_PT", [128, TS])
        gate = sbt("s_gate", [128, TS])
        M0 = sbt("s_M0", [128, TS])
        Rm = sbt("s_Rm", [128, 128])
        ident = sbt("s_ident", [128, 128])
        ones1 = sbt("s_ones1", [1, 128])
        onesln = sbt("s_onesln", [128, 1])
        onesden = sbt("s_onesden", [128, 33])
        ind33 = [sbt("s_i33a", [128, 33]), sbt("s_i33b", [128, 33])]
        bcA = sbt("s_bcA", [128, 128])
        cs_t = [sbt("s_cos", [128, TS]), sbt("s_sin", [128, TS])]
        xt = [sbt(f"xt{i}", [128, C * TS], F32) for i in range(2)]
        pool = [sbt(f"pl{i}", [128, C * TS]) for i in range(7)]
        xn = sbt("s_xn", [128, C * TS])
        xnb = sbt("s_xnb", [128, C])
        fullK = sbt("s_fullK", [128, C * TS])
        fVsm = sbt("s_fVsm", [128, KF * NH * HD])
        scr1 = sbt("s_scr1", [128, TS])
        scrA = sbt("s_scrA", [128, TS], F32)
        scrB = sbt("s_scrB", [128, TS], F32)
        ec_sb = sbt("s_ec", [128, TS])
        rden_sb = sbt("s_rden", [128, TS])
        rowf = sbt("s_rowf", [1, 2 * TS], F32)
        istd_r = sbt("s_istd", [1, TS])
        mistd_r = sbt("s_mistd", [1, TS])
        tq_sb = sbt("s_tq", [LR, TS])
        tvl_sb = sbt("s_tvl", [LR, TS])

        ps = [es.enter_context(nc.psum_tensor(f"ps{i}", [128, TS], F32)).ap()
              for i in range(7)]
        pst = es.enter_context(nc.psum_tensor("pst", [128, 128], BF16)).ap()

        sems = {e: es.enter_context(nc.semaphore(f"sem_{e}")) for e in ENGINES}
        block = es.enter_context(nc.Block())

        b = Builder()

        def gq_dma(dst, src, rkeys, wkeys):
            b.op("gq", lambda h, d=dst, s=src: h.dma_start(out=d, in_=s),
                 r=rkeys, w=wkeys, dma=True)

        def sq_dma(dst, src, rkeys, wkeys):
            b.op("sq", lambda h, d=dst, s=src: h.dma_start(out=d, in_=s),
                 r=rkeys, w=wkeys, dma=True)

        for nm, t in [("PT", PT), ("gate", gate), ("M0", M0), ("R", Rm),
                      ("ident", ident), ("ones1", ones1), ("onesln", onesln),
                      ("onesden", onesden), ("ind33a", ind33[0]),
                      ("ind33b", ind33[1]), ("bcA", bcA)]:
            gq_dma(t[:], dram[nm][:], [], [f"k:{nm}"])
        for nm, t in [("aq", aq_sb), ("av", av_sb), ("ao", ao_sb)]:
            gq_dma(t.rearrange("p (c r) -> p c r", r=LR),
                   dram[nm].rearrange("(c p) r -> p c r", p=128), [], [f"k:{nm}"])
        for nm, t in [("bq", bq_sb), ("bv", bv_sb), ("bo", bo_sb)]:
            gq_dma(t[:], dram[nm][:], [], [f"k:{nm}"])
        for nm, t in [("mur", mur_sb), ("muk", muk_sb), ("epsv", eps_sb)]:
            sq_dma(t[:], dram[nm][:], [], [f"k:{nm}"])
        b.op("ve", lambda h: h.tensor_scalar(out=om_r[:], in0=mur_sb[:], scalar1=-1.0,
                                             scalar2=1.0, op0=ALU.mult, op1=ALU.add),
             r=["k:mur"], w=["k:omr"])
        b.op("ve", lambda h: h.tensor_scalar(out=om_k[:], in0=muk_sb[:], scalar1=-1.0,
                                             scalar2=1.0, op0=ALU.mult, op1=ALU.add),
             r=["k:muk"], w=["k:omk"])
        b.op("ve", lambda h: h.memset(ec_sb[:], 0.0), w=["ec"])
        b.op("ve", lambda h: h.memset(rden_sb[:], 0.0), w=["rden"])

        def load_weights_f():
            for i, nm in enumerate(["wrT", "wkT", "wvT"]):
                gq_dma(wsl[i].rearrange("p (c d) -> p c d", d=D),
                       dram[nm].rearrange("(c p) d -> p c d", p=128),
                       [], [f"w:{i}"])
            b.op("ve", lambda h: h.tensor_scalar(
                out=wsl[2][:], in0=wsl[2][:], scalar1=0.5, scalar2=None,
                op0=ALU.mult), r=["w:2"], w=["w:2"])

        def load_weights_a():
            for i, nm in enumerate(["wq", "wk", "wv", "wo"]):
                gq_dma(wsl[i].rearrange("p (c d) -> p c d", d=D),
                       dram[nm].rearrange("(c p) d -> p c d", p=128),
                       [], [f"w:{i}"])

        def wslice(slot, kc, dc):
            return wsl[slot][:, kc * D + dc * 128: kc * D + dc * 128 + 128]

        def csl(buf, c):
            return buf[:, c * TS:(c + 1) * TS]

        def esl(hp, sc):
            return csl(pool[4], 4 * hp + sc)

        def ekey(hp, sc):
            return f"pl4:{4 * hp + sc}"

        def emit_ln_a(xtile, xkeys, sq_, sqn):
            xb_ = pool[0]
            b.op("se", lambda h: h.copy(out=xb_[:], in_=xtile[:]),
                 r=xkeys, w=ck("pl0"))
            b.op("se", lambda h: h.activation(out=sq_[:], in_=xb_[:], func=AF.Square),
                 r=ck("pl0"), w=ck(sqn))
            for c in range(C):
                b.op("te", lambda h, c=c: h.matmul(
                    ps[6][0:1, :], onesln[:], csl(xb_, c),
                    start=(c == 0), stop=(c == C - 1)),
                    r=[f"pl0:{c}", "k:onesln"], w=["ps6"])
            for c in range(C):
                b.op("te", lambda h, c=c: h.matmul(
                    ps[5][0:1, :], onesln[:], csl(sq_, c),
                    start=(c == 0), stop=(c == C - 1)),
                    r=[f"{sqn}:{c}", "k:onesln"], w=["ps5"])
            b.op("se", lambda h: h.activation(
                out=rowf[0:1, 0:TS], in_=ps[6][0:1, :], func=AF.Square),
                r=["ps6"], w=["rowq"])
            b.op("ve", lambda h: h.tensor_tensor(
                out=rowf[0:1, TS:2 * TS], in0=ps[5][0:1, :], in1=rowf[0:1, 0:TS],
                op=ALU.subtract), r=["ps5", "rowq"], w=["rowv"])
            b.op("se", lambda h: h.activation(
                out=rowf[0:1, TS:2 * TS], in_=rowf[0:1, TS:2 * TS], func=AF.Ln,
                bias=eps_sb[0:1, :]), r=["rowv", "k:epsv"], w=["rowv"])
            b.op("se", lambda h: h.activation(
                out=istd_r[0:1, :], in_=rowf[0:1, TS:2 * TS], func=AF.Exp,
                scale=-0.5), r=["rowv"], w=["istd"])
            b.op("ve", lambda h: h.tensor_tensor(
                out=mistd_r[0:1, :], in0=ps[6][0:1, :], in1=istd_r[0:1, :],
                op=ALU.mult), r=["ps6", "istd"], w=["mistd"])

        def emit_ln_b(xnb_, xnk):
            xb_ = pool[0]
            b.op("te", lambda h: h.matmul(ps[4][:], ones1[:], istd_r[0:1, :],
                                          start=True, stop=True),
                 r=["istd", "k:ones1"], w=["ps4"])
            b.op("te", lambda h: h.matmul(ps[5][:], ones1[:], mistd_r[0:1, :],
                                          start=True, stop=True),
                 r=["mistd", "k:ones1"], w=["ps5"])
            for c in range(C):
                b.op("ve", lambda h, c=c: h.tensor_tensor(
                    out=scrA[:], in0=csl(xb_, c), in1=ps[4][:], op=ALU.mult),
                    r=[f"pl0:{c}", "ps4"], w=["scrA"])
                b.op("ve", lambda h, c=c, xnb_=xnb_: h.tensor_tensor(
                    out=csl(xnb_, c), in0=scrA[:], in1=ps[5][:], op=ALU.subtract),
                    r=["scrA", "ps5"], w=[f"{xnk}:{c}"])

        def f_bufs(nt):
            if nt % 2 == 0:
                return pool[1], pool[2], "pl1", "pl2"
            return pool[5], pool[6], "pl5", "pl6"

        def emit_f_A(bi, nt, src, src_nm):
            tok0 = nt * TS
            par = nt % 2
            xtile = xt[par]
            xkeys = ck(f"xt{par}")
            for dc in range(C):
                sq_dma(csl(xtile, dc),
                       src[bi].rearrange("(c p) t -> p c t", p=128)
                       [:, dc, tok0:tok0 + TS],
                       [f"d:{src_nm}:{bi}:{nt}:{dc}"], [f"xt{par}:{dc}"])
            if nt == 0:
                b.op("ve", lambda h: h.memset(xnb[:], 0.0), r=[], w=["xnb"])
            else:
                xv = xn.rearrange("p (c t) -> p c t", t=TS)
                b.op("ve", lambda h, xv=xv: h.tensor_copy(out=xnb[:],
                                                          in_=xv[:, :, TS - 1]),
                     r=ck("xn"), w=["xnb"])
            xr_, xk_, xrn, xkn = f_bufs(nt)
            emit_ln_a(xtile, xkeys, xr_, xrn)

        def emit_f_A2(bi, nt):
            xr_, xk_, xrn, xkn = f_bufs(nt)
            emit_ln_b(xn, "xn")
            d_ = pool[0]
            for c in range(C):
                xc = csl(xn, c)
                b.op("ve", lambda h, c=c, xc=xc: h.tensor_tensor(
                    out=d_[:, c * TS + 1:(c + 1) * TS],
                    in0=xc[:, 1:TS], in1=xc[:, 0:TS - 1], op=ALU.subtract),
                    r=[f"xn:{c}"], w=[f"pl0:{c}"])
                b.op("ve", lambda h, c=c, xc=xc: h.tensor_tensor(
                    out=d_[:, c * TS:c * TS + 1],
                    in0=xc[:, 0:1], in1=xnb[:, c:c + 1], op=ALU.subtract),
                    r=[f"xn:{c}", "xnb"], w=[f"pl0:{c}"])
                for mu, dst_, key in ((om_r, xr_, xrn), (om_k, xk_, xkn)):
                    b.op("ve", lambda h, c=c, mu=mu, dst_=dst_: h.tensor_scalar(
                        out=csl(dst_, c), in0=csl(d_, c),
                        scalar1=mu[:, c:c + 1], scalar2=None, op0=ALU.mult),
                        r=[f"pl0:{c}", "k:omr", "k:omk"], w=[f"{key}:{c}"])
                    b.op("ve", lambda h, c=c, dst_=dst_, xc=xc: h.tensor_tensor(
                        out=csl(dst_, c), in0=xc, in1=csl(dst_, c), op=ALU.subtract),
                        r=[f"xn:{c}", f"{key}:{c}"], w=[f"{key}:{c}"])

        def emit_f_B1(bi, nt):
            xr_, xk_, xrn, xkn = f_bufs(nt)
            r_, ka_ = pool[3], pool[4]
            for dc in range(C):
                bank = dc % 4
                for kc in range(C):
                    b.op("te", lambda h, dc=dc, kc=kc, bank=bank: h.matmul(
                        ps[bank][:], wslice(0, kc, dc), csl(xr_, kc),
                        start=(kc == 0), stop=(kc == C - 1)),
                        r=[f"{xrn}:{kc}", "w:0"], w=[f"ps{bank}"])
                b.op("se", lambda h, dc=dc, bank=bank: h.activation(
                    out=csl(r_, dc), in_=ps[bank][:], func=AF.Sigmoid),
                    r=[f"ps{bank}"], w=[f"pl3:{dc}"])
            for dc in range(C):
                bank = dc % 4
                for kc in range(C):
                    b.op("te", lambda h, dc=dc, kc=kc, bank=bank: h.matmul(
                        ps[bank][:], wslice(1, kc, dc), csl(xk_, kc),
                        start=(kc == 0), stop=(kc == C - 1)),
                        r=[f"{xkn}:{kc}", "w:1"], w=[f"ps{bank}"])
                b.op("se", lambda h, dc=dc, bank=bank: h.activation(
                    out=csl(ka_, dc), in_=ps[bank][:], func=AF.Relu),
                    r=[f"ps{bank}"], w=[f"pl4:{dc}"])
                b.op("se", lambda h, dc=dc: h.activation(
                    out=csl(ka_, dc), in_=csl(ka_, dc), func=AF.Square),
                    r=[f"pl4:{dc}"], w=[f"pl4:{dc}"])

        def emit_f_B2(bi, nt, dst, dst_nm):
            tok0 = nt * TS
            par = nt % 2
            xtile = xt[par]
            xkeys = ck(f"xt{par}")
            r_, ka_ = pool[3], pool[4]
            for dc in range(C):
                bank = dc % 4
                for kc in range(C):
                    b.op("te", lambda h, dc=dc, kc=kc, bank=bank: h.matmul(
                        ps[bank][:], wslice(2, kc, dc), csl(ka_, kc),
                        start=(kc == 0), stop=(kc == C - 1)),
                        r=[f"pl4:{kc}", "w:2"], w=[f"ps{bank}"])
                b.op("ve", lambda h, dc=dc, bank=bank: h.tensor_tensor(
                    out=scrB[:], in0=csl(r_, dc), in1=ps[bank][:], op=ALU.mult),
                    r=[f"pl3:{dc}", f"ps{bank}"], w=["scrB"])
                b.op("ve", lambda h, dc=dc: h.tensor_tensor(
                    out=csl(xtile, dc), in0=csl(xtile, dc), in1=scrB[:], op=ALU.add),
                    r=["scrB", f"xt{par}:{dc}"], w=[f"xt{par}:{dc}"])
            for dc in range(C):
                gq_dma(dst[bi].rearrange("(c p) t -> p c t", p=128)
                       [:, dc, tok0:tok0 + TS],
                       csl(xtile, dc),
                       [f"xt{par}:{dc}"], [f"d:{dst_nm}:{bi}:{nt}:{dc}"])

        def a_xn(ti):
            return (xn, "xn")

        def emit_a_S1(bi, nt, src, src_nm):
            ti = bi * NT + nt
            tok0 = nt * TS
            par = nt % 2
            xtile = xt[par]
            xkeys = ck(f"xt{par}")
            sq_dma(xtile.rearrange("p (c t) -> p c t", t=TS),
                   src[bi].rearrange("(c p) t -> p c t", p=128)[:, :, tok0:tok0 + TS],
                   [f"d:{src_nm}:{bi}:{nt}:{dc}" for dc in range(C)], xkeys)
            sq_dma(cs_t[0][:], dram["cosPb"][:, tok0:tok0 + TS], [], ["cs0"])
            sq_dma(cs_t[1][:], dram["sinPb"][:, tok0:tok0 + TS], [], ["cs1"])
            emit_ln_a(xtile, xkeys, pool[6], "pl6")

        def emit_a_S1b(bi, nt):
            ti = bi * NT + nt
            xnbuf, xnkey = a_xn(ti)
            emit_ln_b(xnbuf, xnkey)

        def emit_a_S2(bi, nt):
            ti = bi * NT + nt
            xnb_, xnk = a_xn(ti)
            qsb_, aq_, pK_, pV_ = pool[0], pool[1], pool[2], pool[3]
            cosS = cs_t[0][:]
            sinS = cs_t[1][:]
            PTS = PT[:]
            gateS = gate[:]
            for c in range(C):
                b.op("te", lambda h, c=c: h.matmul(
                    ps[3][0:LR, :], aq_sb[:, c * LR:(c + 1) * LR], csl(xnb_, c),
                    start=(c == 0), stop=(c == C - 1)),
                    r=[f"{xnk}:{c}", "k:aq"], w=["ps3"])
            b.op("ve", lambda h: h.tensor_copy(out=tq_sb[:], in_=ps[3][0:LR, :]),
                 r=["ps3"], w=["tq"])
            for c in range(C):
                b.op("te", lambda h, c=c: h.matmul(
                    ps[3][0:LR, :], av_sb[:, c * LR:(c + 1) * LR], csl(xnb_, c),
                    start=(c == 0), stop=(c == C - 1)),
                    r=[f"{xnk}:{c}", "k:av"], w=["ps3"])
            b.op("ve", lambda h: h.tensor_copy(out=tvl_sb[:], in_=ps[3][0:LR, :]),
                 r=["ps3"], w=["tvl"])
            for c in range(C):
                bank = c % 2
                for kc in range(C):
                    b.op("te", lambda h, c=c, kc=kc, bank=bank: h.matmul(
                        ps[bank][:], wslice(0, kc, c), csl(xnb_, kc),
                        start=(kc == 0), stop=False),
                        r=[f"{xnk}:{kc}", "w:0"], w=[f"ps{bank}"])
                b.op("te", lambda h, c=c, bank=bank: h.matmul(
                    ps[bank][:], bq_sb[:, c * 128:(c + 1) * 128], tq_sb[:],
                    start=False, stop=True), r=["tq", "k:bq"], w=[f"ps{bank}"])
                b.op("ve", lambda h, c=c, bank=bank: h.tensor_copy(
                    out=csl(qsb_, c), in_=ps[bank][:]), r=[f"ps{bank}"], w=[f"pl0:{c}"])
                b.op("te", lambda h, c=c: h.matmul(
                    ps[2][:], Rm[:], csl(qsb_, c), start=True, stop=True),
                    r=[f"pl0:{c}", "k:R"], w=["ps2"])
                b.op("ve", lambda h, c=c, cosS=cosS: h.tensor_tensor(
                    out=csl(aq_, c), in0=csl(qsb_, c), in1=cosS, op=ALU.mult),
                    r=[f"pl0:{c}", "cs0"], w=[f"pl1:{c}"])
                b.op("ve", lambda h, c=c, sinS=sinS: h.tensor_tensor(
                    out=scr1[:], in0=ps[2][:], in1=sinS, op=ALU.mult),
                    r=["ps2", "cs1"], w=["scr1"])
                b.op("ve", lambda h, c=c: h.tensor_tensor(
                    out=csl(aq_, c), in0=csl(aq_, c), in1=scr1[:], op=ALU.add),
                    r=["scr1", f"pl1:{c}"], w=[f"pl1:{c}"])
                for kc in range(C):
                    b.op("te", lambda h, c=c, kc=kc, bank=bank: h.matmul(
                        ps[bank][:], wslice(1, kc, c), csl(xnb_, kc),
                        start=(kc == 0), stop=(kc == C - 1)),
                        r=[f"{xnk}:{kc}", "w:1"], w=[f"ps{bank}"])
                b.op("ve", lambda h, c=c, bank=bank: h.tensor_copy(
                    out=csl(qsb_, c), in_=ps[bank][:]), r=[f"ps{bank}"], w=[f"pl0:{c}"])
                b.op("te", lambda h, c=c: h.matmul(
                    ps[2][:], Rm[:], csl(qsb_, c), start=True, stop=True),
                    r=[f"pl0:{c}", "k:R"], w=["ps2"])
                b.op("ve", lambda h, c=c, cosS=cosS: h.tensor_tensor(
                    out=csl(pK_, c), in0=csl(qsb_, c), in1=cosS, op=ALU.mult),
                    r=[f"pl0:{c}", "cs0"], w=[f"pl2:{c}"])
                b.op("ve", lambda h, c=c, sinS=sinS: h.tensor_tensor(
                    out=scr1[:], in0=ps[2][:], in1=sinS, op=ALU.mult),
                    r=["ps2", "cs1"], w=["scr1"])
                b.op("ve", lambda h, c=c: h.tensor_tensor(
                    out=csl(pK_, c), in0=csl(pK_, c), in1=scr1[:], op=ALU.add),
                    r=["scr1", f"pl2:{c}"], w=[f"pl2:{c}"])
                b.op("ve", lambda h, c=c, gateS=gateS: h.tensor_tensor_scan(
                    out=csl(pK_, c), data0=gateS, data1=csl(pK_, c), initial=0.0,
                    op0=ALU.mult, op1=ALU.add),
                    r=[f"pl2:{c}", "k:gate"], w=[f"pl2:{c}"])
                for kc in range(C):
                    b.op("te", lambda h, c=c, kc=kc, bank=bank: h.matmul(
                        ps[bank][:], wslice(2, kc, c), csl(xnb_, kc),
                        start=(kc == 0), stop=False),
                        r=[f"{xnk}:{kc}", "w:2"], w=[f"ps{bank}"])
                b.op("te", lambda h, c=c, bank=bank: h.matmul(
                    ps[bank][:], bv_sb[:, c * 128:(c + 1) * 128], tvl_sb[:],
                    start=False, stop=True), r=["tvl", "k:bv"], w=[f"ps{bank}"])
                b.op("ve", lambda h, c=c, bank=bank, PTS=PTS: h.tensor_tensor(
                    out=csl(pV_, c), in0=ps[bank][:], in1=PTS, op=ALU.mult),
                    r=[f"ps{bank}", "k:PT"], w=[f"pl3:{c}"])
                b.op("ve", lambda h, c=c, gateS=gateS: h.tensor_tensor_scan(
                    out=csl(pV_, c), data0=gateS, data1=csl(pV_, c), initial=0.0,
                    op0=ALU.mult, op1=ALU.add),
                    r=[f"pl3:{c}", "k:gate"], w=[f"pl3:{c}"])
                pKv = csl(pK_, c).rearrange("p (s k) -> p s k", k=KF)
                b.op("gq", lambda h, c=c, pKv=pKv: h.tensor_copy(
                    out=fullK[:, c * TS + nt * 128: c * TS + nt * 128 + 128],
                    in_=pKv[:, :, KF - 1]), r=[f"pl2:{c}"], w=[f"fK:{c}:{nt}"])
                pVv = csl(pV_, c).rearrange("p (s k) -> p s k", k=KF)
                b.op("gq", lambda h, c=c, pVv=pVv: h.tensor_copy(
                    out=scr1[:, 0:128], in_=pVv[:, :, KF - 1]),
                    r=[f"pl3:{c}"], w=["scr1"])
                b.op("te", lambda h, c=c: h.transpose(pst[:], scr1[:, 0:128], ident[:]),
                     r=["scr1", "k:ident"], w=["pst"])
                for j in range(2):
                    hh = 2 * c + j
                    b.op("ve", lambda h, hh=hh, j=j: h.tensor_copy(
                        out=fVsm[:, (nt * NH + hh) * HD: (nt * NH + hh) * HD + HD],
                        in_=pst[:, 64 * j: 64 * j + HD]),
                        r=["pst"], w=[f"fV:{hh}:{nt}"])

        def emit_a_S3(bi, nt):
            qsb_, aq_, pK_, pV_ = pool[0], pool[1], pool[2], pool[3]
            ao_ = pool[6]
            ebuf = pool[4]
            PTS = PT[:]
            for c in range(C):
                b.op("gq", lambda h, c=c: h.tensor_tensor(
                    out=scr1[:], in0=csl(aq_, c), in1=csl(pK_, c), op=ALU.mult),
                    r=[f"pl1:{c}", f"pl2:{c}"], w=["scr1"])
                for j in range(2):
                    b.op("te", lambda h, j=j: h.matmul(
                        ps[3][32 * j:32 * j + 32, :], ind33[j][:, 0:32], scr1[:],
                        start=True, stop=True, skip_group_check=True),
                        r=["scr1", "k:ind33a", "k:ind33b"], w=["ps3"])
                b.op("se", lambda h: h.activation(
                    out=ec_sb[0:64, :], in_=ps[3][0:64, :], func=AF.Exp,
                    scale=1.0 / 16.0), r=["ps3"], w=["ec"])
                pvbank = c % 2
                for sc in range(nt + 1):
                    for j in range(2):
                        sbank = 2 + j
                        b.op("te", lambda h, c=c, j=j, sc=sc, sbank=sbank: h.matmul(
                            ps[sbank][:],
                            fullK[64 * j:64 * j + 64,
                                  c * TS + sc * 128: c * TS + sc * 128 + 128],
                            aq_[64 * j:64 * j + 64, c * TS:(c + 1) * TS],
                            start=True, stop=True),
                            r=[f"fK:{c}:{sc}", f"pl1:{c}"], w=[f"ps{sbank}"])
                    for j in range(2):
                        sbank = 2 + j
                        b.op("se", lambda h, j=j, sc=sc, sbank=sbank: h.activation(
                            out=ebuf[:, (4 * j + sc) * TS:(4 * j + sc + 1) * TS],
                            in_=ps[sbank][:], func=AF.Exp, scale=1.0 / 16.0),
                            r=[f"ps{sbank}"], w=[f"pl4:{4 * j + sc}"])
                for j in range(2):
                    b.op("ve", lambda h, j=j: h.tensor_tensor(
                        out=ebuf[:, (4 * j + nt) * TS:(4 * j + nt + 1) * TS],
                        in0=ebuf[:, (4 * j + nt) * TS:(4 * j + nt + 1) * TS],
                        in1=M0[:], op=ALU.mult),
                        r=[f"pl4:{4 * j + nt}", "k:M0"], w=[f"pl4:{4 * j + nt}"])
                for sc in range(nt + 1):
                    for j in range(2):
                        b.op("te", lambda h, j=j, sc=sc: h.matmul(
                            ps[6][32 * j:32 * j + 32, :], onesden[:, 0:32],
                            ebuf[:, (4 * j + sc) * TS:(4 * j + sc + 1) * TS],
                            start=(sc == 0), stop=(sc == nt),
                            skip_group_check=True),
                            r=[f"pl4:{4 * j + sc}", "k:onesden"], w=["ps6"])
                for sc in range(nt + 1):
                    for j in range(2):
                        hh = 2 * c + j
                        b.op("te", lambda h, hh=hh, j=j, sc=sc, pvbank=pvbank:
                             h.matmul(
                                 ps[pvbank][64 * j:64 * j + 64, :],
                                 fVsm[:, (sc * NH + hh) * HD:
                                      (sc * NH + hh) * HD + HD],
                                 ebuf[:, (4 * j + sc) * TS:(4 * j + sc + 1) * TS],
                                 start=(sc == 0), stop=(sc == nt),
                                 tile_position=(0, 64 * j),
                                 skip_group_check=True),
                             r=[f"fV:{hh}:{sc}", f"pl4:{4 * j + sc}"],
                             w=[f"ps{pvbank}"])
                b.op("ve", lambda h: h.tensor_tensor(
                    out=scrA[0:64, :], in0=ps[6][0:64, :], in1=ec_sb[0:64, :],
                    op=ALU.add), r=["ps6", "ec"], w=["scrA"])
                b.op("se", lambda h: h.activation(
                    out=scrA[0:64, :], in_=scrA[0:64, :], func=AF.Ln),
                    r=["scrA"], w=["scrA"])
                b.op("se", lambda h: h.activation(
                    out=rden_sb[0:64, :], in_=scrA[0:64, :], func=AF.Exp,
                    scale=-1.0), r=["scrA"], w=["rden"])
                ebank = 2 + (c % 2)
                b.op("te", lambda h, ebank=ebank: h.matmul(
                    ps[ebank][:], bcA[:], ec_sb[:], start=True, stop=True),
                    r=["ec", "k:bcA"], w=[f"ps{ebank}"])
                b.op("ve", lambda h, c=c, ebank=ebank: h.tensor_tensor(
                    out=scrA[:], in0=ps[ebank][:], in1=csl(pV_, c), op=ALU.mult),
                    r=[f"ps{ebank}", f"pl3:{c}"], w=["scrA"])
                b.op("ve", lambda h, pvbank=pvbank: h.tensor_tensor(
                    out=scrB[:], in0=ps[pvbank][:], in1=scrA[:], op=ALU.add),
                    r=[f"ps{pvbank}", "scrA"], w=["scrB"])
                b.op("te", lambda h, ebank=ebank: h.matmul(
                    ps[ebank][:], bcA[:], rden_sb[:], start=True, stop=True),
                    r=["rden", "k:bcA"], w=[f"ps{ebank}"])
                b.op("ve", lambda h, ebank=ebank: h.tensor_tensor(
                    out=scrB[:], in0=scrB[:], in1=ps[ebank][:], op=ALU.mult),
                    r=["scrB", f"ps{ebank}"], w=["scrB"])
                b.op("ve", lambda h, c=c, PTS=PTS: h.tensor_tensor(
                    out=csl(ao_, c), in0=scrB[:], in1=PTS, op=ALU.mult),
                    r=["scrB", "k:PT"], w=[f"pl6:{c}"])

        def emit_a_S4(bi, nt, dst, dst_nm):
            tok0 = nt * TS
            par = nt % 2
            xtile = xt[par]
            xkeys = ck(f"xt{par}")
            ao_ = pool[6]
            for c in range(C):
                b.op("te", lambda h, c=c: h.matmul(
                    ps[3][0:LR, :], ao_sb[:, c * LR:(c + 1) * LR], csl(ao_, c),
                    start=(c == 0), stop=(c == C - 1)),
                    r=[f"pl6:{c}", "k:ao"], w=["ps3"])
            b.op("ve", lambda h: h.tensor_copy(out=tq_sb[:], in_=ps[3][0:LR, :]),
                 r=["ps3"], w=["tq"])
            for dc in range(C):
                bank = dc % 2
                for kc in range(C):
                    b.op("te", lambda h, dc=dc, kc=kc, bank=bank: h.matmul(
                        ps[bank][:], wslice(3, kc, dc), csl(ao_, kc),
                        start=(kc == 0), stop=False),
                        r=[f"pl6:{kc}", "w:3"], w=[f"ps{bank}"])
                b.op("te", lambda h, dc=dc, bank=bank: h.matmul(
                    ps[bank][:], bo_sb[:, dc * 128:(dc + 1) * 128], tq_sb[:],
                    start=False, stop=True), r=["tq", "k:bo"], w=[f"ps{bank}"])
                b.op("ve", lambda h, dc=dc, bank=bank: h.tensor_tensor(
                    out=csl(xtile, dc), in0=csl(xtile, dc), in1=ps[bank][:],
                    op=ALU.add),
                    r=[f"ps{bank}", f"xt{par}:{dc}"], w=[f"xt{par}:{dc}"])
            gq_dma(dst[bi].rearrange("(c p) t -> p c t", p=128)[:, :, tok0:tok0 + TS],
                   xtile.rearrange("p (c t) -> p c t", t=TS),
                   xkeys, [f"d:{dst_nm}:{bi}:{nt}:{dc}" for dc in range(C)])

        tiles = [(bi, nt) for bi in range(NB) for nt in range(NT)]

        def run_f_phase(src, src_nm, dst, dst_nm):
            n = len(tiles)
            emit_f_A(*tiles[0], src, src_nm)
            emit_f_A2(*tiles[0])
            emit_f_A(*tiles[1], src, src_nm)
            emit_f_A2(*tiles[1])
            emit_f_B1(*tiles[0])
            for i in range(1, n):
                emit_f_B2(*tiles[i - 1], dst, dst_nm)
                if i + 1 < n:
                    emit_f_A(*tiles[i + 1], src, src_nm)
                emit_f_B1(*tiles[i])
                if i + 1 < n:
                    emit_f_A2(*tiles[i + 1])
            emit_f_B2(*tiles[n - 1], dst, dst_nm)

        def run_a_phase(src, src_nm, dst, dst_nm):
            n = len(tiles)
            emit_a_S1(*tiles[0], src, src_nm)
            emit_a_S1b(*tiles[0])
            for i in range(n):
                emit_a_S2(*tiles[i])
                if i + 1 < n:
                    emit_a_S1(*tiles[i + 1], src, src_nm)
                emit_a_S3(*tiles[i])
                if i + 1 < n:
                    emit_a_S1b(*tiles[i + 1])
                emit_a_S4(*tiles[i], dst, dst_nm)

        load_weights_f()
        if STAGE == 1:
            run_f_phase(dram["x"], "x", dram["out"], "out")
        else:
            run_f_phase(dram["x"], "x", dram["x1"], "x1")
            load_weights_a()
            if STAGE == 2:
                run_a_phase(dram["x1"], "x1", dram["out"], "out")
            else:
                run_a_phase(dram["x1"], "x1", dram["x2"], "x2")
                load_weights_f()
                run_f_phase(dram["x2"], "x2", dram["out"], "out")

        b.finalize(block, sems)
    return nc


def make_consts():
    H = np.array([[1, 1, 1, 1], [1, -1, 1, -1], [1, 1, -1, -1], [1, -1, -1, 1]],
                 np.float32)
    i_idx = np.arange(S)
    c_idx = np.arange(128)
    PTm = H[(i_idx % 4)[None, :], (c_idx % 4)[:, None]].astype(np.float32)
    inv_freq = 1.0 / 10000.0 ** (np.arange(0, HD, 2, dtype=np.float64) / HD)
    theta = i_idx[None, :] * inv_freq[(c_idx % HD) % 32][:, None]
    cosP = (np.cos(theta) * PTm).astype(np.float32)
    sinP = (np.sin(theta) * PTm).astype(np.float32)
    gate = np.broadcast_to(
        np.where(i_idx % 4 == 0, 0.0, 1.0).astype(np.float32), (128, S)).copy()
    M0 = (np.arange(128)[:, None] < (np.arange(TS) // KF)[None, :]).astype(np.float32)
    R = np.zeros((128, 128), np.float32)
    for m in range(128):
        mm = m % HD
        base = HD * (m // HD)
        if mm < 32:
            R[base + mm + 32, m] = -1.0
        else:
            R[base + mm - 32, m] = 1.0
    onesden = np.zeros((128, 33), np.float32)
    onesden[:, 0] = 1.0
    ind33a = np.zeros((128, 33), np.float32)
    ind33a[0:64, 0] = 1.0
    ind33b = np.zeros((128, 33), np.float32)
    ind33b[64:128, 0] = 1.0
    bcA = np.zeros((128, 128), np.float32)
    bcA[0, 0:64] = 1.0
    bcA[32, 64:128] = 1.0
    bcB = np.zeros((128, 128), np.float32)
    bcB[64, 0:64] = 1.0
    bcB[96, 64:128] = 1.0
    return dict(cosP=cosP, sinP=sinP, PT=np.ascontiguousarray(PTm[:, :TS]),
                gate=np.ascontiguousarray(gate[:, :TS]), M0=M0, R=R,
                ident=np.eye(128, dtype=np.float32),
                ones1=np.ones((1, 128), np.float32),
                onesln=np.full((128, 1), 1.0 / D, np.float32),
                onesden=onesden, ind33a=ind33a, ind33b=ind33b, bcA=bcA, bcB=bcB,
                epsv=np.full((128, 1), 1e-5, np.float32))


def kernel(**inp):
    x = np.asarray(inp["L0_attn_hidden_states"], np.float32)
    xT = np.ascontiguousarray(x.transpose(0, 2, 1))
    shared = dict(
        wq=np.ascontiguousarray(inp["L0_attn_Wq"], np.float32),
        wk=np.ascontiguousarray(inp["L0_attn_Wk"], np.float32),
        wv=np.ascontiguousarray(inp["L0_attn_Wv"], np.float32),
        wo=np.ascontiguousarray(inp["L0_attn_Wo"], np.float32),
        wrT=np.ascontiguousarray(np.asarray(inp["L0_ffn_W_r"], np.float32).T),
        wkT=np.ascontiguousarray(np.asarray(inp["L0_ffn_W_k"], np.float32).T),
        wvT=np.ascontiguousarray(np.asarray(inp["L0_ffn_W_v"], np.float32).T),
        aq=np.ascontiguousarray(inp["L0_attn_Aq"], np.float32),
        av=np.ascontiguousarray(inp["L0_attn_Av"], np.float32),
        ao=np.ascontiguousarray(inp["L0_attn_Ao"], np.float32),
        bq=np.ascontiguousarray(inp["L0_attn_Bq"], np.float32),
        bv=np.ascontiguousarray(inp["L0_attn_Bv"], np.float32),
        bo=np.ascontiguousarray(inp["L0_attn_Bo"], np.float32),
        mur=np.ascontiguousarray(
            np.asarray(inp["L0_ffn_mu_r"], np.float32).reshape(C, 128).T),
        muk=np.ascontiguousarray(
            np.asarray(inp["L0_ffn_mu_k"], np.float32).reshape(C, 128).T),
    )
    shared.update(make_consts())
    import ml_dtypes
    shared["cosPb"] = shared["cosP"].astype(ml_dtypes.bfloat16)
    shared["sinPb"] = shared["sinP"].astype(ml_dtypes.bfloat16)
    nc = build_nc()
    in_maps = []
    for core in range(NCORES):
        m = dict(shared)
        m["x"] = np.ascontiguousarray(xT[2 * core: 2 * core + 2])
        in_maps.append(m)
    res = run_bass_kernel_spmd(nc, in_maps, core_ids=list(range(NCORES)))
    y = np.concatenate([r["out"] for r in res.results], axis=0)
    return np.ascontiguousarray(y.transpose(0, 2, 1))
